# version 02
# baseline (speedup 1.0000x reference)
import os
import numpy as np
from contextlib import ExitStack

import concourse.bass as bass
import concourse.mybir as mybir
from concourse.bass import IndirectOffsetOnAxis
from concourse.bass_utils import run_bass_kernel_spmd

F32 = mybir.dt.float32
BF16 = mybir.dt.bfloat16
FP8 = mybir.dt.float8e4
I32 = mybir.dt.int32
U32 = mybir.dt.uint32
AF = mybir.ActivationFunctionType
ALU = mybir.AluOpType
AX = mybir.AxisListType

NCORE = 8
T = 8192
TPC = 1024
H = 2048
E = 8
CAP = 1024
IA = 1024
IF = 5632
GT = 256
NG = TPC // GT
NT = GT // 128
HC = H // 128
IFC = IF // 128
IAC = IA // 128
DBG = os.environ.get("KDBG", "")
KPH = int(os.environ.get("KPH", "5"))
KP2 = int(os.environ.get("KP2", "9"))
KQT = int(os.environ.get("KQT", "9"))


class Ctr:
    def __init__(self, sem, step=1):
        self.sem, self.v, self.step = sem, 0, step

    def tick(self, instr, n=None):
        n = self.step if n is None else n
        instr.then_inc(self.sem, n)
        self.v += n
        return self.v


def build_program():
    nc = bass.Bass("TRN2", target_bir_lowering=False)
    root = ExitStack()

    cl = Ctr(root.enter_context(nc.semaphore("c_ld")), 16)
    cs = Ctr(root.enter_context(nc.semaphore("c_lds")), 16)
    cg = Ctr(root.enter_context(nc.semaphore("c_ldg")), 16)
    cv = Ctr(root.enter_context(nc.semaphore("c_v")), 1)
    ca = Ctr(root.enter_context(nc.semaphore("c_a")), 1)
    ct = Ctr(root.enter_context(nc.semaphore("c_t")), 1)
    cp = Ctr(root.enter_context(nc.semaphore("c_g")), 1)
    ALLC = [cl, cs, cg, cv, ca, ct, cp]

    def join(eng, *ctrs):
        for c in (ctrs or ALLC):
            if c.v:
                eng.wait_ge(c.sem, c.v)

    d_xsh = nc.dram_tensor("xsh", [TPC, H], F32, kind="ExternalInput")
    d_xfT = nc.dram_tensor("xfT", [H, T], F32, kind="ExternalInput")
    d_gidx = nc.dram_tensor("gidx", [128, 8], I32, kind="ExternalInput")
    d_Wg = nc.dram_tensor("Wg", [E, H, 2 * IA], F32, kind="ExternalInput")
    d_Wd = nc.dram_tensor("Wd", [E, IA, H], F32, kind="ExternalInput")
    d_Ng = nc.dram_tensor("Ng", [E, H], F32, kind="ExternalInput")
    d_Nd = nc.dram_tensor("Nd", [E, IA], F32, kind="ExternalInput")
    d_sWg = nc.dram_tensor("sWg", [H, 2 * IA], F32, kind="ExternalInput")
    d_sWd = nc.dram_tensor("sWd", [IA, H], F32, kind="ExternalInput")
    d_sNg = nc.dram_tensor("sNg", [1, H], F32, kind="ExternalInput")
    d_sNd = nc.dram_tensor("sNd", [1, IA], F32, kind="ExternalInput")
    d_gnw = nc.dram_tensor("gnw", [1, H], F32, kind="ExternalInput")
    d_Wgate = nc.dram_tensor("Wgate", [H, E], F32, kind="ExternalInput")
    d_Wsg = nc.dram_tensor("Wsg", [1, H], F32, kind="ExternalInput")
    d_wg = nc.dram_tensor("wgate_f", [IF, H], F32, kind="ExternalInput")
    d_wu = nc.dram_tensor("wup_f", [IF, H], F32, kind="ExternalInput")
    d_wd = nc.dram_tensor("wdown_f", [H, IF], F32, kind="ExternalInput")
    d_out = nc.dram_tensor("out", [TPC, H], F32, kind="ExternalOutput")

    s_xp = nc.dram_tensor("s_xp", [TPC, H], F32, kind="Internal")
    s_rtab = nc.dram_tensor("s_rtab", [T, 8], F32, kind="Internal")
    s_xg = nc.dram_tensor("s_xg", [E * CAP, H], F32, kind="Internal")
    s_moe = nc.dram_tensor("s_moe", [E * CAP, H], F32, kind="Internal")
    s_acc = nc.dram_tensor("s_acc", [TPC, H], F32, kind="Internal")
    s_qwgT = nc.dram_tensor("s_qwgT", [HC, IFC // 4, 128, 512], FP8, kind="Internal")
    s_qwuT = nc.dram_tensor("s_qwuT", [HC, IFC // 4, 128, 512], FP8, kind="Internal")
    s_qwdT = nc.dram_tensor("s_qwdT", [IFC, HC // 4, 128, 512], FP8, kind="Internal")
    s_qsWg = nc.dram_tensor("s_qsWg", [H, 2 * IA], FP8, kind="Internal")
    s_qsWd = nc.dram_tensor("s_qsWd", [IA, H], FP8, kind="Internal")
    s_wgs = nc.dram_tensor("s_wgs", [1, IF], F32, kind="Internal")
    s_wus = nc.dram_tensor("s_wus", [1, IF], F32, kind="Internal")
    s_wds = nc.dram_tensor("s_wds", [1, H], F32, kind="Internal")

    dbg = {}
    if "rt" in DBG:
        dbg["dbg_rt"] = nc.dram_tensor("dbg_rt", [TPC, 8], F32, kind="ExternalOutput")
    if "xp" in DBG:
        dbg["dbg_xp"] = nc.dram_tensor("dbg_xp", [TPC, H], F32, kind="ExternalOutput")
    if "acc" in DBG:
        dbg["dbg_acc"] = nc.dram_tensor("dbg_acc", [TPC, H], F32, kind="ExternalOutput")


    sbr = lambda n, s, d: root.enter_context(nc.sbuf_tensor(n, s, d))
    identf = sbr("identf", [128, 128], F32)
    ones_c = sbr("ones_c", [128, 1], F32)
    ones_cb = sbr("ones_cb", [128, 1], BF16)
    ones_r = sbr("ones_r", [1, 128], F32)
    iotaP = sbr("iotaP", [128, 1], I32)
    iotaPf = sbr("iotaPf", [128, 1], F32)
    iotaF = sbr("iotaF", [128, 128], I32)
    lidx = sbr("lidx", [128, 8], U32)
    stg_row = sbr("stg_row", [1, H], F32)
    b_wm = sbr("b_wm", [128, 64], F32)
    rt_mine = sbr("rt_mine", [128, 8, 8], F32)
    g_idx = sbr("g_idx", [128, 8], U32)
    u32c = sbr("u32c", [128, 2], U32)
    stgf = sbr("stgf", [128, 256], F32)
    colx = sbr("colx", [128, 16], F32)
    wrow = sbr("wrow", [1, 64], F32)
    wcl = sbr("wcl", [128, 4], F32)

    PS = [root.enter_context(nc.psum_tensor(f"psb{i}", [128, 512], F32)) for i in range(8)]

    with nc.Block() as blk:
        @blk.sync
        def _(sync):
            cl.tick(sync.dma_start(out=g_idx[:].bitcast(I32), in_=d_gidx[:]))

        @blk.gpsimd
        def _(gp):
            cp.tick(gp.iota(iotaP[:], pattern=[[0, 1]], base=0, channel_multiplier=1))
            cp.tick(gp.iota(iotaF[:], pattern=[[1, 128]], base=0, channel_multiplier=0))
            cp.tick(gp.iota(lidx[:].bitcast(I32), pattern=[[128, 8]], base=0,
                            channel_multiplier=1))

        @blk.vector
        def _(v):
            join(v, cp)
            cv.tick(v.tensor_copy(out=iotaPf[:], in_=iotaP[:]))
            v.wait_ge(cv.sem, cv.v)
            cv.tick(v.tensor_scalar(out=identf[:], in0=iotaF[:], scalar1=iotaPf[:, 0:1],
                                    scalar2=None, op0=ALU.is_equal))
            cv.tick(v.memset(ones_c[:], 1.0))
            cv.tick(v.memset(ones_cb[:], 1.0))
            cv.tick(v.memset(ones_r[:], 1.0))

    def bcast_from_dram(dst, dram_ap, n):
        with nc.Block() as blk:
            @blk.sync
            def _(sync):
                join(sync)
                cl.tick(sync.dma_start(out=stg_row[0:1, 0:n], in_=dram_ap))

            base_v = cv.v

            @blk.tensor
            def _(t):
                join(t, cl, cv)
                for i in range(0, n, 512):
                    w = min(512, n - i)
                    ch = i // 512
                    if ch >= 2:
                        t.wait_ge(cv.sem, base_v + ch - 1)
                    ct.tick(t.matmul(PS[ch % 2][:, 0:w], ones_r[:],
                                     stg_row[0:1, i:i + w], start=True, stop=True))

            base_t = ct.v

            @blk.vector
            def _(v):
                nch = (n + 511) // 512
                for i in range(0, n, 512):
                    w = min(512, n - i)
                    ch = i // 512
                    v.wait_ge(ct.sem, base_t - nch + ch + 1)
                    cv.tick(v.tensor_copy(out=dst[:, i:i + w], in_=PS[ch % 2][:, 0:w]))

    if KPH < 1:
        root.close()
        return nc
    p1o = ExitStack()
    sbo = lambda n, s, d: p1o.enter_context(nc.sbuf_tensor(n, s, d))
    rt = sbo("rt", [128, 64, 10], F32)
    iot8i = sbo("iot8i", [128, 64, 8], I32)
    iot8 = sbo("iot8", [128, 64, 8], F32)
    iot8p = sbo("iot8p", [128, 64, 8], F32)
    L3 = sbo("L3", [128, 64, 8], F32)
    P3 = sbo("P3", [128, 64, 8], F32)
    M3 = sbo("M3", [128, 64, 8], F32)
    oh1 = sbo("oh1", [128, 64, 8], F32)
    oh2 = sbo("oh2", [128, 64, 8], F32)
    K3 = sbo("K3", [128, 64, 8], F32)
    C3 = sbo("C3", [128, 64, 8], F32)
    sc = sbo("sc", [128, 64, 8], F32)
    rpk = sbo("rpk", [128, 64, 8], F32)

    def col(tile, k):
        return tile[:, :, k:k + 1]

    def colb(tile, k):
        return tile[:, :, k:k + 1].to_broadcast([128, 64, 8])

    p1a = ExitStack()
    sba = lambda n, s, d: p1a.enter_context(nc.sbuf_tensor(n, s, d))
    xp0 = sba("xp0", [128, 2048], F32)
    xp1 = sba("xp1", [128, 2048], F32)
    xpieces = [xp0, xp1]
    xsq = sba("xsq", [128, 2048], BF16)
    sums = sba("sums", [33, T], F32)
    wgp = sba("wgp", [128, HC * 9], F32)
    gnwF = sba("gnwF", [128, HC], F32)
    csr = sba("csr", [1, 8], F32)

    with nc.Block() as blk:
        @blk.sync
        def _(sync):
            join(sync)
            cl.tick(sync.dma_start(out=wgp[:].rearrange("p (c e) -> p c e", c=HC)[:, :, 0:8],
                                   in_=d_Wgate[:].rearrange("(c p) e -> p c e", p=128)))
            for c in range(HC):
                cl.tick(sync.dma_start(
                    out=gnwF[:, c:c + 1],
                    in_=d_gnw[0:1, c * 128:(c + 1) * 128]
                    .rearrange("o (p q) -> (o p) q", q=1)))

        @blk.gpsimd
        def _(gp):
            cp.tick(gp.iota(iot8i[:], pattern=[[0, 64], [1, 8]], base=0,
                            channel_multiplier=0))

        @blk.vector
        def _(v):
            join(v, cl, cp)
            wgp3 = wgp[:].rearrange("p (c e) -> p c e", c=HC)
            cv.tick(v.tensor_mul(wgp3[:, :, 0:8], wgp3[:, :, 0:8],
                                 gnwF[:].unsqueeze(2).to_broadcast([128, HC, 8])))
            cv.tick(v.memset(wgp3[:, :, 8:9], 1.0))
            cv.tick(v.memset(sums[:], 0.0))
            cv.tick(v.tensor_copy(out=iot8[:], in_=iot8i[:]))
            v.wait_ge(cv.sem, cv.v)
            cv.tick(v.tensor_scalar(out=iot8p[:], in0=iot8[:], scalar1=64.0,
                                    scalar2=None, op0=ALU.add))

    with nc.Block() as blk:
        @blk.tensor
        def _(t):
            join(t)
            ins = None
            for c in range(HC):
                ins = t.matmul(PS[0][0:1, 0:8], ones_c[:], wgp[:, c * 9:c * 9 + 8],
                               start=(c == 0), stop=(c == HC - 1))
            ct.tick(ins)

        @blk.vector
        def _(v):
            join(v, ct)
            cv.tick(v.tensor_copy(out=csr[:], in_=PS[0][0:1, 0:8]))

    with nc.Block() as blk:
        @blk.tensor
        def _(t):
            join(t)
            ct.tick(t.matmul(PS[1][:, 0:8], ones_r[:], csr[:], start=True, stop=True))

        @blk.vector
        def _(v):
            join(v, ct)
            cv.tick(v.tensor_copy(out=colx[:, 8:16], in_=PS[1][:, 0:8]))

    with nc.Block() as blk:
        NPC, NSL = 4, 4
        base_l = cl.v
        base_a = ca.v
        base_t = ct.v
        base_v = cv.v

        @blk.sync
        def _(sync):
            join(sync)
            n = 0
            for c in range(HC):
                for q in range(NPC):
                    if n >= 2:
                        sync.wait_ge(ct.sem, base_t + (n - 1) * 2 * NSL)
                        sync.wait_ge(ca.sem, base_a + n - 1)
                    cl.tick(sync.dma_start(out=xpieces[n % 2][:],
                                           in_=d_xfT[c * 128:(c + 1) * 128,
                                                     q * 2048:(q + 1) * 2048]))
                    n += 1

        @blk.scalar
        def _(a):
            n = 0
            for c in range(HC):
                for q in range(NPC):
                    a.wait_ge(cl.sem, base_l + (n + 1) * 16)
                    if n >= 1:
                        a.wait_ge(ct.sem, base_t + n * 2 * NSL)
                    ca.tick(a.activation(out=xsq[:], in_=xpieces[n % 2][:], func=AF.Square))
                    n += 1

        @blk.tensor
        def _(t):
            n = 0
            for c in range(HC):
                for q in range(NPC):
                    t.wait_ge(cl.sem, base_l + (n + 1) * 16)
                    t.wait_ge(ca.sem, base_a + n + 1)
                    for j in range(NSL):
                        gs = n * NSL + j
                        bk = (gs % 4) * 2
                        if gs >= 4:
                            t.wait_ge(cv.sem, base_v + (gs - 3) * 2)
                        ct.tick(t.matmul(PS[bk][0:9, 0:512], wgp[:, c * 9:(c + 1) * 9],
                                         xpieces[n % 2][:, j * 512:(j + 1) * 512],
                                         start=True, stop=True))
                        ct.tick(t.matmul(PS[bk + 1][0:1, 0:512], ones_cb[:],
                                         xsq[:, j * 512:(j + 1) * 512],
                                         start=True, stop=True))
                    n += 1

        @blk.vector
        def _(v):
            n = 0
            for c in range(HC):
                for q in range(NPC):
                    for j in range(NSL):
                        gs = n * NSL + j
                        bk = (gs % 4) * 2
                        tok = (q * NSL + j)
                        v.wait_ge(ct.sem, base_t + (gs + 1) * 2)
                        sl = slice(tok * 512, (tok + 1) * 512)
                        if c == 0:
                            cv.tick(v.tensor_copy(out=sums[0:9, sl], in_=PS[bk][0:9, 0:512]))
                            cv.tick(v.tensor_copy(out=sums[32:33, sl],
                                                  in_=PS[bk + 1][0:1, 0:512]))
                        else:
                            v.wait_ge(cv.sem, cv.v)
                            cv.tick(v.tensor_add(sums[0:9, sl], sums[0:9, sl],
                                                 PS[bk][0:9, 0:512]))
                            cv.tick(v.tensor_add(sums[32:33, sl], sums[32:33, sl],
                                                 PS[bk + 1][0:1, 0:512]))
                    n += 1

    with nc.Block() as blk:
        base_v = cv.v
        base_t = ct.v

        @blk.tensor
        def _(t):
            join(t)
            for k in range(64):
                if k >= 2:
                    t.wait_ge(cv.sem, base_v + (k - 1) * 2)
                ct.tick(t.transpose(PS[k % 2][0:128, 0:9], sums[0:9, k * 128:(k + 1) * 128],
                                    identf[0:9, 0:9]))
                ct.tick(t.transpose(PS[2 + k % 2][0:128, 0:1],
                                    sums[32:33, k * 128:(k + 1) * 128],
                                    identf[32:33, 32:33]))

        @blk.vector
        def _(v):
            for k in range(64):
                v.wait_ge(ct.sem, base_t + 2 * k + 2)
                cv.tick(v.tensor_copy(out=rt[:, k, 0:9], in_=PS[k % 2][0:128, 0:9]))
                cv.tick(v.tensor_copy(out=rt[:, k, 9:10], in_=PS[2 + k % 2][0:128, 0:1]))

    p1a.close()

    p1b = ExitStack()
    sbb = lambda n, s, d: p1b.enter_context(nc.sbuf_tensor(n, s, d))
    maskT = sbb("maskT", [8, T], BF16)
    cumT = sbb("cumT", [8, T], F32)
    keptT = sbb("keptT", [8, T], BF16)
    keptf = sbb("keptf", [8, T], F32)

    with nc.Block() as blk:
        @blk.vector
        def _(v):
            join(v)

            def vt(ins):
                cv.tick(ins)
                v.wait_ge(cv.sem, cv.v)

            vt(v.tensor_scalar(out=col(sc, 0), in0=rt[:, :, 8:9], scalar1=1.0 / H,
                               scalar2=None, op0=ALU.mult))
            vt(v.tensor_scalar(out=col(sc, 1), in0=rt[:, :, 9:10], scalar1=1.0 / H,
                               scalar2=None, op0=ALU.mult))
            vt(v.tensor_mul(col(sc, 2), col(sc, 0), col(sc, 0)))
            vt(v.tensor_sub(col(sc, 1), col(sc, 1), col(sc, 2)))
            vt(v.tensor_scalar(out=col(sc, 2), in0=col(sc, 1), scalar1=1e-5,
                               scalar2=None, op0=ALU.add))
            vt(v.reciprocal(col(sc, 4), col(sc, 2)))
            vt(v.tensor_mul(col(sc, 5), col(sc, 1), col(sc, 4)))
            vt(v.tensor_scalar(out=col(sc, 5), in0=col(sc, 5), scalar1=1e-6,
                               scalar2=None, op0=ALU.add))

        @blk.scalar
        def _(a):
            join(a, cv)
            ca.tick(a.activation(out=col(sc, 6), in_=col(sc, 2), func=AF.Sqrt))
            ca.tick(a.activation(out=col(sc, 7), in_=col(sc, 5), func=AF.Sqrt))

        @blk.vector
        def _(v):
            join(v, ca)

            def vt(ins):
                cv.tick(ins)
                v.wait_ge(cv.sem, cv.v)

            vt(v.reciprocal(col(sc, 2), col(sc, 6)))
            vt(v.reciprocal(col(sc, 3), col(sc, 7)))
            vt(v.tensor_mul(col(sc, 3), col(sc, 3), col(sc, 2)))
            vt(v.tensor_mul(L3[:], colx[:, 8:16].unsqueeze(1)
                            .to_broadcast([128, 64, 8]), colb(sc, 0)))
            vt(v.tensor_sub(L3[:], rt[:, :, 0:8], L3[:]))
            vt(v.tensor_mul(L3[:], L3[:], colb(sc, 3)))
            vt(v.reduce_max(col(sc, 4), L3[:], axis=AX.X))
            vt(v.tensor_sub(P3[:], L3[:], colb(sc, 4)))

        @blk.scalar
        def _(a):
            join(a, cv)
            ca.tick(a.activation(out=P3[:], in_=P3[:], func=AF.Exp))

        @blk.vector
        def _(v):
            join(v, ca)

            def vt(ins):
                cv.tick(ins)
                v.wait_ge(cv.sem, cv.v)

            vt(v.reduce_sum(col(sc, 5), P3[:], axis=AX.X))
            vt(v.reciprocal(col(sc, 5), col(sc, 5)))
            vt(v.tensor_mul(P3[:], P3[:], colb(sc, 5)))
            vt(v.reduce_max(col(sc, 4), L3[:], axis=AX.X))
            vt(v.tensor_tensor(out=M3[:], in0=L3[:], in1=colb(sc, 4), op=ALU.is_ge))
            vt(v.scalar_tensor_tensor(out=M3[:], in0=M3[:], scalar=-64.0, in1=iot8p[:],
                                      op0=ALU.mult, op1=ALU.add))
            vt(v.tensor_reduce(col(sc, 6), M3[:], axis=AX.X, op=ALU.min))
            vt(v.tensor_tensor(out=oh1[:], in0=iot8[:], in1=colb(sc, 6), op=ALU.is_equal))
            vt(v.tensor_scalar(out=M3[:], in0=oh1[:], scalar1=1e30, scalar2=None,
                               op0=ALU.mult))
            vt(v.tensor_sub(M3[:], L3[:], M3[:]))
            vt(v.reduce_max(col(sc, 4), M3[:], axis=AX.X))
            vt(v.tensor_tensor(out=K3[:], in0=M3[:], in1=colb(sc, 4), op=ALU.is_ge))
            vt(v.scalar_tensor_tensor(out=K3[:], in0=K3[:], scalar=-64.0, in1=iot8p[:],
                                      op0=ALU.mult, op1=ALU.add))
            vt(v.tensor_reduce(col(sc, 7), K3[:], axis=AX.X, op=ALU.min))
            vt(v.tensor_tensor(out=oh2[:], in0=iot8[:], in1=colb(sc, 7), op=ALU.is_equal))
            vt(v.tensor_mul(M3[:], P3[:], oh1[:]))
            vt(v.reduce_sum(col(sc, 0), M3[:], axis=AX.X))
            vt(v.tensor_mul(M3[:], P3[:], oh2[:]))
            vt(v.reduce_sum(col(sc, 1), M3[:], axis=AX.X))
            vt(v.tensor_add(M3[:], oh1[:], oh2[:]))
            vt(v.tensor_scalar(out=K3[:], in0=P3[:], scalar1=0.0, scalar2=None,
                               op0=ALU.is_gt))
            vt(v.tensor_mul(M3[:], M3[:], K3[:]))

    with nc.Block() as blk:
        base_v = cv.v
        base_t = ct.v

        @blk.tensor
        def _(t):
            join(t)
            for k in range(64):
                if k >= 2:
                    t.wait_ge(cv.sem, base_v + k - 1)
                ct.tick(t.transpose(PS[k % 2][0:8, 0:128], M3[:, k, :], identf[:]))

        @blk.vector
        def _(v):
            for k in range(64):
                v.wait_ge(ct.sem, base_t + k + 1)
                cv.tick(v.tensor_copy(out=maskT[0:8, k * 128:(k + 1) * 128],
                                      in_=PS[k % 2][0:8, 0:128]))

    with nc.Block() as blk:
        @blk.vector
        def _(v):
            join(v)

            def vt(ins):
                cv.tick(ins)
                v.wait_ge(cv.sem, cv.v)

            vt(v.tensor_tensor_scan(out=cumT[:], data0=maskT[:], data1=maskT[:],
                                    initial=0.0, op0=ALU.add, op1=ALU.max))
            vt(v.tensor_scalar(out=keptT[:], in0=cumT[:], scalar1=float(CAP),
                               scalar2=None, op0=ALU.is_le))
            vt(v.tensor_mul(keptT[:], keptT[:], maskT[:]))
            vt(v.tensor_copy(out=keptf[:], in_=keptT[:]))
            for s in range(NCORE):
                sl = slice(s * TPC, (s + 1) * TPC)
                vt(v.tensor_tensor_scan(out=cumT[:, sl], data0=keptT[:, sl],
                                        data1=keptT[:, sl], initial=0.0,
                                        op0=ALU.add, op1=ALU.max))

    with nc.Block() as blk:
        base_t = ct.v
        base_v = cv.v

        @blk.tensor
        def _(t):
            join(t)
            for k in range(64):
                if k >= 2:
                    t.wait_ge(cv.sem, base_v + (k - 1) * 2)
                ct.tick(t.transpose(PS[4 + k % 2][0:128, 0:8],
                                    keptf[0:8, k * 128:(k + 1) * 128], identf[0:8, 0:8]))
                ct.tick(t.transpose(PS[6 + k % 2][0:128, 0:8],
                                    cumT[0:8, k * 128:(k + 1) * 128], identf[0:8, 0:8]))

        @blk.vector
        def _(v):
            for k in range(64):
                v.wait_ge(ct.sem, base_t + 2 * k + 2)
                cv.tick(v.tensor_copy(out=K3[:, k, :], in_=PS[4 + k % 2][0:128, 0:8]))
                cv.tick(v.tensor_copy(out=C3[:, k, :], in_=PS[6 + k % 2][0:128, 0:8]))

    with nc.Block() as blk:
        @blk.vector
        def _(v):
            join(v)

            def vt(ins):
                cv.tick(ins)
                v.wait_ge(cv.sem, cv.v)

            vt(v.tensor_mul(M3[:], K3[:], oh1[:]))
            vt(v.reduce_sum(col(sc, 2), M3[:], axis=AX.X))
            vt(v.tensor_mul(M3[:], K3[:], oh2[:]))
            vt(v.reduce_sum(col(sc, 3), M3[:], axis=AX.X))
            vt(v.tensor_mul(M3[:], C3[:], oh1[:]))
            vt(v.reduce_sum(col(sc, 4), M3[:], axis=AX.X))
            vt(v.tensor_mul(M3[:], C3[:], oh2[:]))
            vt(v.reduce_sum(col(sc, 5), M3[:], axis=AX.X))
            vt(v.tensor_scalar(out=col(sc, 4), in0=col(sc, 4), scalar1=1.0,
                               scalar2=None, op0=ALU.subtract))
            vt(v.tensor_scalar(out=col(sc, 5), in0=col(sc, 5), scalar1=1.0,
                               scalar2=None, op0=ALU.subtract))
            vt(v.tensor_copy(out=col(rpk, 0), in_=col(sc, 6)))
            vt(v.tensor_copy(out=col(rpk, 1), in_=col(sc, 7)))
            vt(v.tensor_mul(col(rpk, 2), col(sc, 0), col(sc, 2)))
            vt(v.tensor_mul(col(rpk, 3), col(sc, 1), col(sc, 3)))
            vt(v.tensor_scalar(out=col(rpk, 4), in0=col(sc, 6), scalar1=float(CAP),
                               scalar2=None, op0=ALU.mult))
            vt(v.tensor_add(col(rpk, 4), col(rpk, 4), col(sc, 4)))
            vt(v.tensor_scalar(out=col(rpk, 6), in0=col(sc, 2), scalar1=-65536.0,
                               scalar2=65536.0, op0=ALU.mult, op1=ALU.add))
            vt(v.tensor_add(col(rpk, 4), col(rpk, 4), col(rpk, 6)))
            vt(v.tensor_scalar(out=col(rpk, 5), in0=col(sc, 7), scalar1=float(CAP),
                               scalar2=None, op0=ALU.mult))
            vt(v.tensor_add(col(rpk, 5), col(rpk, 5), col(sc, 5)))
            vt(v.tensor_scalar(out=col(rpk, 6), in0=col(sc, 3), scalar1=-65536.0,
                               scalar2=65536.0, op0=ALU.mult, op1=ALU.add))
            vt(v.tensor_add(col(rpk, 5), col(rpk, 5), col(rpk, 6)))
            vt(v.tensor_copy(out=col(rpk, 6), in_=col(sc, 4)))
            vt(v.tensor_copy(out=col(rpk, 7), in_=col(sc, 5)))

        @blk.sync
        def _(sync):
            join(sync, cv)
            cl.tick(sync.dma_start(out=s_rtab[:].rearrange("(p k) f -> p k f", p=128),
                                   in_=rpk[:]))

        @blk.gpsimd
        def _(gp):
            join(gp, cl)
            for k in range(8):
                cg.tick(gp.indirect_dma_start(
                    out=rt_mine[:, k, :], out_offset=None,
                    in_=s_rtab[:],
                    in_offset=IndirectOffsetOnAxis(ap=g_idx[:, k:k + 1], axis=0),
                    bounds_check=T - 1, oob_is_err=False))

    if "rt" in DBG:
        with nc.Block() as blk:
            @blk.sync
            def _(sync):
                join(sync)
                for k in range(8):
                    cl.tick(sync.dma_start(out=dbg["dbg_rt"][k * 128:(k + 1) * 128, :],
                                           in_=rt_mine[:, k, :]))

    p1b.close()
    p1o.close()

    if KPH < 2:
        root.close()
        return nc
    mean_slot = {}
    slot_n = [0]

    p2 = ExitStack()
    sb2 = lambda n, s, d: p2.enter_context(nc.sbuf_tensor(n, s, d))
    wch0 = sb2("wch0", [128, IF], F32)
    wch1 = sb2("wch1", [128, IF], F32)
    wchs = [wch0, wch1]
    q8ch = sb2("q8ch", [128, IF], FP8)
    absd = sb2("absd", [128, IF], BF16)
    i32ch = sb2("i32ch", [128, 2048], I32)
    stag = sb2("stag", [128, 16 * 512], FP8)
    stagd = sb2("stagd", [128, 44 * 512], FP8)

    def mean_job(mk, nch, wdt, key):
        sidx = slot_n[0]
        mean_slot[key] = sidx
        slot_n[0] += 2
        with nc.Block() as blk:
            base_l = cl.v
            base_a = ca.v

            @blk.sync
            def _(sync):
                join(sync)
                for c in range(nch):
                    if c >= 2:
                        sync.wait_ge(ca.sem, base_a + c - 1)
                    cl.tick(sync.dma_start(out=wchs[c % 2][:, 0:wdt], in_=mk(c)))

            base_v = cv.v

            @blk.scalar
            def _(a):
                for c in range(nch):
                    a.wait_ge(cl.sem, base_l + (c + 1) * 16)
                    if c >= 1:
                        a.wait_ge(cv.sem, base_v + c)
                    ca.tick(a.activation(out=absd[:, 0:wdt], in_=wchs[c % 2][:, 0:wdt],
                                         func=AF.Abs, accum_out=wcl[:, 0:1]))

            @blk.vector
            def _(v):
                for c in range(nch):
                    v.wait_ge(ca.sem, base_a + c + 1)
                    v.wait_ge(cv.sem, cv.v)
                    if c == 0:
                        cv.tick(v.tensor_copy(out=wcl[:, 1:2], in_=wcl[:, 0:1]))
                    else:
                        cv.tick(v.tensor_add(wcl[:, 1:2], wcl[:, 1:2], wcl[:, 0:1]))

        with nc.Block() as blk:
            @blk.tensor
            def _(t):
                join(t)
                ct.tick(t.matmul(PS[0][0:1, 0:1], wcl[:, 1:2], ones_c[:],
                                 start=True, stop=True))

            @blk.vector
            def _(v):
                join(v, ct)
                nel = float(nch * 128 * wdt)
                cv.tick(v.tensor_scalar(out=wrow[0:1, sidx + 1:sidx + 2],
                                        in0=PS[0][0:1, 0:1], scalar1=1.0 / nel,
                                        scalar2=1e-5, op0=ALU.mult, op1=ALU.max))
                v.wait_ge(cv.sem, cv.v)
                cv.tick(v.reciprocal(wrow[0:1, sidx:sidx + 1],
                                     wrow[0:1, sidx + 1:sidx + 2]))

    if KP2 >= 1:
        mean_job(lambda c: d_sWg[c * 128:(c + 1) * 128, :], HC, 2048, "sWg")
    if KP2 >= 2:
        mean_job(lambda c: d_sWd[c * 128:(c + 1) * 128, :], IAC, 2048, "sWd")
        for e in range(E):
            mean_job(lambda c, e=e: d_Wg[e, c * 128:(c + 1) * 128, :], HC, 2048, f"Wg{e}")
            mean_job(lambda c, e=e: d_Wd[e, c * 128:(c + 1) * 128, :], IAC, 2048, f"Wd{e}")

    with nc.Block() as blk:
        @blk.tensor
        def _(t):
            join(t)
            ct.tick(t.matmul(PS[1][:, 0:64], ones_r[:], wrow[:], start=True, stop=True))

        @blk.vector
        def _(v):
            join(v, ct)
            cv.tick(v.tensor_copy(out=b_wm[:], in_=PS[1][:, 0:64]))

    def tern_quant2(mk_in, mk_out, nch, wdt, key):
        sidx = mean_slot[key]
        with nc.Block() as blk:
            base_l = cl.v
            base_a = ca.v
            base_v = cv.v
            base_s = cs.v

            @blk.sync
            def _(sync):
                join(sync)
                for c in range(nch):
                    if c >= 2:
                        sync.wait_ge(ca.sem, base_a + c - 1)
                    cl.tick(sync.dma_start(out=wchs[c % 2][:, 0:wdt], in_=mk_in(c)))

            @blk.scalar
            def _(a):
                for c in range(nch):
                    a.wait_ge(cl.sem, base_l + (c + 1) * 16)
                    if c >= 1:
                        a.wait_ge(cv.sem, base_v + c)
                    ca.tick(a.activation(out=i32ch[:, 0:wdt], in_=wchs[c % 2][:, 0:wdt],
                                         func=AF.Identity, scale=b_wm[:, sidx:sidx + 1]))
                    a.wait_ge(cv.sem, base_v + c + 1)
                    cs.tick(a.dma_start(out=mk_out(c), in_=q8ch[:, 0:wdt]))

            @blk.vector
            def _(v):
                for c in range(nch):
                    v.wait_ge(ca.sem, base_a + c + 1)
                    if c >= 1:
                        v.wait_ge(cs.sem, base_s + c * 16)
                    cv.tick(v.tensor_scalar(out=q8ch[:, 0:wdt], in0=i32ch[:, 0:wdt],
                                            scalar1=-1.0, scalar2=1.0,
                                            op0=ALU.max, op1=ALU.min))

    if KP2 >= 4:
        tern_quant2(lambda c: d_sWg[c * 128:(c + 1) * 128, :],
                    lambda c: s_qsWg[c * 128:(c + 1) * 128, :], HC, 2048, "sWg")
        tern_quant2(lambda c: d_sWd[c * 128:(c + 1) * 128, :],
                    lambda c: s_qsWd[c * 128:(c + 1) * 128, :], IAC, 2048, "sWd")

    def fp8_quant_T(load_ap, nch, rowlen, wsrow_store, stag_buf, tile_in_group,
                    flush_after, nflush):
        n_tr = rowlen // 128
        vper = 5 + (n_tr if KQT >= 2 else 0)
        tper = n_tr + (1 if KQT >= 5 else 0)
        aper = n_tr + (1 if KQT >= 5 else 0)
        with nc.Block() as blk:
            base_l = cl.v
            base_v = cv.v
            base_a = ca.v
            base_t = ct.v
            base_g = cg.v

            @blk.sync
            def _(sync):
                join(sync)
                for c in range(nch):
                    if c >= 2:
                        sync.wait_ge(cv.sem, base_v + (c - 1) * vper)
                    cl.tick(sync.dma_start(out=wchs[c % 2][:, 0:rowlen], in_=load_ap(c)))

            @blk.vector
            def _(v):
                for c in range(nch):
                    v.wait_ge(cl.sem, base_l + (c + 1) * 16)
                    v.wait_ge(cv.sem, cv.v)
                    cv.tick(v.reduce_max(wcl[:, 0:1], wchs[c % 2][:, 0:rowlen],
                                         axis=AX.X, apply_absolute_value=True))
                    v.wait_ge(cv.sem, cv.v)
                    cv.tick(v.reciprocal(wcl[:, 1:2], wcl[:, 0:1]))
                    v.wait_ge(cv.sem, cv.v)
                    cv.tick(v.tensor_scalar(out=wcl[:, 2:3], in0=wcl[:, 1:2],
                                            scalar1=224.0, scalar2=None, op0=ALU.mult))
                    cv.tick(v.tensor_scalar(out=wcl[:, 3:4], in0=wcl[:, 0:1],
                                            scalar1=1.0 / 224.0, scalar2=None,
                                            op0=ALU.mult))
                    v.wait_ge(cv.sem, cv.v)
                    cv.tick(v.tensor_scalar(out=q8ch[:, 0:rowlen], in0=wchs[c % 2][:, 0:rowlen],
                                            scalar1=wcl[:, 2:3], scalar2=None, op0=ALU.mult))
                    v.wait_ge(cv.sem, cv.v)
                    if KQT < 2:
                        continue
                    for tr in range(n_tr):
                        idx = c * n_tr + tr
                        if idx >= 2 and KQT >= 3:
                            t_idx = (c * tper + tr - 2 if tr >= 2
                                     else (c - 1) * tper + n_tr + tr - 2)
                            v.wait_ge(ct.sem, base_t + t_idx + 1)
                        cv.tick(v.tensor_copy(out=stgf[:, (idx % 2) * 128:(idx % 2) * 128 + 128],
                                              in_=q8ch[:, tr * 128:(tr + 1) * 128]))

            @blk.tensor
            def _(t):
                if KQT < 3:
                    return
                for c in range(nch):
                    for tr in range(n_tr):
                        t.wait_ge(cv.sem, base_v + c * vper + 5 + tr + 1)
                        aidx = c * aper + tr
                        if aidx >= 2 and KQT >= 4:
                            t.wait_ge(ca.sem, base_a + aidx - 1)
                        ct.tick(t.transpose(PS[4 + (c * n_tr + tr) % 2][0:128, 0:128],
                                            stgf[:, ((c * n_tr + tr) % 2) * 128:
                                                 ((c * n_tr + tr) % 2) * 128 + 128],
                                            identf[:]))
                    if KQT < 5:
                        continue
                    t.wait_ge(cv.sem, base_v + c * vper + 4)
                    if c >= 1 and KQT >= 4:
                        t.wait_ge(ca.sem, base_a + c * aper)
                    ct.tick(t.transpose(PS[6][0:1, 0:128], wcl[:, 3:4], identf[:]))

            base_s = cs.v

            @blk.scalar
            def _(a):
                if KQT < 4:
                    return
                for c in range(nch):
                    if c % 4 == 0 and c > 0 and KQT >= 6:
                        a.wait_ge(cs.sem, base_s + (c // 4) * nflush * 16)
                    for tr in range(n_tr):
                        a.wait_ge(ct.sem, base_t + c * tper + tr + 1)
                        dstt, dsto = tile_in_group(c, tr)
                        ca.tick(a.activation(out=stag_buf[:, dstt * 512 + dsto * 128:
                                                          dstt * 512 + dsto * 128 + 128],
                                             in_=PS[4 + (c * n_tr + tr) % 2][0:128, 0:128],
                                             func=AF.Identity))
                    if KQT >= 5:
                        a.wait_ge(ct.sem, base_t + c * tper + tper)
                        ca.tick(a.activation(out=wsrow_store(c), in_=PS[6][0:1, 0:128],
                                             func=AF.Identity))
                    if KQT >= 6:
                        for (dram_ap, ssl) in flush_after(c):
                            cs.tick(a.dma_start(out=dram_ap, in_=stag_buf[:, ssl]))

    wsr_wg = sb2("wsr_wg", [1, IF], F32)
    wsr_wu = sb2("wsr_wu", [1, IF], F32)
    wsr_wd = sb2("wsr_wd", [1, H], F32)

    def tile_ig_up(c, tr):
        return (tr, c % 4)

    def flush_up(s_out):
        def fl(c):
            if c % 4 == 3:
                ic4 = c // 4
                return [(s_out[hc, ic4, :, :], slice(hc * 512, (hc + 1) * 512))
                        for hc in range(HC)]
            return []
        return fl

    def tile_ig_dn(c, tr):
        return (tr, c % 4)

    def flush_dn(c):
        if c % 4 == 3:
            hc4 = c // 4
            return [(s_qwdT[ic, hc4, :, :], slice(ic * 512, (ic + 1) * 512))
                    for ic in range(IFC)]
        return []

    if KP2 >= 5:
        fp8_quant_T(lambda c: d_wg[c * 128:(c + 1) * 128, :], IFC, 2048,
                    lambda c: wsr_wg[0:1, c * 128:(c + 1) * 128], stag, tile_ig_up,
                    flush_up(s_qwgT), HC)
    if KP2 >= 6:
        fp8_quant_T(lambda c: d_wu[c * 128:(c + 1) * 128, :], IFC, 2048,
                    lambda c: wsr_wu[0:1, c * 128:(c + 1) * 128], stag, tile_ig_up,
                    flush_up(s_qwuT), HC)
    if KP2 >= 7:
        fp8_quant_T(lambda c: d_wd[c * 128:(c + 1) * 128, :], HC, IF,
                    lambda c: wsr_wd[0:1, c * 128:(c + 1) * 128], stagd, tile_ig_dn,
                    flush_dn, IFC)

    with nc.Block() as blk:
        @blk.sync
        def _(sync):
            join(sync)
            cl.tick(sync.dma_start(out=s_wgs[:], in_=wsr_wg[:]))
            cl.tick(sync.dma_start(out=s_wus[:], in_=wsr_wu[:]))
            cl.tick(sync.dma_start(out=s_wds[:], in_=wsr_wd[:]))

    p2.close()

    if KPH < 3:
        root.close()
        return nc
    p4 = ExitStack()
    sb4 = lambda n, s, d: p4.enter_context(nc.sbuf_tensor(n, s, d))
    b_sNg = sb4("b_sNg", [128, H], F32)
    b_sNd = sb4("b_sNd", [128, IA], F32)
    b_Wsg = sb4("b_Wsg", [128, H], F32)
    arena = sb4("arena", [128, NT * IF], F32)
    qtTs = sb4("qtTs", [128, NT * IF], BF16)
    qxT = sb4("qxT", [128, NT * H], BF16)
    accg = sb4("accg", [128, NT * H], F32)
    xpg = sb4("xpg", [128, NT * H], F32)
    q8g = sb4("q8g", [128, NT * IF], FP8)
    gch = sb4("gch", [128, NT * 512], F32)
    uch = sb4("uch", [128, NT * 512], F32)
    sich = sb4("sich", [128, NT * 512], F32)
    slabU = sb4("slabU", [128, IFC * 512], FP8)
    wslab = slabU
    wdslab = slabU
    tslab = slabU
    wbc = sb4("wbc", [128, 1024], F32)
    shar = arena[:, 0:NT * H]
    shy = arena[:, NT * H:2 * NT * H]
    shh = arena[:, 2 * NT * H:2 * NT * H + NT * IA]
    i32g = qtTs[:].bitcast(I32)[:, 0:NT * H]
    k1T = sb4("k1T", [128, NT * H], BF16)
    k2T = sb4("k2T", [128, NT * IA], BF16)
    gcol = sb4("gcol", [128, NT * 16], F32)

    bcast_from_dram(b_sNg, d_sNg[0:1, :], H)
    bcast_from_dram(b_sNd, d_sNd[0:1, :], IA)
    bcast_from_dram(b_Wsg, d_Wsg[0:1, :], H)

    def gc(kt, i):
        return gcol[:, kt * 16 + i:kt * 16 + i + 1]

    def transpose_set(srcf, dst, nblk, blk128):
        with nc.Block() as blk:
            base_t = ct.v
            base_a = ca.v

            @blk.tensor
            def _(t):
                join(t)
                for i in range(nblk):
                    if i >= 2:
                        t.wait_ge(ca.sem, base_a + i - 1)
                    ct.tick(t.transpose(PS[4 + i % 2][0:128, 0:128], srcf(i), identf[:]))

            @blk.scalar
            def _(a):
                for i in range(nblk):
                    a.wait_ge(ct.sem, base_t + i + 1)
                    ca.tick(a.activation(out=dst(i), in_=PS[4 + i % 2][0:128, 0:128],
                                         func=AF.Identity))

    for g in range(NG):
        r0 = g * GT

        with nc.Block() as blk:
            @blk.sync
            def _(sync):
                join(sync)
                cl.tick(sync.dma_start(out=accg[:].rearrange("p (k h) -> p k h", k=NT),
                                       in_=d_xsh[r0:r0 + GT, :]
                                       .rearrange("(k p) h -> p k h", p=128)))

            @blk.scalar
            def _(a):
                join(a, cl)
                for kt in range(NT):
                    ca.tick(a.activation(out=shar[:, kt * H:(kt + 1) * H],
                                         in_=accg[:, kt * H:(kt + 1) * H],
                                         func=AF.Square, accum_out=gc(kt, 0)))

            @blk.vector
            def _(v):
                join(v, ca, cl)

                def vt(ins):
                    cv.tick(ins)
                    v.wait_ge(cv.sem, cv.v)

                for kt in range(NT):
                    sl = slice(kt * H, (kt + 1) * H)
                    vt(v.reduce_sum(gc(kt, 1), accg[:, sl], axis=AX.X))
                    vt(v.tensor_scalar(out=gc(kt, 1), in0=gc(kt, 1), scalar1=1.0 / H,
                                       scalar2=None, op0=ALU.mult))
                    vt(v.tensor_scalar(out=gc(kt, 0), in0=gc(kt, 0), scalar1=1.0 / H,
                                       scalar2=None, op0=ALU.mult))
                    vt(v.tensor_mul(gc(kt, 2), gc(kt, 1), gc(kt, 1)))
                    vt(v.tensor_sub(gc(kt, 0), gc(kt, 0), gc(kt, 2)))
                    vt(v.tensor_scalar(out=gc(kt, 2), in0=gc(kt, 0), scalar1=1e-5,
                                       scalar2=None, op0=ALU.add))

            @blk.scalar
            def _(a):
                join(a, cv)
                for kt in range(NT):
                    ca.tick(a.activation(out=gc(kt, 3), in_=gc(kt, 2), func=AF.Sqrt))

            @blk.vector
            def _(v):
                join(v, ca)

                def vt(ins):
                    cv.tick(ins)
                    v.wait_ge(cv.sem, cv.v)

                for kt in range(NT):
                    sl = slice(kt * H, (kt + 1) * H)
                    vt(v.reciprocal(gc(kt, 3), gc(kt, 3)))
                    vt(v.tensor_scalar(out=gc(kt, 4), in0=gc(kt, 1), scalar1=-1.0,
                                       scalar2=None, op0=ALU.mult))
                    vt(v.tensor_scalar(out=xpg[:, sl], in0=accg[:, sl],
                                       scalar1=gc(kt, 4), scalar2=gc(kt, 3),
                                       op0=ALU.add, op1=ALU.mult))
                    vt(v.reciprocal(gc(kt, 5), gc(kt, 2)))
                    vt(v.tensor_mul(gc(kt, 5), gc(kt, 5), gc(kt, 0)))
                    vt(v.tensor_scalar(out=gc(kt, 5), in0=gc(kt, 5), scalar1=1e-8,
                                       scalar2=None, op0=ALU.add))

            @blk.scalar
            def _(a):
                join(a, cv)
                for kt in range(NT):
                    ca.tick(a.activation(out=gc(kt, 6), in_=gc(kt, 5), func=AF.Sqrt))
                for kt in range(NT):
                    cs.tick(a.dma_start(out=s_xp[r0 + kt * 128:r0 + (kt + 1) * 128, :],
                                        in_=xpg[:, kt * H:(kt + 1) * H]))

            @blk.vector
            def _(v):
                join(v, ca)
                for kt in range(NT):
                    v.wait_ge(cv.sem, cv.v)
                    cv.tick(v.reciprocal(gc(kt, 5), gc(kt, 6)))

            @blk.gpsimd
            def _(gp):
                join(gp, cv)
                for kt in range(NT):
                    gk = g * NT + kt
                    cp.tick(gp.tensor_copy(out=u32c[:, 0:1], in_=rt_mine[:, gk, 4:5]))
                    cp.tick(gp.tensor_copy(out=u32c[:, 1:2], in_=rt_mine[:, gk, 5:6]))
                    gp.wait_ge(cp.sem, cp.v)
                    cg.tick(gp.indirect_dma_start(
                        out=s_xg[:],
                        out_offset=IndirectOffsetOnAxis(ap=u32c[:, 0:1], axis=0),
                        in_=xpg[:, kt * H:(kt + 1) * H], in_offset=None,
                        bounds_check=E * CAP - 1, oob_is_err=False))
                    cg.tick(gp.indirect_dma_start(
                        out=s_xg[:],
                        out_offset=IndirectOffsetOnAxis(ap=u32c[:, 1:2], axis=0),
                        in_=xpg[:, kt * H:(kt + 1) * H], in_offset=None,
                        bounds_check=E * CAP - 1, oob_is_err=False))

        with nc.Block() as blk:
            @blk.vector
            def _(v):
                join(v)

                def vt(ins):
                    cv.tick(ins)
                    v.wait_ge(cv.sem, cv.v)

                for kt in range(NT):
                    sl = slice(kt * H, (kt + 1) * H)
                    vt(v.reduce_max(gc(kt, 7), xpg[:, sl], axis=AX.X,
                                    apply_absolute_value=True))
                    vt(v.reciprocal(gc(kt, 8), gc(kt, 7)))
                    vt(v.tensor_scalar(out=gc(kt, 8), in0=gc(kt, 8), scalar1=224.0,
                                       scalar2=None, op0=ALU.mult))
                    vt(v.tensor_scalar(out=gc(kt, 9), in0=gc(kt, 7), scalar1=1.0 / 224.0,
                                       scalar2=None, op0=ALU.mult))
                    vt(v.tensor_scalar(out=q8g[:, sl], in0=xpg[:, sl],
                                       scalar1=gc(kt, 8), scalar2=None, op0=ALU.mult))
                for kt in range(NT):
                    sl = slice(kt * H, (kt + 1) * H)
                    vt(v.tensor_copy(out=shar[:, sl], in_=q8g[:, sl]))

        transpose_set(lambda i: shar[:, i * 128:(i + 1) * 128],
                      lambda i: qxT[:, i * 128:(i + 1) * 128], NT * HC, 128)

        for ic in range(IF // 512):
            for (s_w, s_ws, dst) in ((s_qwgT, s_wgs, gch), (s_qwuT, s_wus, uch)):
                with nc.Block() as blk:
                    @blk.sync
                    def _(sync):
                        join(sync)
                        for c in range(HC):
                            cl.tick(sync.dma_start(out=wslab[:, c * 512:(c + 1) * 512],
                                                   in_=s_w[c, ic, :, :]))
                        cl.tick(sync.dma_start(out=stg_row[0:1, 0:512],
                                               in_=s_ws[0:1, ic * 512:(ic + 1) * 512]))

                    @blk.tensor
                    def _(t):
                        join(t, cl, cv, ca)
                        ct.tick(t.matmul(PS[2][:, 0:512], ones_r[:], stg_row[0:1, 0:512],
                                         start=True, stop=True))
                        for kt in range(NT):
                            ins = None
                            for c in range(HC):
                                ins = t.matmul(PS[kt][0:128, 0:512],
                                               qxT[:, (kt * HC + c) * 128:
                                                   (kt * HC + c) * 128 + 128],
                                               wslab[:, c * 512:(c + 1) * 512],
                                               start=(c == 0), stop=(c == HC - 1))
                            ct.tick(ins)

                    @blk.vector
                    def _(v):
                        join(v, ct)
                        v.wait_ge(cv.sem, cv.v)
                        cv.tick(v.tensor_copy(out=wbc[:, 0:512], in_=PS[2][:, 0:512]))
                        for kt in range(NT):
                            v.wait_ge(cv.sem, cv.v)
                            cv.tick(v.scalar_tensor_tensor(
                                out=dst[:, kt * 512:(kt + 1) * 512],
                                in0=PS[kt][0:128, 0:512], scalar=gc(kt, 9),
                                in1=wbc[:, 0:512], op0=ALU.mult, op1=ALU.mult))

            with nc.Block() as blk:
                @blk.scalar
                def _(a):
                    join(a, cv)
                    ca.tick(a.activation(out=sich[:], in_=gch[:], func=AF.Silu))

                @blk.vector
                def _(v):
                    join(v, ca)
                    for kt in range(NT):
                        for sub in range(4):
                            icb = ic * 4 + sub
                            off = icb * NT * 128 + kt * 128
                            v.wait_ge(cv.sem, cv.v)
                            cv.tick(v.tensor_mul(
                                arena[:, off:off + 128],
                                sich[:, kt * 512 + sub * 128:kt * 512 + (sub + 1) * 128],
                                uch[:, kt * 512 + sub * 128:kt * 512 + (sub + 1) * 128]))

        with nc.Block() as blk:
            @blk.vector
            def _(v):
                join(v)

                def vt(ins):
                    cv.tick(ins)
                    v.wait_ge(cv.sem, cv.v)

                for kt in range(NT):
                    tv = arena[:].rearrange("p (i k c) -> p i k c", i=IFC, k=NT)
                    vt(v.reduce_max(gch[:, kt * IFC:(kt + 1) * IFC].unsqueeze(2),
                                    tv[:, :, kt, :], axis=AX.X,
                                    apply_absolute_value=True))
                    vt(v.reduce_max(gc(kt, 10), gch[:, kt * IFC:(kt + 1) * IFC],
                                    axis=AX.X))
                    vt(v.reciprocal(gc(kt, 11), gc(kt, 10)))
                    vt(v.tensor_scalar(out=gc(kt, 11), in0=gc(kt, 11), scalar1=224.0,
                                       scalar2=None, op0=ALU.mult))
                    vt(v.tensor_scalar(out=gc(kt, 12), in0=gc(kt, 10), scalar1=1.0 / 224.0,
                                       scalar2=None, op0=ALU.mult))
                for kt in range(NT):
                    for icb in range(IFC):
                        off = icb * NT * 128 + kt * 128
                        cv.tick(v.tensor_scalar(out=q8g[:, off:off + 128],
                                                in0=arena[:, off:off + 128],
                                                scalar1=gc(kt, 11), scalar2=None,
                                                op0=ALU.mult))
                v.wait_ge(cv.sem, cv.v)
                cv.tick(v.tensor_copy(out=arena[:], in_=q8g[:]))

        def _qt_src(i):
            kt, icb = i // IFC, i % IFC
            off = icb * NT * 128 + kt * 128
            return arena[:, off:off + 128]

        def _qt_dst(i):
            kt, icb = i // IFC, i % IFC
            off = (kt * IFC + icb) * 128
            return qtTs[:, off:off + 128]

        transpose_set(_qt_src, _qt_dst, NT * IFC, 128)

        for hc4 in range(HC // 4):
            with nc.Block() as blk:
                @blk.sync
                def _(sync):
                    join(sync)
                    for c in range(IFC):
                        cl.tick(sync.dma_start(out=wdslab[:, c * 512:(c + 1) * 512],
                                               in_=s_qwdT[c, hc4, :, :]))
                    cl.tick(sync.dma_start(out=stg_row[0:1, 0:512],
                                           in_=s_wds[0:1, hc4 * 512:(hc4 + 1) * 512]))

                @blk.tensor
                def _(t):
                    join(t, cl, cv, ca)
                    ct.tick(t.matmul(PS[2][:, 0:512], ones_r[:], stg_row[0:1, 0:512],
                                     start=True, stop=True))
                    for kt in range(NT):
                        ins = None
                        for c in range(IFC):
                            ins = t.matmul(PS[kt][0:128, 0:512],
                                           qtTs[:, (kt * IFC + c) * 128:
                                                (kt * IFC + c) * 128 + 128],
                                           wdslab[:, c * 512:(c + 1) * 512],
                                           start=(c == 0), stop=(c == IFC - 1))
                        ct.tick(ins)

                @blk.vector
                def _(v):
                    join(v, ct)
                    v.wait_ge(cv.sem, cv.v)
                    cv.tick(v.tensor_copy(out=wbc[:, 0:512], in_=PS[2][:, 0:512]))
                    for kt in range(NT):
                        v.wait_ge(cv.sem, cv.v)
                        cv.tick(v.scalar_tensor_tensor(
                            out=gch[:, kt * 512:(kt + 1) * 512],
                            in0=PS[kt][0:128, 0:512], scalar=gc(kt, 12),
                            in1=wbc[:, 0:512], op0=ALU.mult, op1=ALU.mult))
                        v.wait_ge(cv.sem, cv.v)
                        sl = slice(kt * H + hc4 * 512, kt * H + (hc4 + 1) * 512)
                        cv.tick(v.tensor_add(accg[:, sl], accg[:, sl],
                                             gch[:, kt * 512:(kt + 1) * 512]))

        with nc.Block() as blk:
            @blk.vector
            def _(v):
                join(v)

                def vt(ins):
                    cv.tick(ins)
                    v.wait_ge(cv.sem, cv.v)

                for kt in range(NT):
                    sl = slice(kt * H, (kt + 1) * H)
                    vt(v.scalar_tensor_tensor(out=shar[:, sl], in0=xpg[:, sl],
                                              scalar=gc(kt, 5), in1=b_sNg[:],
                                              op0=ALU.mult, op1=ALU.mult))
                    vt(v.reduce_max(gc(kt, 13), shar[:, sl], axis=AX.X,
                                    apply_absolute_value=True))
                    vt(v.tensor_scalar(out=gc(kt, 13), in0=gc(kt, 13), scalar1=1e-5,
                                       scalar2=None, op0=ALU.max))
                    vt(v.reciprocal(gc(kt, 14), gc(kt, 13)))
                    vt(v.tensor_scalar(out=gc(kt, 14), in0=gc(kt, 14), scalar1=127.0,
                                       scalar2=None, op0=ALU.mult))
                    vt(v.tensor_scalar(out=gc(kt, 13), in0=gc(kt, 13), scalar1=1.0 / 127.0,
                                       scalar2=None, op0=ALU.mult))

            @blk.scalar
            def _(a):
                join(a, cv)
                for kt in range(NT):
                    sl = slice(kt * H, (kt + 1) * H)
                    ca.tick(a.activation(out=i32g[:, sl], in_=shar[:, sl],
                                         func=AF.Identity, scale=gc(kt, 14)))

            @blk.vector
            def _(v):
                join(v, ca)
                cv.tick(v.tensor_copy(out=shar[:, 0:NT * H], in_=i32g[:, 0:NT * H]))

        transpose_set(lambda i: shar[:, i * 128:(i + 1) * 128],
                      lambda i: k1T[:, i * 128:(i + 1) * 128], NT * HC, 128)

        for oc in range(4):
            with nc.Block() as blk:
                @blk.sync
                def _(sync):
                    join(sync)
                    for c in range(HC):
                        cl.tick(sync.dma_start(
                            out=tslab[:, c * 512:(c + 1) * 512],
                            in_=s_qsWg[c * 128:(c + 1) * 128, oc * 512:(oc + 1) * 512]))

                @blk.tensor
                def _(t):
                    join(t, cl, ca)
                    for kt in range(NT):
                        ins = None
                        for c in range(HC):
                            ins = t.matmul(PS[kt][0:128, 0:512],
                                           k1T[:, (kt * HC + c) * 128:
                                               (kt * HC + c) * 128 + 128],
                                           tslab[:, c * 512:(c + 1) * 512],
                                           start=(c == 0), stop=(c == HC - 1))
                        ct.tick(ins)

                @blk.vector
                def _(v):
                    join(v, ct)
                    sw = mean_slot["sWg"]
                    for kt in range(NT):
                        v.wait_ge(cv.sem, cv.v)
                        cv.tick(v.tensor_mul(gc(kt, 15), gc(kt, 13),
                                             b_wm[:, sw + 1:sw + 2]))
                        v.wait_ge(cv.sem, cv.v)
                        cv.tick(v.tensor_scalar(
                            out=shy[:, kt * H + oc * 512:kt * H + (oc + 1) * 512],
                            in0=PS[kt][0:128, 0:512], scalar1=gc(kt, 15),
                            scalar2=None, op0=ALU.mult))

        with nc.Block() as blk:
            @blk.scalar
            def _(a):
                join(a, cv)
                for kt in range(NT):
                    ca.tick(a.activation(out=shar[:, kt * H:kt * H + IA],
                                         in_=shy[:, kt * H:kt * H + IA], func=AF.Silu))

            @blk.vector
            def _(v):
                join(v, ca)
                for kt in range(NT):
                    v.wait_ge(cv.sem, cv.v)
                    cv.tick(v.tensor_mul(shh[:, kt * IA:(kt + 1) * IA],
                                         shar[:, kt * H:kt * H + IA],
                                         shy[:, kt * H + IA:kt * H + 2 * IA]))

            @blk.scalar
            def _(a):
                join(a, cv)
                for kt in range(NT):
                    ca.tick(a.activation(out=shar[:, kt * H:kt * H + IA],
                                         in_=shh[:, kt * IA:(kt + 1) * IA],
                                         func=AF.Square, accum_out=gc(kt, 0)))

            @blk.vector
            def _(v):
                join(v, ca)
                for kt in range(NT):
                    v.wait_ge(cv.sem, cv.v)
                    cv.tick(v.tensor_scalar(out=gc(kt, 0), in0=gc(kt, 0), scalar1=1.0 / IA,
                                            scalar2=1e-8, op0=ALU.mult, op1=ALU.add))

            @blk.scalar
            def _(a):
                join(a, cv)
                for kt in range(NT):
                    ca.tick(a.activation(out=gc(kt, 1), in_=gc(kt, 0), func=AF.Sqrt))

            @blk.vector
            def _(v):
                join(v, ca)

                def vt(ins):
                    cv.tick(ins)
                    v.wait_ge(cv.sem, cv.v)

                for kt in range(NT):
                    vt(v.reciprocal(gc(kt, 1), gc(kt, 1)))
                    vt(v.scalar_tensor_tensor(out=shar[:, kt * H:kt * H + IA],
                                              in0=shh[:, kt * IA:(kt + 1) * IA],
                                              scalar=gc(kt, 1), in1=b_sNd[:],
                                              op0=ALU.mult, op1=ALU.mult))
                    vt(v.reduce_max(gc(kt, 2), shar[:, kt * H:kt * H + IA], axis=AX.X,
                                    apply_absolute_value=True))
                    vt(v.tensor_scalar(out=gc(kt, 2), in0=gc(kt, 2), scalar1=1e-5,
                                       scalar2=None, op0=ALU.max))
                    vt(v.reciprocal(gc(kt, 3), gc(kt, 2)))
                    vt(v.tensor_scalar(out=gc(kt, 3), in0=gc(kt, 3), scalar1=127.0,
                                       scalar2=None, op0=ALU.mult))
                    vt(v.tensor_scalar(out=gc(kt, 2), in0=gc(kt, 2), scalar1=1.0 / 127.0,
                                       scalar2=None, op0=ALU.mult))

            @blk.scalar
            def _(a):
                join(a, cv)
                for kt in range(NT):
                    ca.tick(a.activation(out=i32g[:, kt * IA:(kt + 1) * IA],
                                         in_=shar[:, kt * H:kt * H + IA],
                                         func=AF.Identity, scale=gc(kt, 3)))

            @blk.vector
            def _(v):
                join(v, ca)
                cv.tick(v.tensor_copy(out=shar[:, 0:NT * IA], in_=i32g[:, 0:NT * IA]))

        transpose_set(lambda i: shar[:, i * 128:(i + 1) * 128],
                      lambda i: k2T[:, i * 128:(i + 1) * 128], NT * IAC, 128)

        with nc.Block() as blk:
            @blk.vector
            def _(v):
                join(v)
                for kt in range(NT):
                    sl = slice(kt * H, (kt + 1) * H)
                    v.wait_ge(cv.sem, cv.v)
                    cv.tick(v.tensor_mul(shy[:, sl], xpg[:, sl], b_Wsg[:]))
                    v.wait_ge(cv.sem, cv.v)
                    cv.tick(v.reduce_sum(gc(kt, 4), shy[:, sl], axis=AX.X))

            @blk.scalar
            def _(a):
                join(a, cv)
                for kt in range(NT):
                    ca.tick(a.activation(out=gc(kt, 4), in_=gc(kt, 4), func=AF.Sigmoid))

        for oc in range(4):
            with nc.Block() as blk:
                @blk.sync
                def _(sync):
                    join(sync)
                    for c in range(IAC):
                        cl.tick(sync.dma_start(
                            out=tslab[:, c * 512:(c + 1) * 512],
                            in_=s_qsWd[c * 128:(c + 1) * 128, oc * 512:(oc + 1) * 512]))

                @blk.tensor
                def _(t):
                    join(t, cl, ca)
                    for kt in range(NT):
                        ins = None
                        for c in range(IAC):
                            ins = t.matmul(PS[kt][0:128, 0:512],
                                           k2T[:, (kt * IAC + c) * 128:
                                               (kt * IAC + c) * 128 + 128],
                                           tslab[:, c * 512:(c + 1) * 512],
                                           start=(c == 0), stop=(c == IAC - 1))
                        ct.tick(ins)

                @blk.vector
                def _(v):
                    join(v, ct)
                    sw = mean_slot["sWd"]
                    for kt in range(NT):
                        v.wait_ge(cv.sem, cv.v)
                        cv.tick(v.tensor_mul(gc(kt, 15), gc(kt, 2), b_wm[:, sw + 1:sw + 2]))
                        v.wait_ge(cv.sem, cv.v)
                        cv.tick(v.tensor_scalar(out=gch[:, kt * 512:(kt + 1) * 512],
                                                in0=PS[kt][0:128, 0:512],
                                                scalar1=gc(kt, 15), scalar2=None,
                                                op0=ALU.mult))
                        v.wait_ge(cv.sem, cv.v)
                        cv.tick(v.tensor_mul(gch[:, kt * 512:(kt + 1) * 512],
                                             gch[:, kt * 512:(kt + 1) * 512],
                                             gc(kt, 4).to_broadcast([128, 512])))
                        v.wait_ge(cv.sem, cv.v)
                        sl = slice(kt * H + oc * 512, kt * H + (oc + 1) * 512)
                        cv.tick(v.tensor_add(accg[:, sl], accg[:, sl],
                                             gch[:, kt * 512:(kt + 1) * 512]))

        with nc.Block() as blk:
            @blk.sync
            def _(sync):
                join(sync)
                cl.tick(sync.dma_start(out=s_acc[r0:r0 + GT, :]
                                       .rearrange("(k p) h -> p k h", p=128),
                                       in_=accg[:].rearrange("p (k h) -> p k h", k=NT)))

    if "xp" in DBG:
        with nc.Block() as blk:
            @blk.sync
            def _(sync):
                join(sync)
                cl.tick(sync.dma_start(out=dbg["dbg_xp"][:], in_=s_xp[:]))

    p4.close()

    if KPH < 4:
        root.close()
        return nc
    p5 = ExitStack()
    sb5 = lambda n, s, d: p5.enter_context(nc.sbuf_tensor(n, s, d))
    qWgE = sb5("qWgE", [128, HC * 2048], FP8)
    qWdE = sb5("qWdE", [128, IAC * 2048], FP8)
    ewch = sb5("ewch", [128, 2048], F32)
    ei32 = sb5("ei32", [128, 2048], I32)
    b_NgE = sb5("b_NgE", [128, H], F32)
    b_NdE = sb5("b_NdE", [128, IA], F32)
    xg = sb5("xg", [128, H], F32)
    exn = sb5("exn", [128, H], F32)
    ekT = sb5("ekT", [128, H], BF16)
    ek2T = sb5("ek2T", [128, IA], BF16)
    eyb = sb5("eyb", [128, 2 * IA], F32)
    ehh = sb5("ehh", [128, IA], F32)
    ecol = sb5("ecol", [128, 8], F32)

    for e in range(E):
        bcast_from_dram(b_NgE, d_Ng[e:e + 1, :], H)
        bcast_from_dram(b_NdE, d_Nd[e:e + 1, :], IA)

        for (dsrc, nch, qdst, key) in ((d_Wg, HC, qWgE, f"Wg{e}"),
                                       (d_Wd, IAC, qWdE, f"Wd{e}")):
            sidx = mean_slot[key]
            with nc.Block() as blk:
                base_l = cl.v
                base_a = ca.v
                base_v = cv.v

                @blk.sync
                def _(sync):
                    join(sync)
                    for c in range(nch):
                        if c >= 1:
                            sync.wait_ge(ca.sem, base_a + c)
                        cl.tick(sync.dma_start(out=ewch[:],
                                               in_=dsrc[e, c * 128:(c + 1) * 128, :]))

                @blk.scalar
                def _(a):
                    for c in range(nch):
                        a.wait_ge(cl.sem, base_l + (c + 1) * 16)
                        if c >= 1:
                            a.wait_ge(cv.sem, base_v + c)
                        ca.tick(a.activation(out=ei32[:], in_=ewch[:], func=AF.Identity,
                                             scale=b_wm[:, sidx:sidx + 1]))

                @blk.vector
                def _(v):
                    for c in range(nch):
                        v.wait_ge(ca.sem, base_a + c + 1)
                        cv.tick(v.tensor_scalar(out=qdst[:, c * 2048:(c + 1) * 2048],
                                                in0=ei32[:], scalar1=-1.0, scalar2=1.0,
                                                op0=ALU.max, op1=ALU.min))

        for i in range(CAP // 128):
            with nc.Block() as blk:
                @blk.sync
                def _(sync):
                    join(sync)
                    cl.tick(sync.dma_start(out=xg[:],
                                           in_=s_xg[e * CAP + i * 128:
                                                    e * CAP + (i + 1) * 128, :]))

                @blk.scalar
                def _(a):
                    join(a, cl)
                    ca.tick(a.activation(out=exn[:], in_=xg[:], func=AF.Square,
                                         accum_out=ecol[:, 0:1]))

                @blk.vector
                def _(v):
                    join(v, ca)
                    cv.tick(v.tensor_scalar(out=ecol[:, 0:1], in0=ecol[:, 0:1],
                                            scalar1=1.0 / H, scalar2=1e-8,
                                            op0=ALU.mult, op1=ALU.add))

                @blk.scalar
                def _(a):
                    join(a, cv)
                    ca.tick(a.activation(out=ecol[:, 1:2], in_=ecol[:, 0:1], func=AF.Sqrt))

                @blk.vector
                def _(v):
                    join(v, ca)

                    def vt(ins):
                        cv.tick(ins)
                        v.wait_ge(cv.sem, cv.v)

                    vt(v.reciprocal(ecol[:, 1:2], ecol[:, 1:2]))
                    vt(v.scalar_tensor_tensor(out=exn[:], in0=xg[:], scalar=ecol[:, 1:2],
                                              in1=b_NgE[:], op0=ALU.mult, op1=ALU.mult))
                    vt(v.reduce_max(ecol[:, 2:3], exn[:], axis=AX.X,
                                    apply_absolute_value=True))
                    vt(v.tensor_scalar(out=ecol[:, 2:3], in0=ecol[:, 2:3], scalar1=1e-5,
                                       scalar2=None, op0=ALU.max))
                    vt(v.reciprocal(ecol[:, 3:4], ecol[:, 2:3]))
                    vt(v.tensor_scalar(out=ecol[:, 3:4], in0=ecol[:, 3:4], scalar1=127.0,
                                       scalar2=None, op0=ALU.mult))
                    vt(v.tensor_scalar(out=ecol[:, 2:3], in0=ecol[:, 2:3],
                                       scalar1=1.0 / 127.0, scalar2=None, op0=ALU.mult))

                @blk.scalar
                def _(a):
                    join(a, cv)
                    ca.tick(a.activation(out=ei32[:], in_=exn[:], func=AF.Identity,
                                         scale=ecol[:, 3:4]))

                @blk.vector
                def _(v):
                    join(v, ca)
                    cv.tick(v.tensor_copy(out=exn[:], in_=ei32[:]))

            transpose_set(lambda i2: exn[:, i2 * 128:(i2 + 1) * 128],
                          lambda i2: ekT[:, i2 * 128:(i2 + 1) * 128], HC, 128)

            with nc.Block() as blk:
                @blk.tensor
                def _(t):
                    join(t)
                    for oc in range(4):
                        ins = None
                        for c in range(HC):
                            ins = t.matmul(PS[oc][0:128, 0:512],
                                           ekT[:, c * 128:(c + 1) * 128],
                                           qWgE[:, c * 2048 + oc * 512:
                                                c * 2048 + (oc + 1) * 512],
                                           start=(c == 0), stop=(c == HC - 1))
                        ct.tick(ins)

                @blk.vector
                def _(v):
                    join(v, ct)
                    sidx = mean_slot[f"Wg{e}"]
                    v.wait_ge(cv.sem, cv.v)
                    cv.tick(v.tensor_mul(ecol[:, 4:5], ecol[:, 2:3],
                                         b_wm[:, sidx + 1:sidx + 2]))
                    for oc in range(4):
                        v.wait_ge(cv.sem, cv.v)
                        cv.tick(v.tensor_scalar(out=eyb[:, oc * 512:(oc + 1) * 512],
                                                in0=PS[oc][0:128, 0:512],
                                                scalar1=ecol[:, 4:5], scalar2=None,
                                                op0=ALU.mult))

                @blk.scalar
                def _(a):
                    join(a, cv)
                    ca.tick(a.activation(out=exn[:, 0:IA], in_=eyb[:, 0:IA], func=AF.Silu))

                @blk.vector
                def _(v):
                    join(v, ca)
                    cv.tick(v.tensor_mul(ehh[:], exn[:, 0:IA], eyb[:, IA:2 * IA]))

                @blk.scalar
                def _(a):
                    join(a, cv)
                    ca.tick(a.activation(out=exn[:, 0:IA], in_=ehh[:], func=AF.Square,
                                         accum_out=ecol[:, 0:1]))

                @blk.vector
                def _(v):
                    join(v, ca)
                    cv.tick(v.tensor_scalar(out=ecol[:, 0:1], in0=ecol[:, 0:1],
                                            scalar1=1.0 / IA, scalar2=1e-8,
                                            op0=ALU.mult, op1=ALU.add))

                @blk.scalar
                def _(a):
                    join(a, cv)
                    ca.tick(a.activation(out=ecol[:, 1:2], in_=ecol[:, 0:1], func=AF.Sqrt))

                @blk.vector
                def _(v):
                    join(v, ca)

                    def vt(ins):
                        cv.tick(ins)
                        v.wait_ge(cv.sem, cv.v)

                    vt(v.reciprocal(ecol[:, 1:2], ecol[:, 1:2]))
                    vt(v.scalar_tensor_tensor(out=exn[:, 0:IA], in0=ehh[:],
                                              scalar=ecol[:, 1:2], in1=b_NdE[:],
                                              op0=ALU.mult, op1=ALU.mult))
                    vt(v.reduce_max(ecol[:, 2:3], exn[:, 0:IA], axis=AX.X,
                                    apply_absolute_value=True))
                    vt(v.tensor_scalar(out=ecol[:, 2:3], in0=ecol[:, 2:3], scalar1=1e-5,
                                       scalar2=None, op0=ALU.max))
                    vt(v.reciprocal(ecol[:, 3:4], ecol[:, 2:3]))
                    vt(v.tensor_scalar(out=ecol[:, 3:4], in0=ecol[:, 3:4], scalar1=127.0,
                                       scalar2=None, op0=ALU.mult))
                    vt(v.tensor_scalar(out=ecol[:, 2:3], in0=ecol[:, 2:3],
                                       scalar1=1.0 / 127.0, scalar2=None, op0=ALU.mult))

                @blk.scalar
                def _(a):
                    join(a, cv)
                    ca.tick(a.activation(out=ei32[:, 0:IA], in_=exn[:, 0:IA],
                                         func=AF.Identity, scale=ecol[:, 3:4]))

                @blk.vector
                def _(v):
                    join(v, ca)
                    cv.tick(v.tensor_copy(out=exn[:, 0:IA], in_=ei32[:, 0:IA]))

            transpose_set(lambda i2: exn[:, i2 * 128:(i2 + 1) * 128],
                          lambda i2: ek2T[:, i2 * 128:(i2 + 1) * 128], IAC, 128)

            with nc.Block() as blk:
                @blk.tensor
                def _(t):
                    join(t)
                    for oc in range(4):
                        ins = None
                        for c in range(IAC):
                            ins = t.matmul(PS[oc][0:128, 0:512],
                                           ek2T[:, c * 128:(c + 1) * 128],
                                           qWdE[:, c * 2048 + oc * 512:
                                                c * 2048 + (oc + 1) * 512],
                                           start=(c == 0), stop=(c == IAC - 1))
                        ct.tick(ins)

                @blk.vector
                def _(v):
                    join(v, ct)
                    sidx = mean_slot[f"Wd{e}"]
                    v.wait_ge(cv.sem, cv.v)
                    cv.tick(v.tensor_mul(ecol[:, 4:5], ecol[:, 2:3],
                                         b_wm[:, sidx + 1:sidx + 2]))
                    for oc in range(4):
                        v.wait_ge(cv.sem, cv.v)
                        cv.tick(v.tensor_scalar(out=exn[:, oc * 512:(oc + 1) * 512],
                                                in0=PS[oc][0:128, 0:512],
                                                scalar1=ecol[:, 4:5], scalar2=None,
                                                op0=ALU.mult))

                @blk.sync
                def _(sync):
                    join(sync, cv)
                    cl.tick(sync.dma_start(
                        out=s_moe[e * CAP + i * 128:e * CAP + (i + 1) * 128, :],
                        in_=exn[:]))

    p5.close()

    if KPH < 5:
        root.close()
        return nc
    p6 = ExitStack()
    sb6 = lambda n, s, d: p6.enter_context(nc.sbuf_tensor(n, s, d))
    acc2 = sb6("acc2", [128, H], F32)
    B1 = sb6("B1", [128, H], F32)
    B2 = sb6("B2", [128, H], F32)
    fidx = sb6("fidx", [128, 2], U32)

    with nc.Block() as blk:
        @blk.vector
        def _(v):
            join(v)
            cv.tick(v.memset(B1[:], 0.0))
            cv.tick(v.memset(B2[:], 0.0))

    for k in range(8):
        with nc.Block() as blk:
            @blk.gpsimd
            def _(gp):
                join(gp)
                cp.tick(gp.tensor_copy(out=fidx[:, 0:1], in_=rt_mine[:, k, 4:5]))
                cp.tick(gp.tensor_copy(out=fidx[:, 1:2], in_=rt_mine[:, k, 5:6]))
                gp.wait_ge(cp.sem, cp.v)
                cg.tick(gp.indirect_dma_start(
                    out=B1[:], out_offset=None, in_=s_moe[:],
                    in_offset=IndirectOffsetOnAxis(ap=fidx[:, 0:1], axis=0),
                    bounds_check=T - 1, oob_is_err=False))
                cg.tick(gp.indirect_dma_start(
                    out=B2[:], out_offset=None, in_=s_moe[:],
                    in_offset=IndirectOffsetOnAxis(ap=fidx[:, 1:2], axis=0),
                    bounds_check=T - 1, oob_is_err=False))

            @blk.sync
            def _(sync):
                join(sync)
                cl.tick(sync.dma_start(out=acc2[:], in_=s_acc[k * 128:(k + 1) * 128, :]))

            @blk.vector
            def _(v):
                join(v, cg, cl)
                v.wait_ge(cv.sem, cv.v)
                cv.tick(v.scalar_tensor_tensor(out=acc2[:], in0=B1[:],
                                               scalar=rt_mine[:, k, 2:3], in1=acc2[:],
                                               op0=ALU.mult, op1=ALU.add))
                v.wait_ge(cv.sem, cv.v)
                cv.tick(v.scalar_tensor_tensor(out=acc2[:], in0=B2[:],
                                               scalar=rt_mine[:, k, 3:4], in1=acc2[:],
                                               op0=ALU.mult, op1=ALU.add))

            @blk.sync
            def _(sync):
                join(sync, cv)
                cl.tick(sync.dma_start(out=d_out[k * 128:(k + 1) * 128, :], in_=acc2[:]))

    if "acc" in DBG:
        with nc.Block() as blk:
            @blk.sync
            def _(sync):
                join(sync)
                cl.tick(sync.dma_start(out=dbg["dbg_acc"][:], in_=s_acc[:]))

    p6.close()

    with nc.Block() as blk:
        @blk.sync
        def _(sync):
            join(sync)

    root.close()
    return nc


_NC_CACHE = None
_LAST_RES = None


def _dn_idx(inputs):
    return np.asarray(inputs["L0_ffn_down_reorder_index"]).astype(np.int64)


def kernel(**inputs):
    global _NC_CACHE
    xf = np.ascontiguousarray(inputs["L0_attn_x"].reshape(T, H).astype(np.float32))
    xfT = np.ascontiguousarray(xf.T)

    if _NC_CACHE is None:
        _NC_CACHE = build_program()
    nc = _NC_CACHE

    common = {
        "xfT": xfT,
        "Wg": np.ascontiguousarray(inputs["L0_attn_Wg"], dtype=np.float32),
        "Wd": np.ascontiguousarray(inputs["L0_attn_Wd"], dtype=np.float32),
        "Ng": np.ascontiguousarray(inputs["L0_attn_Ng"], dtype=np.float32),
        "Nd": np.ascontiguousarray(inputs["L0_attn_Nd"], dtype=np.float32),
        "sWg": np.ascontiguousarray(inputs["L0_attn_sWg"], dtype=np.float32),
        "sWd": np.ascontiguousarray(inputs["L0_attn_sWd"], dtype=np.float32),
        "sNg": np.ascontiguousarray(inputs["L0_attn_sNg"], dtype=np.float32).reshape(1, H),
        "sNd": np.ascontiguousarray(inputs["L0_attn_sNd"], dtype=np.float32).reshape(1, IA),
        "gnw": np.ascontiguousarray(inputs["L0_attn_gate_norm_w"],
                                    dtype=np.float32).reshape(1, H),
        "Wgate": np.ascontiguousarray(inputs["L0_attn_W_gate"], dtype=np.float32),
        "Wsg": np.ascontiguousarray(inputs["L0_attn_W_shared_gate"],
                                    dtype=np.float32).reshape(1, H),
        "wgate_f": np.ascontiguousarray(inputs["L0_ffn_w_gate"][_dn_idx(inputs)], dtype=np.float32),
        "wup_f": np.ascontiguousarray(inputs["L0_ffn_w_up"][_dn_idx(inputs)], dtype=np.float32),
        "wdown_f": np.ascontiguousarray(inputs["L0_ffn_w_down"], dtype=np.float32),
    }
    in_maps = []
    for c in range(NCORE):
        m = dict(common)
        m["xsh"] = np.ascontiguousarray(xf[c * TPC:(c + 1) * TPC, :])
        tglob = c * TPC + np.arange(TPC)
        gi = ((tglob % 128) * 64 + (tglob // 128)).astype(np.int32)
        m["gidx"] = np.ascontiguousarray(gi.reshape(8, 128).T)
        in_maps.append(m)

    res = run_bass_kernel_spmd(nc, in_maps, core_ids=list(range(NCORE)))
    global _LAST_RES
    _LAST_RES = res
    out = np.concatenate([np.asarray(res.results[c]["out"]) for c in range(NCORE)], axis=0)
    return out.reshape(inputs["L0_attn_x"].shape).astype(np.float32)


# version 03
# speedup vs baseline: 1.0993x; 1.0993x over previous
import os
import numpy as np
from contextlib import ExitStack

import concourse.bass as bass
import concourse.mybir as mybir
from concourse.bass import IndirectOffsetOnAxis
from concourse.bass_utils import run_bass_kernel_spmd

F32 = mybir.dt.float32
BF16 = mybir.dt.bfloat16
FP8 = mybir.dt.float8e4
I32 = mybir.dt.int32
U32 = mybir.dt.uint32
AF = mybir.ActivationFunctionType
ALU = mybir.AluOpType
AX = mybir.AxisListType

NCORE = 8
T = 8192
TPC = 1024
H = 2048
E = 8
CAP = 1024
IA = 1024
IF = 5632
GT = 256
NG = TPC // GT
NT = GT // 128
HC = H // 128
IFC = IF // 128
IAC = IA // 128
DBG = os.environ.get("KDBG", "")
KPH = int(os.environ.get("KPH", "5"))
KP2 = int(os.environ.get("KP2", "9"))
KQT = int(os.environ.get("KQT", "9"))


class Ctr:
    def __init__(self, sem, step=1):
        self.sem, self.v, self.step = sem, 0, step

    def tick(self, instr, n=None):
        n = self.step if n is None else n
        instr.then_inc(self.sem, n)
        self.v += n
        return self.v


def build_program():
    nc = bass.Bass("TRN2", target_bir_lowering=False)
    root = ExitStack()

    cl = Ctr(root.enter_context(nc.semaphore("c_ld")), 16)
    cs = Ctr(root.enter_context(nc.semaphore("c_lds")), 16)
    cg = Ctr(root.enter_context(nc.semaphore("c_ldg")), 16)
    cv = Ctr(root.enter_context(nc.semaphore("c_v")), 1)
    ca = Ctr(root.enter_context(nc.semaphore("c_a")), 1)
    ct = Ctr(root.enter_context(nc.semaphore("c_t")), 1)
    cp = Ctr(root.enter_context(nc.semaphore("c_g")), 1)
    ALLC = [cl, cs, cg, cv, ca, ct, cp]

    def join(eng, *ctrs):
        for c in (ctrs or ALLC):
            if c.v:
                eng.wait_ge(c.sem, c.v)

    d_xsh = nc.dram_tensor("xsh", [TPC, H], F32, kind="ExternalInput")
    d_xfT = nc.dram_tensor("xfT", [H, T], F32, kind="ExternalInput")
    d_gidx = nc.dram_tensor("gidx", [128, 8], I32, kind="ExternalInput")
    d_Wg = nc.dram_tensor("Wg", [E, H, 2 * IA], F32, kind="ExternalInput")
    d_Wd = nc.dram_tensor("Wd", [E, IA, H], F32, kind="ExternalInput")
    d_Ng = nc.dram_tensor("Ng", [E, H], F32, kind="ExternalInput")
    d_Nd = nc.dram_tensor("Nd", [E, IA], F32, kind="ExternalInput")
    d_sWg = nc.dram_tensor("sWg", [H, 2 * IA], F32, kind="ExternalInput")
    d_sWd = nc.dram_tensor("sWd", [IA, H], F32, kind="ExternalInput")
    d_sNg = nc.dram_tensor("sNg", [1, H], F32, kind="ExternalInput")
    d_sNd = nc.dram_tensor("sNd", [1, IA], F32, kind="ExternalInput")
    d_gnw = nc.dram_tensor("gnw", [1, H], F32, kind="ExternalInput")
    d_Wgate = nc.dram_tensor("Wgate", [H, E], F32, kind="ExternalInput")
    d_Wsg = nc.dram_tensor("Wsg", [1, H], F32, kind="ExternalInput")
    d_wg = nc.dram_tensor("wgate_f", [IF, H], F32, kind="ExternalInput")
    d_wu = nc.dram_tensor("wup_f", [IF, H], F32, kind="ExternalInput")
    d_wd = nc.dram_tensor("wdown_f", [H, IF], F32, kind="ExternalInput")
    d_out = nc.dram_tensor("out", [TPC, H], F32, kind="ExternalOutput")

    s_xp = nc.dram_tensor("s_xp", [TPC, H], F32, kind="Internal")
    s_rtab = nc.dram_tensor("s_rtab", [T, 8], F32, kind="Internal")
    s_xg = nc.dram_tensor("s_xg", [E * CAP, H], F32, kind="Internal")
    s_moe = nc.dram_tensor("s_moe", [E * CAP, H], F32, kind="Internal")
    s_acc = nc.dram_tensor("s_acc", [TPC, H], F32, kind="Internal")
    s_qwgT = nc.dram_tensor("s_qwgT", [HC, IFC // 4, 128, 512], FP8, kind="Internal")
    s_qwuT = nc.dram_tensor("s_qwuT", [HC, IFC // 4, 128, 512], FP8, kind="Internal")
    s_qwdT = nc.dram_tensor("s_qwdT", [IFC, HC // 4, 128, 512], FP8, kind="Internal")
    s_qsWg = nc.dram_tensor("s_qsWg", [H, 2 * IA], FP8, kind="Internal")
    s_qsWd = nc.dram_tensor("s_qsWd", [IA, H], FP8, kind="Internal")
    s_wgs = nc.dram_tensor("s_wgs", [1, IF], F32, kind="Internal")
    s_wus = nc.dram_tensor("s_wus", [1, IF], F32, kind="Internal")
    s_wds = nc.dram_tensor("s_wds", [1, H], F32, kind="Internal")

    dbg = {}
    if "rt" in DBG:
        dbg["dbg_rt"] = nc.dram_tensor("dbg_rt", [TPC, 8], F32, kind="ExternalOutput")
    if "xp" in DBG:
        dbg["dbg_xp"] = nc.dram_tensor("dbg_xp", [TPC, H], F32, kind="ExternalOutput")
    if "acc" in DBG:
        dbg["dbg_acc"] = nc.dram_tensor("dbg_acc", [TPC, H], F32, kind="ExternalOutput")


    sbr = lambda n, s, d: root.enter_context(nc.sbuf_tensor(n, s, d))
    identf = sbr("identf", [128, 128], F32)
    ones_c = sbr("ones_c", [128, 1], F32)
    ones_cb = sbr("ones_cb", [128, 1], BF16)
    ones_r = sbr("ones_r", [1, 128], F32)
    iotaP = sbr("iotaP", [128, 1], I32)
    iotaPf = sbr("iotaPf", [128, 1], F32)
    iotaF = sbr("iotaF", [128, 128], I32)
    lidx = sbr("lidx", [128, 8], U32)
    stg_row = sbr("stg_row", [1, H], F32)
    b_wm = sbr("b_wm", [128, 64], F32)
    rt_mine = sbr("rt_mine", [128, 8, 8], F32)
    g_idx = sbr("g_idx", [128, 8], U32)
    u32c = sbr("u32c", [128, 2], U32)
    stgf = sbr("stgf", [128, 256], F32)
    colx = sbr("colx", [128, 16], F32)
    wrow = sbr("wrow", [1, 64], F32)
    wcl = sbr("wcl", [128, 4], F32)

    PS = [root.enter_context(nc.psum_tensor(f"psb{i}", [128, 512], F32)) for i in range(8)]

    with nc.Block() as blk:
        @blk.sync
        def _(sync):
            cl.tick(sync.dma_start(out=g_idx[:].bitcast(I32), in_=d_gidx[:]))

        @blk.gpsimd
        def _(gp):
            cp.tick(gp.iota(iotaP[:], pattern=[[0, 1]], base=0, channel_multiplier=1))
            cp.tick(gp.iota(iotaF[:], pattern=[[1, 128]], base=0, channel_multiplier=0))
            cp.tick(gp.iota(lidx[:].bitcast(I32), pattern=[[128, 8]], base=0,
                            channel_multiplier=1))

        @blk.vector
        def _(v):
            join(v, cp)
            cv.tick(v.tensor_copy(out=iotaPf[:], in_=iotaP[:]))
            v.wait_ge(cv.sem, cv.v)
            cv.tick(v.tensor_scalar(out=identf[:], in0=iotaF[:], scalar1=iotaPf[:, 0:1],
                                    scalar2=None, op0=ALU.is_equal))
            cv.tick(v.memset(ones_c[:], 1.0))
            cv.tick(v.memset(ones_cb[:], 1.0))
            cv.tick(v.memset(ones_r[:], 1.0))

    def bcast_from_dram(dst, dram_ap, n):
        with nc.Block() as blk:
            @blk.sync
            def _(sync):
                join(sync)
                cl.tick(sync.dma_start(out=stg_row[0:1, 0:n], in_=dram_ap))

            base_v = cv.v

            @blk.tensor
            def _(t):
                join(t, cl, cv)
                for i in range(0, n, 512):
                    w = min(512, n - i)
                    ch = i // 512
                    if ch >= 2:
                        t.wait_ge(cv.sem, base_v + ch - 1)
                    ct.tick(t.matmul(PS[ch % 2][:, 0:w], ones_r[:],
                                     stg_row[0:1, i:i + w], start=True, stop=True))

            base_t = ct.v

            @blk.vector
            def _(v):
                nch = (n + 511) // 512
                for i in range(0, n, 512):
                    w = min(512, n - i)
                    ch = i // 512
                    v.wait_ge(ct.sem, base_t - nch + ch + 1)
                    cv.tick(v.tensor_copy(out=dst[:, i:i + w], in_=PS[ch % 2][:, 0:w]))

    if KPH < 1:
        root.close()
        return nc
    p1o = ExitStack()
    sbo = lambda n, s, d: p1o.enter_context(nc.sbuf_tensor(n, s, d))
    rt = sbo("rt", [128, 64, 10], F32)
    iot8i = sbo("iot8i", [128, 64, 8], I32)
    iot8 = sbo("iot8", [128, 64, 8], F32)
    iot8p = sbo("iot8p", [128, 64, 8], F32)
    L3 = sbo("L3", [128, 64, 8], F32)
    P3 = sbo("P3", [128, 64, 8], F32)
    M3 = sbo("M3", [128, 64, 8], F32)
    oh1 = sbo("oh1", [128, 64, 8], F32)
    oh2 = sbo("oh2", [128, 64, 8], F32)
    K3 = sbo("K3", [128, 64, 8], F32)
    C3 = sbo("C3", [128, 64, 8], F32)
    sc = sbo("sc", [128, 64, 8], F32)
    rpk = sbo("rpk", [128, 64, 8], F32)

    def col(tile, k):
        return tile[:, :, k:k + 1]

    def colb(tile, k):
        return tile[:, :, k:k + 1].to_broadcast([128, 64, 8])

    p1a = ExitStack()
    sba = lambda n, s, d: p1a.enter_context(nc.sbuf_tensor(n, s, d))
    xp0 = sba("xp0", [128, 2048], F32)
    xp1 = sba("xp1", [128, 2048], F32)
    xpieces = [xp0, xp1]
    xsq = sba("xsq", [128, 2048], BF16)
    sums = sba("sums", [33, T], F32)
    wgp = sba("wgp", [128, HC * 9], F32)
    gnwF = sba("gnwF", [128, HC], F32)
    csr = sba("csr", [1, 8], F32)

    with nc.Block() as blk:
        @blk.sync
        def _(sync):
            join(sync)
            cl.tick(sync.dma_start(out=wgp[:].rearrange("p (c e) -> p c e", c=HC)[:, :, 0:8],
                                   in_=d_Wgate[:].rearrange("(c p) e -> p c e", p=128)))
            for c in range(HC):
                cl.tick(sync.dma_start(
                    out=gnwF[:, c:c + 1],
                    in_=d_gnw[0:1, c * 128:(c + 1) * 128]
                    .rearrange("o (p q) -> (o p) q", q=1)))

        @blk.gpsimd
        def _(gp):
            cp.tick(gp.iota(iot8i[:], pattern=[[0, 64], [1, 8]], base=0,
                            channel_multiplier=0))

        @blk.vector
        def _(v):
            join(v, cl, cp)
            wgp3 = wgp[:].rearrange("p (c e) -> p c e", c=HC)
            cv.tick(v.tensor_mul(wgp3[:, :, 0:8], wgp3[:, :, 0:8],
                                 gnwF[:].unsqueeze(2).to_broadcast([128, HC, 8])))
            cv.tick(v.memset(wgp3[:, :, 8:9], 1.0))
            cv.tick(v.memset(sums[:], 0.0))
            cv.tick(v.tensor_copy(out=iot8[:], in_=iot8i[:]))
            v.wait_ge(cv.sem, cv.v)
            cv.tick(v.tensor_scalar(out=iot8p[:], in0=iot8[:], scalar1=64.0,
                                    scalar2=None, op0=ALU.add))

    with nc.Block() as blk:
        @blk.tensor
        def _(t):
            join(t)
            ins = None
            for c in range(HC):
                ins = t.matmul(PS[0][0:1, 0:8], ones_c[:], wgp[:, c * 9:c * 9 + 8],
                               start=(c == 0), stop=(c == HC - 1))
            ct.tick(ins)

        @blk.vector
        def _(v):
            join(v, ct)
            cv.tick(v.tensor_copy(out=csr[:], in_=PS[0][0:1, 0:8]))

    with nc.Block() as blk:
        @blk.tensor
        def _(t):
            join(t)
            ct.tick(t.matmul(PS[1][:, 0:8], ones_r[:], csr[:], start=True, stop=True))

        @blk.vector
        def _(v):
            join(v, ct)
            cv.tick(v.tensor_copy(out=colx[:, 8:16], in_=PS[1][:, 0:8]))

    with nc.Block() as blk:
        NPC, NSL = 4, 4
        base_l = cl.v
        base_a = ca.v
        base_t = ct.v
        base_v = cv.v

        @blk.sync
        def _(sync):
            join(sync)
            n = 0
            for c in range(HC):
                for q in range(NPC):
                    if n >= 2:
                        sync.wait_ge(ct.sem, base_t + (n - 1) * 2 * NSL)
                        sync.wait_ge(ca.sem, base_a + n - 1)
                    cl.tick(sync.dma_start(out=xpieces[n % 2][:],
                                           in_=d_xfT[c * 128:(c + 1) * 128,
                                                     q * 2048:(q + 1) * 2048]))
                    n += 1

        @blk.scalar
        def _(a):
            n = 0
            for c in range(HC):
                for q in range(NPC):
                    a.wait_ge(cl.sem, base_l + (n + 1) * 16)
                    if n >= 1:
                        a.wait_ge(ct.sem, base_t + n * 2 * NSL)
                    ca.tick(a.activation(out=xsq[:], in_=xpieces[n % 2][:], func=AF.Square))
                    n += 1

        @blk.tensor
        def _(t):
            n = 0
            for c in range(HC):
                for q in range(NPC):
                    t.wait_ge(cl.sem, base_l + (n + 1) * 16)
                    t.wait_ge(ca.sem, base_a + n + 1)
                    for j in range(NSL):
                        gs = n * NSL + j
                        bk = (gs % 4) * 2
                        if gs >= 4:
                            t.wait_ge(cv.sem, base_v + (gs - 3) * 2)
                        ct.tick(t.matmul(PS[bk][0:9, 0:512], wgp[:, c * 9:(c + 1) * 9],
                                         xpieces[n % 2][:, j * 512:(j + 1) * 512],
                                         start=True, stop=True))
                        ct.tick(t.matmul(PS[bk + 1][0:1, 0:512], ones_cb[:],
                                         xsq[:, j * 512:(j + 1) * 512],
                                         start=True, stop=True))
                    n += 1

        @blk.vector
        def _(v):
            n = 0
            for c in range(HC):
                for q in range(NPC):
                    for j in range(NSL):
                        gs = n * NSL + j
                        bk = (gs % 4) * 2
                        tok = (q * NSL + j)
                        v.wait_ge(ct.sem, base_t + (gs + 1) * 2)
                        sl = slice(tok * 512, (tok + 1) * 512)
                        if c == 0:
                            cv.tick(v.tensor_copy(out=sums[0:9, sl], in_=PS[bk][0:9, 0:512]))
                            cv.tick(v.tensor_copy(out=sums[32:33, sl],
                                                  in_=PS[bk + 1][0:1, 0:512]))
                        else:
                            v.wait_ge(cv.sem, cv.v)
                            cv.tick(v.tensor_add(sums[0:9, sl], sums[0:9, sl],
                                                 PS[bk][0:9, 0:512]))
                            cv.tick(v.tensor_add(sums[32:33, sl], sums[32:33, sl],
                                                 PS[bk + 1][0:1, 0:512]))
                    n += 1

    with nc.Block() as blk:
        base_v = cv.v
        base_t = ct.v

        @blk.tensor
        def _(t):
            join(t)
            for k in range(64):
                if k >= 2:
                    t.wait_ge(cv.sem, base_v + (k - 1) * 2)
                ct.tick(t.transpose(PS[k % 2][0:128, 0:9], sums[0:9, k * 128:(k + 1) * 128],
                                    identf[0:9, 0:9]))
                ct.tick(t.transpose(PS[2 + k % 2][0:128, 0:1],
                                    sums[32:33, k * 128:(k + 1) * 128],
                                    identf[32:33, 32:33]))

        @blk.vector
        def _(v):
            for k in range(64):
                v.wait_ge(ct.sem, base_t + 2 * k + 2)
                cv.tick(v.tensor_copy(out=rt[:, k, 0:9], in_=PS[k % 2][0:128, 0:9]))
                cv.tick(v.tensor_copy(out=rt[:, k, 9:10], in_=PS[2 + k % 2][0:128, 0:1]))

    p1a.close()

    p1b = ExitStack()
    sbb = lambda n, s, d: p1b.enter_context(nc.sbuf_tensor(n, s, d))
    maskT = sbb("maskT", [8, T], BF16)
    cumT = sbb("cumT", [8, T], F32)
    keptT = sbb("keptT", [8, T], BF16)
    keptf = sbb("keptf", [8, T], F32)

    with nc.Block() as blk:
        @blk.vector
        def _(v):
            join(v)

            def vt(ins):
                cv.tick(ins)
                v.wait_ge(cv.sem, cv.v)

            vt(v.tensor_scalar(out=col(sc, 0), in0=rt[:, :, 8:9], scalar1=1.0 / H,
                               scalar2=None, op0=ALU.mult))
            vt(v.tensor_scalar(out=col(sc, 1), in0=rt[:, :, 9:10], scalar1=1.0 / H,
                               scalar2=None, op0=ALU.mult))
            vt(v.tensor_mul(col(sc, 2), col(sc, 0), col(sc, 0)))
            vt(v.tensor_sub(col(sc, 1), col(sc, 1), col(sc, 2)))
            vt(v.tensor_scalar(out=col(sc, 2), in0=col(sc, 1), scalar1=1e-5,
                               scalar2=None, op0=ALU.add))
            vt(v.reciprocal(col(sc, 4), col(sc, 2)))
            vt(v.tensor_mul(col(sc, 5), col(sc, 1), col(sc, 4)))
            vt(v.tensor_scalar(out=col(sc, 5), in0=col(sc, 5), scalar1=1e-6,
                               scalar2=None, op0=ALU.add))

        @blk.scalar
        def _(a):
            join(a, cv)
            ca.tick(a.activation(out=col(sc, 6), in_=col(sc, 2), func=AF.Sqrt))
            ca.tick(a.activation(out=col(sc, 7), in_=col(sc, 5), func=AF.Sqrt))

        @blk.vector
        def _(v):
            join(v, ca)

            def vt(ins):
                cv.tick(ins)
                v.wait_ge(cv.sem, cv.v)

            vt(v.reciprocal(col(sc, 2), col(sc, 6)))
            vt(v.reciprocal(col(sc, 3), col(sc, 7)))
            vt(v.tensor_mul(col(sc, 3), col(sc, 3), col(sc, 2)))
            vt(v.tensor_mul(L3[:], colx[:, 8:16].unsqueeze(1)
                            .to_broadcast([128, 64, 8]), colb(sc, 0)))
            vt(v.tensor_sub(L3[:], rt[:, :, 0:8], L3[:]))
            vt(v.tensor_mul(L3[:], L3[:], colb(sc, 3)))
            vt(v.reduce_max(col(sc, 4), L3[:], axis=AX.X))
            vt(v.tensor_sub(P3[:], L3[:], colb(sc, 4)))

        @blk.scalar
        def _(a):
            join(a, cv)
            ca.tick(a.activation(out=P3[:], in_=P3[:], func=AF.Exp))

        @blk.vector
        def _(v):
            join(v, ca)

            def vt(ins):
                cv.tick(ins)
                v.wait_ge(cv.sem, cv.v)

            vt(v.reduce_sum(col(sc, 5), P3[:], axis=AX.X))
            vt(v.reciprocal(col(sc, 5), col(sc, 5)))
            vt(v.tensor_mul(P3[:], P3[:], colb(sc, 5)))
            vt(v.reduce_max(col(sc, 4), L3[:], axis=AX.X))
            vt(v.tensor_tensor(out=M3[:], in0=L3[:], in1=colb(sc, 4), op=ALU.is_ge))
            vt(v.scalar_tensor_tensor(out=M3[:], in0=M3[:], scalar=-64.0, in1=iot8p[:],
                                      op0=ALU.mult, op1=ALU.add))
            vt(v.tensor_reduce(col(sc, 6), M3[:], axis=AX.X, op=ALU.min))
            vt(v.tensor_tensor(out=oh1[:], in0=iot8[:], in1=colb(sc, 6), op=ALU.is_equal))
            vt(v.tensor_scalar(out=M3[:], in0=oh1[:], scalar1=1e30, scalar2=None,
                               op0=ALU.mult))
            vt(v.tensor_sub(M3[:], L3[:], M3[:]))
            vt(v.reduce_max(col(sc, 4), M3[:], axis=AX.X))
            vt(v.tensor_tensor(out=K3[:], in0=M3[:], in1=colb(sc, 4), op=ALU.is_ge))
            vt(v.scalar_tensor_tensor(out=K3[:], in0=K3[:], scalar=-64.0, in1=iot8p[:],
                                      op0=ALU.mult, op1=ALU.add))
            vt(v.tensor_reduce(col(sc, 7), K3[:], axis=AX.X, op=ALU.min))
            vt(v.tensor_tensor(out=oh2[:], in0=iot8[:], in1=colb(sc, 7), op=ALU.is_equal))
            vt(v.tensor_mul(M3[:], P3[:], oh1[:]))
            vt(v.reduce_sum(col(sc, 0), M3[:], axis=AX.X))
            vt(v.tensor_mul(M3[:], P3[:], oh2[:]))
            vt(v.reduce_sum(col(sc, 1), M3[:], axis=AX.X))
            vt(v.tensor_add(M3[:], oh1[:], oh2[:]))
            vt(v.tensor_scalar(out=K3[:], in0=P3[:], scalar1=0.0, scalar2=None,
                               op0=ALU.is_gt))
            vt(v.tensor_mul(M3[:], M3[:], K3[:]))

    with nc.Block() as blk:
        base_v = cv.v
        base_t = ct.v

        @blk.tensor
        def _(t):
            join(t)
            for k in range(64):
                if k >= 2:
                    t.wait_ge(cv.sem, base_v + k - 1)
                ct.tick(t.transpose(PS[k % 2][0:8, 0:128], M3[:, k, :], identf[:]))

        @blk.vector
        def _(v):
            for k in range(64):
                v.wait_ge(ct.sem, base_t + k + 1)
                cv.tick(v.tensor_copy(out=maskT[0:8, k * 128:(k + 1) * 128],
                                      in_=PS[k % 2][0:8, 0:128]))

    with nc.Block() as blk:
        @blk.vector
        def _(v):
            join(v)

            def vt(ins):
                cv.tick(ins)
                v.wait_ge(cv.sem, cv.v)

            vt(v.tensor_tensor_scan(out=cumT[:], data0=maskT[:], data1=maskT[:],
                                    initial=0.0, op0=ALU.add, op1=ALU.max))
            vt(v.tensor_scalar(out=keptT[:], in0=cumT[:], scalar1=float(CAP),
                               scalar2=None, op0=ALU.is_le))
            vt(v.tensor_mul(keptT[:], keptT[:], maskT[:]))
            vt(v.tensor_copy(out=keptf[:], in_=keptT[:]))
            for s in range(NCORE):
                sl = slice(s * TPC, (s + 1) * TPC)
                vt(v.tensor_tensor_scan(out=cumT[:, sl], data0=keptT[:, sl],
                                        data1=keptT[:, sl], initial=0.0,
                                        op0=ALU.add, op1=ALU.max))

    with nc.Block() as blk:
        base_t = ct.v
        base_v = cv.v

        @blk.tensor
        def _(t):
            join(t)
            for k in range(64):
                if k >= 2:
                    t.wait_ge(cv.sem, base_v + (k - 1) * 2)
                ct.tick(t.transpose(PS[4 + k % 2][0:128, 0:8],
                                    keptf[0:8, k * 128:(k + 1) * 128], identf[0:8, 0:8]))
                ct.tick(t.transpose(PS[6 + k % 2][0:128, 0:8],
                                    cumT[0:8, k * 128:(k + 1) * 128], identf[0:8, 0:8]))

        @blk.vector
        def _(v):
            for k in range(64):
                v.wait_ge(ct.sem, base_t + 2 * k + 2)
                cv.tick(v.tensor_copy(out=K3[:, k, :], in_=PS[4 + k % 2][0:128, 0:8]))
                cv.tick(v.tensor_copy(out=C3[:, k, :], in_=PS[6 + k % 2][0:128, 0:8]))

    with nc.Block() as blk:
        @blk.vector
        def _(v):
            join(v)

            def vt(ins):
                cv.tick(ins)
                v.wait_ge(cv.sem, cv.v)

            vt(v.tensor_mul(M3[:], K3[:], oh1[:]))
            vt(v.reduce_sum(col(sc, 2), M3[:], axis=AX.X))
            vt(v.tensor_mul(M3[:], K3[:], oh2[:]))
            vt(v.reduce_sum(col(sc, 3), M3[:], axis=AX.X))
            vt(v.tensor_mul(M3[:], C3[:], oh1[:]))
            vt(v.reduce_sum(col(sc, 4), M3[:], axis=AX.X))
            vt(v.tensor_mul(M3[:], C3[:], oh2[:]))
            vt(v.reduce_sum(col(sc, 5), M3[:], axis=AX.X))
            vt(v.tensor_scalar(out=col(sc, 4), in0=col(sc, 4), scalar1=1.0,
                               scalar2=None, op0=ALU.subtract))
            vt(v.tensor_scalar(out=col(sc, 5), in0=col(sc, 5), scalar1=1.0,
                               scalar2=None, op0=ALU.subtract))
            vt(v.tensor_copy(out=col(rpk, 0), in_=col(sc, 6)))
            vt(v.tensor_copy(out=col(rpk, 1), in_=col(sc, 7)))
            vt(v.tensor_mul(col(rpk, 2), col(sc, 0), col(sc, 2)))
            vt(v.tensor_mul(col(rpk, 3), col(sc, 1), col(sc, 3)))
            vt(v.tensor_scalar(out=col(rpk, 4), in0=col(sc, 6), scalar1=float(CAP),
                               scalar2=None, op0=ALU.mult))
            vt(v.tensor_add(col(rpk, 4), col(rpk, 4), col(sc, 4)))
            vt(v.tensor_scalar(out=col(rpk, 6), in0=col(sc, 2), scalar1=-65536.0,
                               scalar2=65536.0, op0=ALU.mult, op1=ALU.add))
            vt(v.tensor_add(col(rpk, 4), col(rpk, 4), col(rpk, 6)))
            vt(v.tensor_scalar(out=col(rpk, 5), in0=col(sc, 7), scalar1=float(CAP),
                               scalar2=None, op0=ALU.mult))
            vt(v.tensor_add(col(rpk, 5), col(rpk, 5), col(sc, 5)))
            vt(v.tensor_scalar(out=col(rpk, 6), in0=col(sc, 3), scalar1=-65536.0,
                               scalar2=65536.0, op0=ALU.mult, op1=ALU.add))
            vt(v.tensor_add(col(rpk, 5), col(rpk, 5), col(rpk, 6)))
            vt(v.tensor_copy(out=col(rpk, 6), in_=col(sc, 4)))
            vt(v.tensor_copy(out=col(rpk, 7), in_=col(sc, 5)))

        @blk.sync
        def _(sync):
            join(sync, cv)
            cl.tick(sync.dma_start(out=s_rtab[:].rearrange("(p k) f -> p k f", p=128),
                                   in_=rpk[:]))

        @blk.gpsimd
        def _(gp):
            join(gp, cl)
            for k in range(8):
                cg.tick(gp.indirect_dma_start(
                    out=rt_mine[:, k, :], out_offset=None,
                    in_=s_rtab[:],
                    in_offset=IndirectOffsetOnAxis(ap=g_idx[:, k:k + 1], axis=0),
                    bounds_check=T - 1, oob_is_err=False))

    if "rt" in DBG:
        with nc.Block() as blk:
            @blk.sync
            def _(sync):
                join(sync)
                for k in range(8):
                    cl.tick(sync.dma_start(out=dbg["dbg_rt"][k * 128:(k + 1) * 128, :],
                                           in_=rt_mine[:, k, :]))

    p1b.close()
    p1o.close()

    if KPH < 2:
        root.close()
        return nc
    mean_slot = {}
    slot_n = [0]

    p2 = ExitStack()
    sb2 = lambda n, s, d: p2.enter_context(nc.sbuf_tensor(n, s, d))
    wch0 = sb2("wch0", [128, IF], F32)
    wch1 = sb2("wch1", [128, IF], F32)
    wchs = [wch0, wch1]
    q8ch = sb2("q8ch", [128, IF], FP8)
    absd = sb2("absd", [128, IF], BF16)
    i32ch = sb2("i32ch", [128, 2048], I32)
    stag = sb2("stag", [128, 16 * 512], FP8)
    stagd = sb2("stagd", [128, 44 * 512], FP8)

    def mean_job(mk, nch, wdt, key):
        sidx = slot_n[0]
        mean_slot[key] = sidx
        slot_n[0] += 2
        with nc.Block() as blk:
            base_l = cl.v
            base_a = ca.v

            @blk.sync
            def _(sync):
                join(sync)
                for c in range(nch):
                    if c >= 2:
                        sync.wait_ge(ca.sem, base_a + c - 1)
                    cl.tick(sync.dma_start(out=wchs[c % 2][:, 0:wdt], in_=mk(c)))

            base_v = cv.v

            @blk.scalar
            def _(a):
                for c in range(nch):
                    a.wait_ge(cl.sem, base_l + (c + 1) * 16)
                    if c >= 1:
                        a.wait_ge(cv.sem, base_v + c)
                    ca.tick(a.activation(out=absd[:, 0:wdt], in_=wchs[c % 2][:, 0:wdt],
                                         func=AF.Abs, accum_out=wcl[:, 0:1]))

            @blk.vector
            def _(v):
                for c in range(nch):
                    v.wait_ge(ca.sem, base_a + c + 1)
                    v.wait_ge(cv.sem, cv.v)
                    if c == 0:
                        cv.tick(v.tensor_copy(out=wcl[:, 1:2], in_=wcl[:, 0:1]))
                    else:
                        cv.tick(v.tensor_add(wcl[:, 1:2], wcl[:, 1:2], wcl[:, 0:1]))

        with nc.Block() as blk:
            @blk.tensor
            def _(t):
                join(t)
                ct.tick(t.matmul(PS[0][0:1, 0:1], wcl[:, 1:2], ones_c[:],
                                 start=True, stop=True))

            @blk.vector
            def _(v):
                join(v, ct)
                nel = float(nch * 128 * wdt)
                cv.tick(v.tensor_scalar(out=wrow[0:1, sidx + 1:sidx + 2],
                                        in0=PS[0][0:1, 0:1], scalar1=1.0 / nel,
                                        scalar2=1e-5, op0=ALU.mult, op1=ALU.max))
                v.wait_ge(cv.sem, cv.v)
                cv.tick(v.reciprocal(wrow[0:1, sidx:sidx + 1],
                                     wrow[0:1, sidx + 1:sidx + 2]))

    if KP2 >= 1:
        mean_job(lambda c: d_sWg[c * 128:(c + 1) * 128, :], HC, 2048, "sWg")
    if KP2 >= 2:
        mean_job(lambda c: d_sWd[c * 128:(c + 1) * 128, :], IAC, 2048, "sWd")
        for e in range(E):
            mean_job(lambda c, e=e: d_Wg[e, c * 128:(c + 1) * 128, :], HC, 2048, f"Wg{e}")
            mean_job(lambda c, e=e: d_Wd[e, c * 128:(c + 1) * 128, :], IAC, 2048, f"Wd{e}")

    with nc.Block() as blk:
        @blk.tensor
        def _(t):
            join(t)
            ct.tick(t.matmul(PS[1][:, 0:64], ones_r[:], wrow[:], start=True, stop=True))

        @blk.vector
        def _(v):
            join(v, ct)
            cv.tick(v.tensor_copy(out=b_wm[:], in_=PS[1][:, 0:64]))

    def tern_quant2(mk_in, mk_out, nch, wdt, key):
        sidx = mean_slot[key]
        with nc.Block() as blk:
            base_l = cl.v
            base_a = ca.v
            base_v = cv.v
            base_s = cs.v

            @blk.sync
            def _(sync):
                join(sync)
                for c in range(nch):
                    if c >= 2:
                        sync.wait_ge(ca.sem, base_a + c - 1)
                    cl.tick(sync.dma_start(out=wchs[c % 2][:, 0:wdt], in_=mk_in(c)))

            @blk.scalar
            def _(a):
                for c in range(nch):
                    a.wait_ge(cl.sem, base_l + (c + 1) * 16)
                    if c >= 1:
                        a.wait_ge(cv.sem, base_v + c)
                    ca.tick(a.activation(out=i32ch[:, 0:wdt], in_=wchs[c % 2][:, 0:wdt],
                                         func=AF.Identity, scale=b_wm[:, sidx:sidx + 1]))
                    a.wait_ge(cv.sem, base_v + c + 1)
                    cs.tick(a.dma_start(out=mk_out(c), in_=q8ch[:, 0:wdt]))

            @blk.vector
            def _(v):
                for c in range(nch):
                    v.wait_ge(ca.sem, base_a + c + 1)
                    if c >= 1:
                        v.wait_ge(cs.sem, base_s + c * 16)
                    cv.tick(v.tensor_scalar(out=q8ch[:, 0:wdt], in0=i32ch[:, 0:wdt],
                                            scalar1=-1.0, scalar2=1.0,
                                            op0=ALU.max, op1=ALU.min))

    if KP2 >= 4:
        tern_quant2(lambda c: d_sWg[c * 128:(c + 1) * 128, :],
                    lambda c: s_qsWg[c * 128:(c + 1) * 128, :], HC, 2048, "sWg")
        tern_quant2(lambda c: d_sWd[c * 128:(c + 1) * 128, :],
                    lambda c: s_qsWd[c * 128:(c + 1) * 128, :], IAC, 2048, "sWd")

    def fp8_quant_T(load_ap, nch, rowlen, wsrow_store, stag_buf, tile_in_group,
                    flush_after, nflush):
        n_tr = rowlen // 128
        vper = 5 + (n_tr if KQT >= 2 else 0)
        tper = n_tr + (1 if KQT >= 5 else 0)
        aper = n_tr + (1 if KQT >= 5 else 0)
        with nc.Block() as blk:
            base_l = cl.v
            base_v = cv.v
            base_a = ca.v
            base_t = ct.v
            base_g = cg.v

            @blk.sync
            def _(sync):
                join(sync)
                for c in range(nch):
                    if c >= 2:
                        sync.wait_ge(cv.sem, base_v + (c - 1) * vper)
                    cl.tick(sync.dma_start(out=wchs[c % 2][:, 0:rowlen], in_=load_ap(c)))

            @blk.vector
            def _(v):
                for c in range(nch):
                    v.wait_ge(cl.sem, base_l + (c + 1) * 16)
                    v.wait_ge(cv.sem, cv.v)
                    cv.tick(v.reduce_max(wcl[:, 0:1], wchs[c % 2][:, 0:rowlen],
                                         axis=AX.X, apply_absolute_value=True))
                    v.wait_ge(cv.sem, cv.v)
                    cv.tick(v.reciprocal(wcl[:, 1:2], wcl[:, 0:1]))
                    v.wait_ge(cv.sem, cv.v)
                    cv.tick(v.tensor_scalar(out=wcl[:, 2:3], in0=wcl[:, 1:2],
                                            scalar1=224.0, scalar2=None, op0=ALU.mult))
                    cv.tick(v.tensor_scalar(out=wcl[:, 3:4], in0=wcl[:, 0:1],
                                            scalar1=1.0 / 224.0, scalar2=None,
                                            op0=ALU.mult))
                    v.wait_ge(cv.sem, cv.v)
                    cv.tick(v.tensor_scalar(out=q8ch[:, 0:rowlen], in0=wchs[c % 2][:, 0:rowlen],
                                            scalar1=wcl[:, 2:3], scalar2=None, op0=ALU.mult))
                    v.wait_ge(cv.sem, cv.v)
                    if KQT < 2:
                        continue
                    for tr in range(n_tr):
                        idx = c * n_tr + tr
                        if idx >= 2 and KQT >= 3:
                            t_idx = (c * tper + tr - 2 if tr >= 2
                                     else (c - 1) * tper + n_tr + tr - 2)
                            v.wait_ge(ct.sem, base_t + t_idx + 1)
                        cv.tick(v.tensor_copy(out=stgf[:, (idx % 2) * 128:(idx % 2) * 128 + 128],
                                              in_=q8ch[:, tr * 128:(tr + 1) * 128]))

            @blk.tensor
            def _(t):
                if KQT < 3:
                    return
                for c in range(nch):
                    for tr in range(n_tr):
                        t.wait_ge(cv.sem, base_v + c * vper + 5 + tr + 1)
                        aidx = c * aper + tr
                        if aidx >= 2 and KQT >= 4:
                            t.wait_ge(ca.sem, base_a + aidx - 1)
                        ct.tick(t.transpose(PS[4 + (c * n_tr + tr) % 2][0:128, 0:128],
                                            stgf[:, ((c * n_tr + tr) % 2) * 128:
                                                 ((c * n_tr + tr) % 2) * 128 + 128],
                                            identf[:]))
                    if KQT < 5:
                        continue
                    t.wait_ge(cv.sem, base_v + c * vper + 4)
                    if c >= 1 and KQT >= 4:
                        t.wait_ge(ca.sem, base_a + c * aper)
                    ct.tick(t.transpose(PS[6][0:1, 0:128], wcl[:, 3:4], identf[:]))

            base_s = cs.v

            @blk.scalar
            def _(a):
                if KQT < 4:
                    return
                for c in range(nch):
                    if c % 4 == 0 and c > 0 and KQT >= 6:
                        a.wait_ge(cs.sem, base_s + (c // 4) * nflush * 16)
                    for tr in range(n_tr):
                        a.wait_ge(ct.sem, base_t + c * tper + tr + 1)
                        dstt, dsto = tile_in_group(c, tr)
                        ca.tick(a.activation(out=stag_buf[:, dstt * 512 + dsto * 128:
                                                          dstt * 512 + dsto * 128 + 128],
                                             in_=PS[4 + (c * n_tr + tr) % 2][0:128, 0:128],
                                             func=AF.Identity))
                    if KQT >= 5:
                        a.wait_ge(ct.sem, base_t + c * tper + tper)
                        ca.tick(a.activation(out=wsrow_store(c), in_=PS[6][0:1, 0:128],
                                             func=AF.Identity))
                    if KQT >= 6:
                        for (dram_ap, ssl) in flush_after(c):
                            cs.tick(a.dma_start(out=dram_ap, in_=stag_buf[:, ssl]))

    wsr_wg = sb2("wsr_wg", [1, IF], F32)
    wsr_wu = sb2("wsr_wu", [1, IF], F32)
    wsr_wd = sb2("wsr_wd", [1, H], F32)

    def tile_ig_up(c, tr):
        return (tr, c % 4)

    def flush_up(s_out):
        def fl(c):
            if c % 4 == 3:
                ic4 = c // 4
                return [(s_out[hc, ic4, :, :], slice(hc * 512, (hc + 1) * 512))
                        for hc in range(HC)]
            return []
        return fl

    def tile_ig_dn(c, tr):
        return (tr, c % 4)

    def flush_dn(c):
        if c % 4 == 3:
            hc4 = c // 4
            return [(s_qwdT[ic, hc4, :, :], slice(ic * 512, (ic + 1) * 512))
                    for ic in range(IFC)]
        return []

    if KP2 >= 5:
        fp8_quant_T(lambda c: d_wg[c * 128:(c + 1) * 128, :], IFC, 2048,
                    lambda c: wsr_wg[0:1, c * 128:(c + 1) * 128], stag, tile_ig_up,
                    flush_up(s_qwgT), HC)
    if KP2 >= 6:
        fp8_quant_T(lambda c: d_wu[c * 128:(c + 1) * 128, :], IFC, 2048,
                    lambda c: wsr_wu[0:1, c * 128:(c + 1) * 128], stag, tile_ig_up,
                    flush_up(s_qwuT), HC)
    if KP2 >= 7:
        fp8_quant_T(lambda c: d_wd[c * 128:(c + 1) * 128, :], HC, IF,
                    lambda c: wsr_wd[0:1, c * 128:(c + 1) * 128], stagd, tile_ig_dn,
                    flush_dn, IFC)

    with nc.Block() as blk:
        @blk.sync
        def _(sync):
            join(sync)
            cl.tick(sync.dma_start(out=s_wgs[:], in_=wsr_wg[:]))
            cl.tick(sync.dma_start(out=s_wus[:], in_=wsr_wu[:]))
            cl.tick(sync.dma_start(out=s_wds[:], in_=wsr_wd[:]))

    p2.close()

    if KPH < 3:
        root.close()
        return nc
    p4 = ExitStack()
    sb4 = lambda n, s, d: p4.enter_context(nc.sbuf_tensor(n, s, d))
    b_sNg = sb4("b_sNg", [128, H], F32)
    b_sNd = sb4("b_sNd", [128, IA], F32)
    b_Wsg = sb4("b_Wsg", [128, H], F32)
    arena = sb4("arena", [128, NT * IF], F32)
    qtTs = sb4("qtTs", [128, NT * IF], BF16)
    qxT = sb4("qxT", [128, NT * H], BF16)
    accg = sb4("accg", [128, NT * H], F32)
    xpg = sb4("xpg", [128, NT * H], F32)
    q8g = sb4("q8g", [128, NT * IF], FP8)
    gch = sb4("gch", [128, NT * 512], F32)
    uch = sb4("uch", [128, NT * 512], F32)
    sich = sb4("sich", [128, NT * 512], F32)
    slabU = sb4("slabU", [128, IFC * 512], FP8)
    wslab = slabU
    wdslab = slabU
    tslab = slabU
    wbc = sb4("wbc", [128, 1024], F32)
    shar = arena[:, 0:NT * H]
    shy = arena[:, NT * H:2 * NT * H]
    shh = arena[:, 2 * NT * H:2 * NT * H + NT * IA]
    i32g = qtTs[:].bitcast(I32)[:, 0:NT * H]
    k1T = sb4("k1T", [128, NT * H], BF16)
    k2T = sb4("k2T", [128, NT * IA], BF16)
    gcol = sb4("gcol", [128, NT * 16], F32)

    bcast_from_dram(b_sNg, d_sNg[0:1, :], H)
    bcast_from_dram(b_sNd, d_sNd[0:1, :], IA)
    bcast_from_dram(b_Wsg, d_Wsg[0:1, :], H)

    def gc(kt, i):
        return gcol[:, kt * 16 + i:kt * 16 + i + 1]

    def transpose_set(srcf, dst, nblk, blk128):
        with nc.Block() as blk:
            base_t = ct.v
            base_a = ca.v

            @blk.tensor
            def _(t):
                join(t)
                for i in range(nblk):
                    if i >= 2:
                        t.wait_ge(ca.sem, base_a + i - 1)
                    ct.tick(t.transpose(PS[4 + i % 2][0:128, 0:128], srcf(i), identf[:]))

            @blk.scalar
            def _(a):
                for i in range(nblk):
                    a.wait_ge(ct.sem, base_t + i + 1)
                    ca.tick(a.activation(out=dst(i), in_=PS[4 + i % 2][0:128, 0:128],
                                         func=AF.Identity))

    for g in range(NG):
        r0 = g * GT

        with nc.Block() as blk:
            @blk.sync
            def _(sync):
                join(sync)
                cl.tick(sync.dma_start(out=accg[:].rearrange("p (k h) -> p k h", k=NT),
                                       in_=d_xsh[r0:r0 + GT, :]
                                       .rearrange("(k p) h -> p k h", p=128)))

            @blk.scalar
            def _(a):
                join(a, cl)
                for kt in range(NT):
                    ca.tick(a.activation(out=shar[:, kt * H:(kt + 1) * H],
                                         in_=accg[:, kt * H:(kt + 1) * H],
                                         func=AF.Square, accum_out=gc(kt, 0)))

            @blk.vector
            def _(v):
                join(v, ca, cl)

                def vt(ins):
                    cv.tick(ins)
                    v.wait_ge(cv.sem, cv.v)

                for kt in range(NT):
                    sl = slice(kt * H, (kt + 1) * H)
                    vt(v.reduce_sum(gc(kt, 1), accg[:, sl], axis=AX.X))
                    vt(v.tensor_scalar(out=gc(kt, 1), in0=gc(kt, 1), scalar1=1.0 / H,
                                       scalar2=None, op0=ALU.mult))
                    vt(v.tensor_scalar(out=gc(kt, 0), in0=gc(kt, 0), scalar1=1.0 / H,
                                       scalar2=None, op0=ALU.mult))
                    vt(v.tensor_mul(gc(kt, 2), gc(kt, 1), gc(kt, 1)))
                    vt(v.tensor_sub(gc(kt, 0), gc(kt, 0), gc(kt, 2)))
                    vt(v.tensor_scalar(out=gc(kt, 2), in0=gc(kt, 0), scalar1=1e-5,
                                       scalar2=None, op0=ALU.add))

            @blk.scalar
            def _(a):
                join(a, cv)
                for kt in range(NT):
                    ca.tick(a.activation(out=gc(kt, 3), in_=gc(kt, 2), func=AF.Sqrt))

            @blk.vector
            def _(v):
                join(v, ca)

                def vt(ins):
                    cv.tick(ins)
                    v.wait_ge(cv.sem, cv.v)

                for kt in range(NT):
                    sl = slice(kt * H, (kt + 1) * H)
                    vt(v.reciprocal(gc(kt, 3), gc(kt, 3)))
                    vt(v.tensor_scalar(out=gc(kt, 4), in0=gc(kt, 1), scalar1=-1.0,
                                       scalar2=None, op0=ALU.mult))
                    vt(v.tensor_scalar(out=xpg[:, sl], in0=accg[:, sl],
                                       scalar1=gc(kt, 4), scalar2=gc(kt, 3),
                                       op0=ALU.add, op1=ALU.mult))
                    vt(v.reciprocal(gc(kt, 5), gc(kt, 2)))
                    vt(v.tensor_mul(gc(kt, 5), gc(kt, 5), gc(kt, 0)))
                    vt(v.tensor_scalar(out=gc(kt, 5), in0=gc(kt, 5), scalar1=1e-8,
                                       scalar2=None, op0=ALU.add))

            @blk.scalar
            def _(a):
                join(a, cv)
                for kt in range(NT):
                    ca.tick(a.activation(out=gc(kt, 6), in_=gc(kt, 5), func=AF.Sqrt))
                for kt in range(NT):
                    cs.tick(a.dma_start(out=s_xp[r0 + kt * 128:r0 + (kt + 1) * 128, :],
                                        in_=xpg[:, kt * H:(kt + 1) * H]))

            @blk.vector
            def _(v):
                join(v, ca)
                for kt in range(NT):
                    v.wait_ge(cv.sem, cv.v)
                    cv.tick(v.reciprocal(gc(kt, 5), gc(kt, 6)))

            @blk.gpsimd
            def _(gp):
                join(gp, cv)
                for kt in range(NT):
                    gk = g * NT + kt
                    cp.tick(gp.tensor_copy(out=u32c[:, 0:1], in_=rt_mine[:, gk, 4:5]))
                    cp.tick(gp.tensor_copy(out=u32c[:, 1:2], in_=rt_mine[:, gk, 5:6]))
                    gp.wait_ge(cp.sem, cp.v)
                    cg.tick(gp.indirect_dma_start(
                        out=s_xg[:],
                        out_offset=IndirectOffsetOnAxis(ap=u32c[:, 0:1], axis=0),
                        in_=xpg[:, kt * H:(kt + 1) * H], in_offset=None,
                        bounds_check=E * CAP - 1, oob_is_err=False))
                    cg.tick(gp.indirect_dma_start(
                        out=s_xg[:],
                        out_offset=IndirectOffsetOnAxis(ap=u32c[:, 1:2], axis=0),
                        in_=xpg[:, kt * H:(kt + 1) * H], in_offset=None,
                        bounds_check=E * CAP - 1, oob_is_err=False))

        with nc.Block() as blk:
            @blk.vector
            def _(v):
                join(v)

                def vt(ins):
                    cv.tick(ins)
                    v.wait_ge(cv.sem, cv.v)

                for kt in range(NT):
                    sl = slice(kt * H, (kt + 1) * H)
                    vt(v.reduce_max(gc(kt, 7), xpg[:, sl], axis=AX.X,
                                    apply_absolute_value=True))
                    vt(v.reciprocal(gc(kt, 8), gc(kt, 7)))
                    vt(v.tensor_scalar(out=gc(kt, 8), in0=gc(kt, 8), scalar1=224.0,
                                       scalar2=None, op0=ALU.mult))
                    vt(v.tensor_scalar(out=gc(kt, 9), in0=gc(kt, 7), scalar1=1.0 / 224.0,
                                       scalar2=None, op0=ALU.mult))
                    vt(v.tensor_scalar(out=q8g[:, sl], in0=xpg[:, sl],
                                       scalar1=gc(kt, 8), scalar2=None, op0=ALU.mult))
                for kt in range(NT):
                    sl = slice(kt * H, (kt + 1) * H)
                    vt(v.tensor_copy(out=shar[:, sl], in_=q8g[:, sl]))

        transpose_set(lambda i: shar[:, i * 128:(i + 1) * 128],
                      lambda i: qxT[:, i * 128:(i + 1) * 128], NT * HC, 128)

        for ic in range(IF // 512):
            for (s_w, s_ws, dst) in ((s_qwgT, s_wgs, gch), (s_qwuT, s_wus, uch)):
                with nc.Block() as blk:
                    @blk.sync
                    def _(sync):
                        join(sync)
                        for c in range(HC):
                            cl.tick(sync.dma_start(out=wslab[:, c * 512:(c + 1) * 512],
                                                   in_=s_w[c, ic, :, :]))
                        cl.tick(sync.dma_start(out=stg_row[0:1, 0:512],
                                               in_=s_ws[0:1, ic * 512:(ic + 1) * 512]))

                    @blk.tensor
                    def _(t):
                        join(t, cl, cv, ca)
                        ct.tick(t.matmul(PS[2][:, 0:512], ones_r[:], stg_row[0:1, 0:512],
                                         start=True, stop=True))
                        for kt in range(NT):
                            ins = None
                            for c in range(HC):
                                ins = t.matmul(PS[kt][0:128, 0:512],
                                               qxT[:, (kt * HC + c) * 128:
                                                   (kt * HC + c) * 128 + 128],
                                               wslab[:, c * 512:(c + 1) * 512],
                                               start=(c == 0), stop=(c == HC - 1))
                            ct.tick(ins)

                    @blk.vector
                    def _(v):
                        join(v, ct)
                        v.wait_ge(cv.sem, cv.v)
                        cv.tick(v.tensor_copy(out=wbc[:, 0:512], in_=PS[2][:, 0:512]))
                        for kt in range(NT):
                            v.wait_ge(cv.sem, cv.v)
                            cv.tick(v.scalar_tensor_tensor(
                                out=dst[:, kt * 512:(kt + 1) * 512],
                                in0=PS[kt][0:128, 0:512], scalar=gc(kt, 9),
                                in1=wbc[:, 0:512], op0=ALU.mult, op1=ALU.mult))

            with nc.Block() as blk:
                @blk.scalar
                def _(a):
                    join(a, cv)
                    ca.tick(a.activation(out=sich[:], in_=gch[:], func=AF.Silu))

                @blk.vector
                def _(v):
                    join(v, ca)
                    for kt in range(NT):
                        for sub in range(4):
                            icb = ic * 4 + sub
                            off = icb * NT * 128 + kt * 128
                            v.wait_ge(cv.sem, cv.v)
                            cv.tick(v.tensor_mul(
                                arena[:, off:off + 128],
                                sich[:, kt * 512 + sub * 128:kt * 512 + (sub + 1) * 128],
                                uch[:, kt * 512 + sub * 128:kt * 512 + (sub + 1) * 128]))

        with nc.Block() as blk:
            @blk.vector
            def _(v):
                join(v)

                def vt(ins):
                    cv.tick(ins)
                    v.wait_ge(cv.sem, cv.v)

                for kt in range(NT):
                    tv = arena[:].rearrange("p (i k c) -> p i k c", i=IFC, k=NT)
                    vt(v.reduce_max(gch[:, kt * IFC:(kt + 1) * IFC].unsqueeze(2),
                                    tv[:, :, kt, :], axis=AX.X,
                                    apply_absolute_value=True))
                    vt(v.reduce_max(gc(kt, 10), gch[:, kt * IFC:(kt + 1) * IFC],
                                    axis=AX.X))
                    vt(v.reciprocal(gc(kt, 11), gc(kt, 10)))
                    vt(v.tensor_scalar(out=gc(kt, 11), in0=gc(kt, 11), scalar1=224.0,
                                       scalar2=None, op0=ALU.mult))
                    vt(v.tensor_scalar(out=gc(kt, 12), in0=gc(kt, 10), scalar1=1.0 / 224.0,
                                       scalar2=None, op0=ALU.mult))
                for kt in range(NT):
                    for icb in range(IFC):
                        off = icb * NT * 128 + kt * 128
                        cv.tick(v.tensor_scalar(out=q8g[:, off:off + 128],
                                                in0=arena[:, off:off + 128],
                                                scalar1=gc(kt, 11), scalar2=None,
                                                op0=ALU.mult))
                v.wait_ge(cv.sem, cv.v)
                cv.tick(v.tensor_copy(out=arena[:], in_=q8g[:]))

        def _qt_src(i):
            kt, icb = i // IFC, i % IFC
            off = icb * NT * 128 + kt * 128
            return arena[:, off:off + 128]

        def _qt_dst(i):
            kt, icb = i // IFC, i % IFC
            off = (kt * IFC + icb) * 128
            return qtTs[:, off:off + 128]

        transpose_set(_qt_src, _qt_dst, NT * IFC, 128)

        for hc4 in range(HC // 4):
            with nc.Block() as blk:
                @blk.sync
                def _(sync):
                    join(sync)
                    for c in range(IFC):
                        cl.tick(sync.dma_start(out=wdslab[:, c * 512:(c + 1) * 512],
                                               in_=s_qwdT[c, hc4, :, :]))
                    cl.tick(sync.dma_start(out=stg_row[0:1, 0:512],
                                           in_=s_wds[0:1, hc4 * 512:(hc4 + 1) * 512]))

                @blk.tensor
                def _(t):
                    join(t, cl, cv, ca)
                    ct.tick(t.matmul(PS[2][:, 0:512], ones_r[:], stg_row[0:1, 0:512],
                                     start=True, stop=True))
                    for kt in range(NT):
                        ins = None
                        for c in range(IFC):
                            ins = t.matmul(PS[kt][0:128, 0:512],
                                           qtTs[:, (kt * IFC + c) * 128:
                                                (kt * IFC + c) * 128 + 128],
                                           wdslab[:, c * 512:(c + 1) * 512],
                                           start=(c == 0), stop=(c == IFC - 1))
                        ct.tick(ins)

                @blk.vector
                def _(v):
                    join(v, ct)
                    v.wait_ge(cv.sem, cv.v)
                    cv.tick(v.tensor_copy(out=wbc[:, 0:512], in_=PS[2][:, 0:512]))
                    for kt in range(NT):
                        v.wait_ge(cv.sem, cv.v)
                        cv.tick(v.scalar_tensor_tensor(
                            out=gch[:, kt * 512:(kt + 1) * 512],
                            in0=PS[kt][0:128, 0:512], scalar=gc(kt, 12),
                            in1=wbc[:, 0:512], op0=ALU.mult, op1=ALU.mult))
                        v.wait_ge(cv.sem, cv.v)
                        sl = slice(kt * H + hc4 * 512, kt * H + (hc4 + 1) * 512)
                        cv.tick(v.tensor_add(accg[:, sl], accg[:, sl],
                                             gch[:, kt * 512:(kt + 1) * 512]))

        with nc.Block() as blk:
            @blk.vector
            def _(v):
                join(v)

                def vt(ins):
                    cv.tick(ins)
                    v.wait_ge(cv.sem, cv.v)

                for kt in range(NT):
                    sl = slice(kt * H, (kt + 1) * H)
                    vt(v.scalar_tensor_tensor(out=shar[:, sl], in0=xpg[:, sl],
                                              scalar=gc(kt, 5), in1=b_sNg[:],
                                              op0=ALU.mult, op1=ALU.mult))
                    vt(v.reduce_max(gc(kt, 13), shar[:, sl], axis=AX.X,
                                    apply_absolute_value=True))
                    vt(v.tensor_scalar(out=gc(kt, 13), in0=gc(kt, 13), scalar1=1e-5,
                                       scalar2=None, op0=ALU.max))
                    vt(v.reciprocal(gc(kt, 14), gc(kt, 13)))
                    vt(v.tensor_scalar(out=gc(kt, 14), in0=gc(kt, 14), scalar1=127.0,
                                       scalar2=None, op0=ALU.mult))
                    vt(v.tensor_scalar(out=gc(kt, 13), in0=gc(kt, 13), scalar1=1.0 / 127.0,
                                       scalar2=None, op0=ALU.mult))

            @blk.scalar
            def _(a):
                join(a, cv)
                for kt in range(NT):
                    sl = slice(kt * H, (kt + 1) * H)
                    ca.tick(a.activation(out=i32g[:, sl], in_=shar[:, sl],
                                         func=AF.Identity, scale=gc(kt, 14)))

            @blk.vector
            def _(v):
                join(v, ca)
                cv.tick(v.tensor_copy(out=shar[:, 0:NT * H], in_=i32g[:, 0:NT * H]))

        transpose_set(lambda i: shar[:, i * 128:(i + 1) * 128],
                      lambda i: k1T[:, i * 128:(i + 1) * 128], NT * HC, 128)

        for oc in range(4):
            with nc.Block() as blk:
                @blk.sync
                def _(sync):
                    join(sync)
                    for c in range(HC):
                        cl.tick(sync.dma_start(
                            out=tslab[:, c * 512:(c + 1) * 512],
                            in_=s_qsWg[c * 128:(c + 1) * 128, oc * 512:(oc + 1) * 512]))

                @blk.tensor
                def _(t):
                    join(t, cl, ca)
                    for kt in range(NT):
                        ins = None
                        for c in range(HC):
                            ins = t.matmul(PS[kt][0:128, 0:512],
                                           k1T[:, (kt * HC + c) * 128:
                                               (kt * HC + c) * 128 + 128],
                                           tslab[:, c * 512:(c + 1) * 512],
                                           start=(c == 0), stop=(c == HC - 1))
                        ct.tick(ins)

                @blk.vector
                def _(v):
                    join(v, ct)
                    sw = mean_slot["sWg"]
                    for kt in range(NT):
                        v.wait_ge(cv.sem, cv.v)
                        cv.tick(v.tensor_mul(gc(kt, 15), gc(kt, 13),
                                             b_wm[:, sw + 1:sw + 2]))
                        v.wait_ge(cv.sem, cv.v)
                        cv.tick(v.tensor_scalar(
                            out=shy[:, kt * H + oc * 512:kt * H + (oc + 1) * 512],
                            in0=PS[kt][0:128, 0:512], scalar1=gc(kt, 15),
                            scalar2=None, op0=ALU.mult))

        with nc.Block() as blk:
            @blk.scalar
            def _(a):
                join(a, cv)
                for kt in range(NT):
                    ca.tick(a.activation(out=shar[:, kt * H:kt * H + IA],
                                         in_=shy[:, kt * H:kt * H + IA], func=AF.Silu))

            @blk.vector
            def _(v):
                join(v, ca)
                for kt in range(NT):
                    v.wait_ge(cv.sem, cv.v)
                    cv.tick(v.tensor_mul(shh[:, kt * IA:(kt + 1) * IA],
                                         shar[:, kt * H:kt * H + IA],
                                         shy[:, kt * H + IA:kt * H + 2 * IA]))

            @blk.scalar
            def _(a):
                join(a, cv)
                for kt in range(NT):
                    ca.tick(a.activation(out=shar[:, kt * H:kt * H + IA],
                                         in_=shh[:, kt * IA:(kt + 1) * IA],
                                         func=AF.Square, accum_out=gc(kt, 0)))

            @blk.vector
            def _(v):
                join(v, ca)
                for kt in range(NT):
                    v.wait_ge(cv.sem, cv.v)
                    cv.tick(v.tensor_scalar(out=gc(kt, 0), in0=gc(kt, 0), scalar1=1.0 / IA,
                                            scalar2=1e-8, op0=ALU.mult, op1=ALU.add))

            @blk.scalar
            def _(a):
                join(a, cv)
                for kt in range(NT):
                    ca.tick(a.activation(out=gc(kt, 1), in_=gc(kt, 0), func=AF.Sqrt))

            @blk.vector
            def _(v):
                join(v, ca)

                def vt(ins):
                    cv.tick(ins)
                    v.wait_ge(cv.sem, cv.v)

                for kt in range(NT):
                    vt(v.reciprocal(gc(kt, 1), gc(kt, 1)))
                    vt(v.scalar_tensor_tensor(out=shar[:, kt * H:kt * H + IA],
                                              in0=shh[:, kt * IA:(kt + 1) * IA],
                                              scalar=gc(kt, 1), in1=b_sNd[:],
                                              op0=ALU.mult, op1=ALU.mult))
                    vt(v.reduce_max(gc(kt, 2), shar[:, kt * H:kt * H + IA], axis=AX.X,
                                    apply_absolute_value=True))
                    vt(v.tensor_scalar(out=gc(kt, 2), in0=gc(kt, 2), scalar1=1e-5,
                                       scalar2=None, op0=ALU.max))
                    vt(v.reciprocal(gc(kt, 3), gc(kt, 2)))
                    vt(v.tensor_scalar(out=gc(kt, 3), in0=gc(kt, 3), scalar1=127.0,
                                       scalar2=None, op0=ALU.mult))
                    vt(v.tensor_scalar(out=gc(kt, 2), in0=gc(kt, 2), scalar1=1.0 / 127.0,
                                       scalar2=None, op0=ALU.mult))

            @blk.scalar
            def _(a):
                join(a, cv)
                for kt in range(NT):
                    ca.tick(a.activation(out=i32g[:, kt * IA:(kt + 1) * IA],
                                         in_=shar[:, kt * H:kt * H + IA],
                                         func=AF.Identity, scale=gc(kt, 3)))

            @blk.vector
            def _(v):
                join(v, ca)
                cv.tick(v.tensor_copy(out=shar[:, 0:NT * IA], in_=i32g[:, 0:NT * IA]))

        transpose_set(lambda i: shar[:, i * 128:(i + 1) * 128],
                      lambda i: k2T[:, i * 128:(i + 1) * 128], NT * IAC, 128)

        with nc.Block() as blk:
            @blk.vector
            def _(v):
                join(v)
                for kt in range(NT):
                    sl = slice(kt * H, (kt + 1) * H)
                    v.wait_ge(cv.sem, cv.v)
                    cv.tick(v.tensor_mul(shy[:, sl], xpg[:, sl], b_Wsg[:]))
                    v.wait_ge(cv.sem, cv.v)
                    cv.tick(v.reduce_sum(gc(kt, 4), shy[:, sl], axis=AX.X))

            @blk.scalar
            def _(a):
                join(a, cv)
                for kt in range(NT):
                    ca.tick(a.activation(out=gc(kt, 4), in_=gc(kt, 4), func=AF.Sigmoid))

        for oc in range(4):
            with nc.Block() as blk:
                @blk.sync
                def _(sync):
                    join(sync)
                    for c in range(IAC):
                        cl.tick(sync.dma_start(
                            out=tslab[:, c * 512:(c + 1) * 512],
                            in_=s_qsWd[c * 128:(c + 1) * 128, oc * 512:(oc + 1) * 512]))

                @blk.tensor
                def _(t):
                    join(t, cl, ca)
                    for kt in range(NT):
                        ins = None
                        for c in range(IAC):
                            ins = t.matmul(PS[kt][0:128, 0:512],
                                           k2T[:, (kt * IAC + c) * 128:
                                               (kt * IAC + c) * 128 + 128],
                                           tslab[:, c * 512:(c + 1) * 512],
                                           start=(c == 0), stop=(c == IAC - 1))
                        ct.tick(ins)

                @blk.vector
                def _(v):
                    join(v, ct)
                    sw = mean_slot["sWd"]
                    for kt in range(NT):
                        v.wait_ge(cv.sem, cv.v)
                        cv.tick(v.tensor_mul(gc(kt, 15), gc(kt, 2), b_wm[:, sw + 1:sw + 2]))
                        v.wait_ge(cv.sem, cv.v)
                        cv.tick(v.tensor_scalar(out=gch[:, kt * 512:(kt + 1) * 512],
                                                in0=PS[kt][0:128, 0:512],
                                                scalar1=gc(kt, 15), scalar2=None,
                                                op0=ALU.mult))
                        v.wait_ge(cv.sem, cv.v)
                        cv.tick(v.tensor_mul(gch[:, kt * 512:(kt + 1) * 512],
                                             gch[:, kt * 512:(kt + 1) * 512],
                                             gc(kt, 4).to_broadcast([128, 512])))
                        v.wait_ge(cv.sem, cv.v)
                        sl = slice(kt * H + oc * 512, kt * H + (oc + 1) * 512)
                        cv.tick(v.tensor_add(accg[:, sl], accg[:, sl],
                                             gch[:, kt * 512:(kt + 1) * 512]))

        with nc.Block() as blk:
            @blk.sync
            def _(sync):
                join(sync)
                cl.tick(sync.dma_start(out=s_acc[r0:r0 + GT, :]
                                       .rearrange("(k p) h -> p k h", p=128),
                                       in_=accg[:].rearrange("p (k h) -> p k h", k=NT)))

    if "xp" in DBG:
        with nc.Block() as blk:
            @blk.sync
            def _(sync):
                join(sync)
                cl.tick(sync.dma_start(out=dbg["dbg_xp"][:], in_=s_xp[:]))

    p4.close()

    if KPH < 4:
        root.close()
        return nc
    p5 = ExitStack()
    sb5 = lambda n, s, d: p5.enter_context(nc.sbuf_tensor(n, s, d))
    qWgE = sb5("qWgE", [128, HC * 2048], FP8)
    qWdE = sb5("qWdE", [128, IAC * 2048], FP8)
    ewch = sb5("ewch", [128, 2048], F32)
    ei32 = sb5("ei32", [128, 2 * 2048], I32)
    b_NgE = sb5("b_NgE", [128, H], F32)
    b_NdE = sb5("b_NdE", [128, IA], F32)
    xg = sb5("xg", [128, 2 * H], F32)
    exn = sb5("exn", [128, 2 * H], F32)
    ekT = sb5("ekT", [128, 2 * H], BF16)
    ek2T = sb5("ek2T", [128, 2 * IA], BF16)
    eyb = sb5("eyb", [128, 2 * 2 * IA], F32)
    ehh = sb5("ehh", [128, 2 * IA], F32)
    ecol = sb5("ecol", [128, 16], F32)

    NTILE = CAP // 128
    VP = 2 + 7 + 1 + 5 + 1 + 1 + 7 + 1 + 5
    APc = 2 + 1 + 1 + 16 + 1 + 1 + 1 + 1 + 8
    TP = 16 + 4 + 8 + 4
    LP = 32

    for e in range(E):
        bcast_from_dram(b_NgE, d_Ng[e:e + 1, :], H)
        bcast_from_dram(b_NdE, d_Nd[e:e + 1, :], IA)

        for (dsrc, nch, qdst, key) in ((d_Wg, HC, qWgE, f"Wg{e}"),
                                       (d_Wd, IAC, qWdE, f"Wd{e}")):
            sidx = mean_slot[key]
            with nc.Block() as blk:
                base_l = cl.v
                base_a = ca.v
                base_v = cv.v

                @blk.sync
                def _(sync):
                    join(sync)
                    for c in range(nch):
                        if c >= 1:
                            sync.wait_ge(ca.sem, base_a + c)
                        cl.tick(sync.dma_start(out=ewch[:],
                                               in_=dsrc[e, c * 128:(c + 1) * 128, :]))

                @blk.scalar
                def _(a):
                    for c in range(nch):
                        a.wait_ge(cl.sem, base_l + (c + 1) * 16)
                        if c >= 1:
                            a.wait_ge(cv.sem, base_v + c)
                        ca.tick(a.activation(out=ei32[:, 0:2048], in_=ewch[:],
                                             func=AF.Identity,
                                             scale=b_wm[:, sidx:sidx + 1]))

                @blk.vector
                def _(v):
                    for c in range(nch):
                        v.wait_ge(ca.sem, base_a + c + 1)
                        cv.tick(v.tensor_scalar(out=qdst[:, c * 2048:(c + 1) * 2048],
                                                in0=ei32[:, 0:2048], scalar1=-1.0,
                                                scalar2=1.0, op0=ALU.max, op1=ALU.min))

        sWg_i = mean_slot[f"Wg{e}"]
        sWd_i = mean_slot[f"Wd{e}"]

        with nc.Block() as blk:
            base_l = cl.v
            base_v = cv.v
            base_a = ca.v
            base_t = ct.v

            def P(i, k):
                return slice((i % 2) * k, (i % 2) * k + k)

            def ec(i, j):
                return ecol[:, (i % 2) * 8 + j:(i % 2) * 8 + j + 1]

            @blk.sync
            def _(sync):
                join(sync)
                for i in range(NTILE):
                    if i >= 2:
                        sync.wait_ge(cv.sem, base_v + (i - 2) * VP + 4)
                    cl.tick(sync.dma_start(out=xg[:, P(i, H)],
                                           in_=s_xg[e * CAP + i * 128:
                                                    e * CAP + (i + 1) * 128, :]))
                    if i >= 1:
                        sync.wait_ge(cv.sem, base_v + i * VP)
                        cl.tick(sync.dma_start(
                            out=s_moe[e * CAP + (i - 1) * 128:e * CAP + i * 128, :],
                            in_=exn[:, P(i - 1, H)]))
                sync.wait_ge(cv.sem, base_v + NTILE * VP)
                cl.tick(sync.dma_start(
                    out=s_moe[e * CAP + (NTILE - 1) * 128:e * CAP + NTILE * 128, :],
                    in_=exn[:, P(NTILE - 1, H)]))

            @blk.scalar
            def _(a):
                for i in range(NTILE):
                    a.wait_ge(cl.sem, base_l + i * LP + 16)
                    if i >= 2:
                        a.wait_ge(cv.sem, base_v + (i - 1) * VP)
                    ca.tick(a.activation(out=ehh[:, P(i, IA)].bitcast(F32),
                                         in_=xg[:, (i % 2) * H:(i % 2) * H + IA],
                                         func=AF.Square, accum_out=ec(i, 0)))
                    ca.tick(a.activation(out=ehh[:, P(i, IA)].bitcast(F32),
                                         in_=xg[:, (i % 2) * H + IA:(i % 2) * H + 2 * IA],
                                         func=AF.Square, accum_out=ec(i, 1)))
                    a.wait_ge(cv.sem, base_v + i * VP + 2)
                    ca.tick(a.activation(out=ec(i, 2), in_=ec(i, 2), func=AF.Sqrt))
                    a.wait_ge(cv.sem, base_v + i * VP + 9)
                    ca.tick(a.activation(out=ei32[:, P(i, 2048)], in_=exn[:, P(i, H)],
                                         func=AF.Identity, scale=ec(i, 5)))
                    for c in range(HC):
                        a.wait_ge(ct.sem, base_t + i * TP + c + 1)
                        ca.tick(a.activation(out=ekT[:, (i % 2) * H + c * 128:
                                                      (i % 2) * H + (c + 1) * 128],
                                             in_=PS[4 + c % 2][0:128, 0:128],
                                             func=AF.Identity))
                    a.wait_ge(cv.sem, base_v + i * VP + 15)
                    ca.tick(a.activation(out=exn[:, (i % 2) * H:(i % 2) * H + IA],
                                         in_=eyb[:, (i % 2) * 2048:(i % 2) * 2048 + IA],
                                         func=AF.Silu))
                    a.wait_ge(cv.sem, base_v + i * VP + 16)
                    ca.tick(a.activation(out=exn[:, (i % 2) * H + IA:(i % 2) * H + 2 * IA]
                                         .bitcast(F32),
                                         in_=ehh[:, P(i, IA)], func=AF.Square,
                                         accum_out=ec(i, 0)))
                    a.wait_ge(cv.sem, base_v + i * VP + 17)
                    ca.tick(a.activation(out=ec(i, 2), in_=ec(i, 2), func=AF.Sqrt))
                    a.wait_ge(cv.sem, base_v + i * VP + 24)
                    ca.tick(a.activation(out=ei32[:, (i % 2) * 2048:(i % 2) * 2048 + IA],
                                         in_=exn[:, (i % 2) * H:(i % 2) * H + IA],
                                         func=AF.Identity, scale=ec(i, 5)))
                    for c in range(IAC):
                        a.wait_ge(ct.sem, base_t + i * TP + 16 + 4 + c + 1)
                        ca.tick(a.activation(out=ek2T[:, (i % 2) * IA + c * 128:
                                                       (i % 2) * IA + (c + 1) * 128],
                                             in_=PS[4 + c % 2][0:128, 0:128],
                                             func=AF.Identity))

            @blk.vector
            def _(v):
                for i in range(NTILE):
                    v.wait_ge(ca.sem, base_a + i * APc + 2)
                    cv.tick(v.scalar_tensor_tensor(out=ec(i, 2), in0=ec(i, 0),
                                                   scalar=1.0, in1=ec(i, 1),
                                                   op0=ALU.mult, op1=ALU.add))
                    v.wait_ge(cv.sem, cv.v)
                    cv.tick(v.tensor_scalar(out=ec(i, 2), in0=ec(i, 2),
                                            scalar1=1.0 / H, scalar2=1e-8,
                                            op0=ALU.mult, op1=ALU.add))
                    v.wait_ge(ca.sem, base_a + i * APc + 3)
                    cv.tick(v.reciprocal(ec(i, 3), ec(i, 2)))
                    v.wait_ge(cv.sem, cv.v)
                    cv.tick(v.scalar_tensor_tensor(out=exn[:, P(i, H)],
                                                   in0=xg[:, P(i, H)],
                                                   scalar=ec(i, 3), in1=b_NgE[:],
                                                   op0=ALU.mult, op1=ALU.mult))
                    v.wait_ge(cv.sem, cv.v)
                    cv.tick(v.reduce_max(ec(i, 4), exn[:, P(i, H)], axis=AX.X,
                                         apply_absolute_value=True))
                    v.wait_ge(cv.sem, cv.v)
                    cv.tick(v.tensor_scalar(out=ec(i, 4), in0=ec(i, 4), scalar1=1e-5,
                                            scalar2=None, op0=ALU.max))
                    v.wait_ge(cv.sem, cv.v)
                    cv.tick(v.reciprocal(ec(i, 5), ec(i, 4)))
                    v.wait_ge(cv.sem, cv.v)
                    cv.tick(v.tensor_scalar(out=ec(i, 5), in0=ec(i, 5), scalar1=127.0,
                                            scalar2=None, op0=ALU.mult))
                    v.wait_ge(cv.sem, cv.v)
                    cv.tick(v.tensor_scalar(out=ec(i, 4), in0=ec(i, 4),
                                            scalar1=1.0 / 127.0, scalar2=None,
                                            op0=ALU.mult))
                    v.wait_ge(ca.sem, base_a + i * APc + 4)
                    cv.tick(v.tensor_copy(out=exn[:, P(i, H)], in_=ei32[:, P(i, 2048)]))
                    v.wait_ge(cv.sem, cv.v)
                    cv.tick(v.tensor_mul(ec(i, 6), ec(i, 4), b_wm[:, sWg_i + 1:sWg_i + 2]))
                    for oc in range(4):
                        v.wait_ge(ct.sem, base_t + i * TP + 16 + oc + 1)
                        cv.tick(v.tensor_scalar(
                            out=eyb[:, (i % 2) * 2048 + oc * 512:
                                    (i % 2) * 2048 + (oc + 1) * 512],
                            in0=PS[oc][0:128, 0:512], scalar1=ec(i, 6),
                            scalar2=None, op0=ALU.mult))
                    v.wait_ge(ca.sem, base_a + i * APc + 21)
                    cv.tick(v.tensor_mul(ehh[:, P(i, IA)],
                                         exn[:, (i % 2) * H:(i % 2) * H + IA],
                                         eyb[:, (i % 2) * 2048 + IA:
                                             (i % 2) * 2048 + 2 * IA]))
                    v.wait_ge(ca.sem, base_a + i * APc + 22)
                    cv.tick(v.tensor_scalar(out=ec(i, 2), in0=ec(i, 0),
                                            scalar1=1.0 / IA, scalar2=1e-8,
                                            op0=ALU.mult, op1=ALU.add))
                    v.wait_ge(ca.sem, base_a + i * APc + 23)
                    cv.tick(v.reciprocal(ec(i, 3), ec(i, 2)))
                    v.wait_ge(cv.sem, cv.v)
                    cv.tick(v.scalar_tensor_tensor(out=exn[:, (i % 2) * H:
                                                           (i % 2) * H + IA],
                                                   in0=ehh[:, P(i, IA)],
                                                   scalar=ec(i, 3), in1=b_NdE[:],
                                                   op0=ALU.mult, op1=ALU.mult))
                    v.wait_ge(cv.sem, cv.v)
                    cv.tick(v.reduce_max(ec(i, 4), exn[:, (i % 2) * H:(i % 2) * H + IA],
                                         axis=AX.X, apply_absolute_value=True))
                    v.wait_ge(cv.sem, cv.v)
                    cv.tick(v.tensor_scalar(out=ec(i, 4), in0=ec(i, 4), scalar1=1e-5,
                                            scalar2=None, op0=ALU.max))
                    v.wait_ge(cv.sem, cv.v)
                    cv.tick(v.reciprocal(ec(i, 5), ec(i, 4)))
                    v.wait_ge(cv.sem, cv.v)
                    cv.tick(v.tensor_scalar(out=ec(i, 5), in0=ec(i, 5), scalar1=127.0,
                                            scalar2=None, op0=ALU.mult))
                    v.wait_ge(cv.sem, cv.v)
                    cv.tick(v.tensor_scalar(out=ec(i, 4), in0=ec(i, 4),
                                            scalar1=1.0 / 127.0, scalar2=None,
                                            op0=ALU.mult))
                    v.wait_ge(ca.sem, base_a + i * APc + 24)
                    cv.tick(v.tensor_copy(out=exn[:, (i % 2) * H:(i % 2) * H + IA],
                                          in_=ei32[:, (i % 2) * 2048:
                                                   (i % 2) * 2048 + IA]))
                    v.wait_ge(cv.sem, cv.v)
                    cv.tick(v.tensor_mul(ec(i, 6), ec(i, 4), b_wm[:, sWd_i + 1:sWd_i + 2]))
                    for oc in range(4):
                        v.wait_ge(ct.sem, base_t + i * TP + 16 + 4 + 8 + oc + 1)
                        cv.tick(v.tensor_scalar(out=exn[:, (i % 2) * H + oc * 512:
                                                        (i % 2) * H + (oc + 1) * 512],
                                                in0=PS[oc][0:128, 0:512],
                                                scalar1=ec(i, 6), scalar2=None,
                                                op0=ALU.mult))

            @blk.tensor
            def _(t):
                t.wait_ge(cv.sem, base_v)
                t.wait_ge(ca.sem, base_a)
                t.wait_ge(cl.sem, base_l)
                for i in range(NTILE):
                    for c in range(HC):
                        t.wait_ge(cv.sem, base_v + i * VP + 10)
                        if c >= 2:
                            t.wait_ge(ca.sem, base_a + i * APc + 4 + c - 1)
                        elif i >= 1:
                            t.wait_ge(ca.sem, base_a + i * APc)
                        ct.tick(t.transpose(PS[4 + c % 2][0:128, 0:128],
                                            exn[:, (i % 2) * H + c * 128:
                                                (i % 2) * H + (c + 1) * 128],
                                            identf[:]))
                    t.wait_ge(ca.sem, base_a + i * APc + 20)
                    if i >= 1:
                        t.wait_ge(cv.sem, base_v + i * VP)
                    for oc in range(4):
                        ins = None
                        for c in range(HC):
                            ins = t.matmul(PS[oc][0:128, 0:512],
                                           ekT[:, (i % 2) * H + c * 128:
                                               (i % 2) * H + (c + 1) * 128],
                                           qWgE[:, c * 2048 + oc * 512:
                                                c * 2048 + (oc + 1) * 512],
                                           start=(c == 0), stop=(c == HC - 1))
                        ct.tick(ins)
                    for c in range(IAC):
                        t.wait_ge(cv.sem, base_v + i * VP + 25)
                        if c >= 2:
                            t.wait_ge(ca.sem, base_a + i * APc + 24 + c - 1)
                        else:
                            t.wait_ge(ca.sem, base_a + i * APc + 20)
                        ct.tick(t.transpose(PS[4 + c % 2][0:128, 0:128],
                                            exn[:, (i % 2) * H + c * 128:
                                                (i % 2) * H + (c + 1) * 128],
                                            identf[:]))
                    t.wait_ge(ca.sem, base_a + i * APc + APc)
                    t.wait_ge(cv.sem, base_v + i * VP + 17)
                    for oc in range(4):
                        ins = None
                        for c in range(IAC):
                            ins = t.matmul(PS[oc][0:128, 0:512],
                                           ek2T[:, (i % 2) * IA + c * 128:
                                                (i % 2) * IA + (c + 1) * 128],
                                           qWdE[:, c * 2048 + oc * 512:
                                                c * 2048 + (oc + 1) * 512],
                                           start=(c == 0), stop=(c == IAC - 1))
                        ct.tick(ins)

    p5.close()

    if KPH < 5:
        root.close()
        return nc
    p6 = ExitStack()
    sb6 = lambda n, s, d: p6.enter_context(nc.sbuf_tensor(n, s, d))
    acc2 = sb6("acc2", [128, H], F32)
    B1 = sb6("B1", [128, H], F32)
    B2 = sb6("B2", [128, H], F32)
    fidx = sb6("fidx", [128, 2], U32)

    with nc.Block() as blk:
        @blk.vector
        def _(v):
            join(v)
            cv.tick(v.memset(B1[:], 0.0))
            cv.tick(v.memset(B2[:], 0.0))

    for k in range(8):
        with nc.Block() as blk:
            @blk.gpsimd
            def _(gp):
                join(gp)
                cp.tick(gp.tensor_copy(out=fidx[:, 0:1], in_=rt_mine[:, k, 4:5]))
                cp.tick(gp.tensor_copy(out=fidx[:, 1:2], in_=rt_mine[:, k, 5:6]))
                gp.wait_ge(cp.sem, cp.v)
                cg.tick(gp.indirect_dma_start(
                    out=B1[:], out_offset=None, in_=s_moe[:],
                    in_offset=IndirectOffsetOnAxis(ap=fidx[:, 0:1], axis=0),
                    bounds_check=T - 1, oob_is_err=False))
                cg.tick(gp.indirect_dma_start(
                    out=B2[:], out_offset=None, in_=s_moe[:],
                    in_offset=IndirectOffsetOnAxis(ap=fidx[:, 1:2], axis=0),
                    bounds_check=T - 1, oob_is_err=False))

            @blk.sync
            def _(sync):
                join(sync)
                cl.tick(sync.dma_start(out=acc2[:], in_=s_acc[k * 128:(k + 1) * 128, :]))

            @blk.vector
            def _(v):
                join(v, cg, cl)
                v.wait_ge(cv.sem, cv.v)
                cv.tick(v.scalar_tensor_tensor(out=acc2[:], in0=B1[:],
                                               scalar=rt_mine[:, k, 2:3], in1=acc2[:],
                                               op0=ALU.mult, op1=ALU.add))
                v.wait_ge(cv.sem, cv.v)
                cv.tick(v.scalar_tensor_tensor(out=acc2[:], in0=B2[:],
                                               scalar=rt_mine[:, k, 3:4], in1=acc2[:],
                                               op0=ALU.mult, op1=ALU.add))

            @blk.sync
            def _(sync):
                join(sync, cv)
                cl.tick(sync.dma_start(out=d_out[k * 128:(k + 1) * 128, :], in_=acc2[:]))

    if "acc" in DBG:
        with nc.Block() as blk:
            @blk.sync
            def _(sync):
                join(sync)
                cl.tick(sync.dma_start(out=dbg["dbg_acc"][:], in_=s_acc[:]))

    p6.close()

    with nc.Block() as blk:
        @blk.sync
        def _(sync):
            join(sync)

    root.close()
    return nc


_NC_CACHE = None
_LAST_RES = None


def _dn_idx(inputs):
    return np.asarray(inputs["L0_ffn_down_reorder_index"]).astype(np.int64)


def kernel(**inputs):
    global _NC_CACHE
    xf = np.ascontiguousarray(inputs["L0_attn_x"].reshape(T, H).astype(np.float32))
    xfT = np.ascontiguousarray(xf.T)

    if _NC_CACHE is None:
        _NC_CACHE = build_program()
    nc = _NC_CACHE

    common = {
        "xfT": xfT,
        "Wg": np.ascontiguousarray(inputs["L0_attn_Wg"], dtype=np.float32),
        "Wd": np.ascontiguousarray(inputs["L0_attn_Wd"], dtype=np.float32),
        "Ng": np.ascontiguousarray(inputs["L0_attn_Ng"], dtype=np.float32),
        "Nd": np.ascontiguousarray(inputs["L0_attn_Nd"], dtype=np.float32),
        "sWg": np.ascontiguousarray(inputs["L0_attn_sWg"], dtype=np.float32),
        "sWd": np.ascontiguousarray(inputs["L0_attn_sWd"], dtype=np.float32),
        "sNg": np.ascontiguousarray(inputs["L0_attn_sNg"], dtype=np.float32).reshape(1, H),
        "sNd": np.ascontiguousarray(inputs["L0_attn_sNd"], dtype=np.float32).reshape(1, IA),
        "gnw": np.ascontiguousarray(inputs["L0_attn_gate_norm_w"],
                                    dtype=np.float32).reshape(1, H),
        "Wgate": np.ascontiguousarray(inputs["L0_attn_W_gate"], dtype=np.float32),
        "Wsg": np.ascontiguousarray(inputs["L0_attn_W_shared_gate"],
                                    dtype=np.float32).reshape(1, H),
        "wgate_f": np.ascontiguousarray(inputs["L0_ffn_w_gate"][_dn_idx(inputs)], dtype=np.float32),
        "wup_f": np.ascontiguousarray(inputs["L0_ffn_w_up"][_dn_idx(inputs)], dtype=np.float32),
        "wdown_f": np.ascontiguousarray(inputs["L0_ffn_w_down"], dtype=np.float32),
    }
    in_maps = []
    for c in range(NCORE):
        m = dict(common)
        m["xsh"] = np.ascontiguousarray(xf[c * TPC:(c + 1) * TPC, :])
        tglob = c * TPC + np.arange(TPC)
        gi = ((tglob % 128) * 64 + (tglob // 128)).astype(np.int32)
        m["gidx"] = np.ascontiguousarray(gi.reshape(8, 128).T)
        in_maps.append(m)

    res = run_bass_kernel_spmd(nc, in_maps, core_ids=list(range(NCORE)))
    global _LAST_RES
    _LAST_RES = res
    out = np.concatenate([np.asarray(res.results[c]["out"]) for c in range(NCORE)], axis=0)
    return out.reshape(inputs["L0_attn_x"].shape).astype(np.float32)


# version 04
# speedup vs baseline: 1.2776x; 1.1622x over previous
import os
import numpy as np
from contextlib import ExitStack

import concourse.bass as bass
import concourse.mybir as mybir
from concourse.bass import IndirectOffsetOnAxis
from concourse.bass_utils import run_bass_kernel_spmd

F32 = mybir.dt.float32
BF16 = mybir.dt.bfloat16
FP8 = mybir.dt.float8e4
I32 = mybir.dt.int32
U32 = mybir.dt.uint32
AF = mybir.ActivationFunctionType
ALU = mybir.AluOpType
AX = mybir.AxisListType

NCORE = 8
T = 8192
TPC = 1024
H = 2048
E = 8
CAP = 1024
IA = 1024
IF = 5632
GT = 256
NG = TPC // GT
NT = GT // 128
HC = H // 128
IFC = IF // 128
IAC = IA // 128
DBG = os.environ.get("KDBG", "")
KPH = int(os.environ.get("KPH", "5"))
KP2 = int(os.environ.get("KP2", "9"))
KQT = int(os.environ.get("KQT", "9"))


class Ctr:
    def __init__(self, sem, step=1):
        self.sem, self.v, self.step = sem, 0, step

    def tick(self, instr, n=None):
        n = self.step if n is None else n
        instr.then_inc(self.sem, n)
        self.v += n
        return self.v


def build_program():
    nc = bass.Bass("TRN2", target_bir_lowering=False)
    root = ExitStack()

    cl = Ctr(root.enter_context(nc.semaphore("c_ld")), 16)
    cs = Ctr(root.enter_context(nc.semaphore("c_lds")), 16)
    cg = Ctr(root.enter_context(nc.semaphore("c_ldg")), 16)
    cv = Ctr(root.enter_context(nc.semaphore("c_v")), 1)
    ca = Ctr(root.enter_context(nc.semaphore("c_a")), 1)
    ct = Ctr(root.enter_context(nc.semaphore("c_t")), 1)
    cp = Ctr(root.enter_context(nc.semaphore("c_g")), 1)
    ALLC = [cl, cs, cg, cv, ca, ct, cp]

    def join(eng, *ctrs):
        for c in (ctrs or ALLC):
            if c.v:
                eng.wait_ge(c.sem, c.v)

    d_xsh = nc.dram_tensor("xsh", [TPC, H], F32, kind="ExternalInput")
    d_xfT = nc.dram_tensor("xfT", [H, T], F32, kind="ExternalInput")
    d_gidx = nc.dram_tensor("gidx", [128, 8], I32, kind="ExternalInput")
    d_Wg = nc.dram_tensor("Wg", [E, H, 2 * IA], F32, kind="ExternalInput")
    d_Wd = nc.dram_tensor("Wd", [E, IA, H], F32, kind="ExternalInput")
    d_Ng = nc.dram_tensor("Ng", [E, H], F32, kind="ExternalInput")
    d_Nd = nc.dram_tensor("Nd", [E, IA], F32, kind="ExternalInput")
    d_sWg = nc.dram_tensor("sWg", [H, 2 * IA], F32, kind="ExternalInput")
    d_sWd = nc.dram_tensor("sWd", [IA, H], F32, kind="ExternalInput")
    d_sNg = nc.dram_tensor("sNg", [1, H], F32, kind="ExternalInput")
    d_sNd = nc.dram_tensor("sNd", [1, IA], F32, kind="ExternalInput")
    d_gnw = nc.dram_tensor("gnw", [1, H], F32, kind="ExternalInput")
    d_Wgate = nc.dram_tensor("Wgate", [H, E], F32, kind="ExternalInput")
    d_Wsg = nc.dram_tensor("Wsg", [1, H], F32, kind="ExternalInput")
    d_wg = nc.dram_tensor("wgate_f", [IF, H], F32, kind="ExternalInput")
    d_wu = nc.dram_tensor("wup_f", [IF, H], F32, kind="ExternalInput")
    d_wd = nc.dram_tensor("wdown_f", [H, IF], F32, kind="ExternalInput")
    d_out = nc.dram_tensor("out", [TPC, H], F32, kind="ExternalOutput")

    s_xp = nc.dram_tensor("s_xp", [TPC, H], F32, kind="Internal")
    s_rtab = nc.dram_tensor("s_rtab", [T, 8], F32, kind="Internal")
    s_xg = nc.dram_tensor("s_xg", [E * CAP, H], F32, kind="Internal")
    s_moe = nc.dram_tensor("s_moe", [E * CAP, H], F32, kind="Internal")
    s_acc = nc.dram_tensor("s_acc", [TPC, H], F32, kind="Internal")
    s_qwgT = nc.dram_tensor("s_qwgT", [HC, IFC // 4, 128, 512], FP8, kind="Internal")
    s_qwuT = nc.dram_tensor("s_qwuT", [HC, IFC // 4, 128, 512], FP8, kind="Internal")
    s_qwdT = nc.dram_tensor("s_qwdT", [IFC, HC // 4, 128, 512], FP8, kind="Internal")
    s_qsWg = nc.dram_tensor("s_qsWg", [H, 2 * IA], FP8, kind="Internal")
    s_qsWd = nc.dram_tensor("s_qsWd", [IA, H], FP8, kind="Internal")
    s_wgs = nc.dram_tensor("s_wgs", [1, IF], F32, kind="Internal")
    s_wus = nc.dram_tensor("s_wus", [1, IF], F32, kind="Internal")
    s_wds = nc.dram_tensor("s_wds", [1, H], F32, kind="Internal")

    dbg = {}
    if "rt" in DBG:
        dbg["dbg_rt"] = nc.dram_tensor("dbg_rt", [TPC, 8], F32, kind="ExternalOutput")
    if "xp" in DBG:
        dbg["dbg_xp"] = nc.dram_tensor("dbg_xp", [TPC, H], F32, kind="ExternalOutput")
    if "acc" in DBG:
        dbg["dbg_acc"] = nc.dram_tensor("dbg_acc", [TPC, H], F32, kind="ExternalOutput")


    sbr = lambda n, s, d: root.enter_context(nc.sbuf_tensor(n, s, d))
    identf = sbr("identf", [128, 128], F32)
    ones_c = sbr("ones_c", [128, 1], F32)
    ones_cb = sbr("ones_cb", [128, 1], BF16)
    ones_r = sbr("ones_r", [1, 128], F32)
    iotaP = sbr("iotaP", [128, 1], I32)
    iotaPf = sbr("iotaPf", [128, 1], F32)
    iotaF = sbr("iotaF", [128, 128], I32)
    lidx = sbr("lidx", [128, 8], U32)
    stg_row = sbr("stg_row", [1, H], F32)
    b_wm = sbr("b_wm", [128, 64], F32)
    rt_mine = sbr("rt_mine", [128, 8, 8], F32)
    g_idx = sbr("g_idx", [128, 8], U32)
    u32c = sbr("u32c", [128, 2], U32)
    stgf = sbr("stgf", [128, 256], F32)
    colx = sbr("colx", [128, 16], F32)
    wrow = sbr("wrow", [1, 64], F32)
    wcl = sbr("wcl", [128, 4], F32)

    PS = [root.enter_context(nc.psum_tensor(f"psb{i}", [128, 512], F32)) for i in range(8)]

    with nc.Block() as blk:
        @blk.sync
        def _(sync):
            cl.tick(sync.dma_start(out=g_idx[:].bitcast(I32), in_=d_gidx[:]))

        @blk.gpsimd
        def _(gp):
            cp.tick(gp.iota(iotaP[:], pattern=[[0, 1]], base=0, channel_multiplier=1))
            cp.tick(gp.iota(iotaF[:], pattern=[[1, 128]], base=0, channel_multiplier=0))
            cp.tick(gp.iota(lidx[:].bitcast(I32), pattern=[[128, 8]], base=0,
                            channel_multiplier=1))

        @blk.vector
        def _(v):
            join(v, cp)
            cv.tick(v.tensor_copy(out=iotaPf[:], in_=iotaP[:]))
            v.wait_ge(cv.sem, cv.v)
            cv.tick(v.tensor_scalar(out=identf[:], in0=iotaF[:], scalar1=iotaPf[:, 0:1],
                                    scalar2=None, op0=ALU.is_equal))
            cv.tick(v.memset(ones_c[:], 1.0))
            cv.tick(v.memset(ones_cb[:], 1.0))
            cv.tick(v.memset(ones_r[:], 1.0))

    def bcast_from_dram(dst, dram_ap, n):
        with nc.Block() as blk:
            @blk.sync
            def _(sync):
                join(sync)
                cl.tick(sync.dma_start(out=stg_row[0:1, 0:n], in_=dram_ap))

            base_v = cv.v

            @blk.tensor
            def _(t):
                join(t, cl, cv)
                for i in range(0, n, 512):
                    w = min(512, n - i)
                    ch = i // 512
                    if ch >= 2:
                        t.wait_ge(cv.sem, base_v + ch - 1)
                    ct.tick(t.matmul(PS[ch % 2][:, 0:w], ones_r[:],
                                     stg_row[0:1, i:i + w], start=True, stop=True))

            base_t = ct.v

            @blk.vector
            def _(v):
                nch = (n + 511) // 512
                for i in range(0, n, 512):
                    w = min(512, n - i)
                    ch = i // 512
                    v.wait_ge(ct.sem, base_t - nch + ch + 1)
                    cv.tick(v.tensor_copy(out=dst[:, i:i + w], in_=PS[ch % 2][:, 0:w]))

    if KPH < 1:
        root.close()
        return nc
    p1o = ExitStack()
    sbo = lambda n, s, d: p1o.enter_context(nc.sbuf_tensor(n, s, d))
    rt = sbo("rt", [128, 64, 10], F32)
    iot8i = sbo("iot8i", [128, 64, 8], I32)
    iot8 = sbo("iot8", [128, 64, 8], F32)
    iot8p = sbo("iot8p", [128, 64, 8], F32)
    L3 = sbo("L3", [128, 64, 8], F32)
    P3 = sbo("P3", [128, 64, 8], F32)
    M3 = sbo("M3", [128, 64, 8], F32)
    oh1 = sbo("oh1", [128, 64, 8], F32)
    oh2 = sbo("oh2", [128, 64, 8], F32)
    K3 = sbo("K3", [128, 64, 8], F32)
    C3 = sbo("C3", [128, 64, 8], F32)
    sc = sbo("sc", [128, 64, 8], F32)
    rpk = sbo("rpk", [128, 64, 8], F32)

    def col(tile, k):
        return tile[:, :, k:k + 1]

    def colb(tile, k):
        return tile[:, :, k:k + 1].to_broadcast([128, 64, 8])

    p1a = ExitStack()
    sba = lambda n, s, d: p1a.enter_context(nc.sbuf_tensor(n, s, d))
    xp0 = sba("xp0", [128, 2048], F32)
    xp1 = sba("xp1", [128, 2048], F32)
    xpieces = [xp0, xp1]
    xsq = sba("xsq", [128, 2048], BF16)
    sums = sba("sums", [33, T], F32)
    wgp = sba("wgp", [128, HC * 9], F32)
    gnwF = sba("gnwF", [128, HC], F32)
    csr = sba("csr", [1, 8], F32)

    with nc.Block() as blk:
        @blk.sync
        def _(sync):
            join(sync)
            cl.tick(sync.dma_start(out=wgp[:].rearrange("p (c e) -> p c e", c=HC)[:, :, 0:8],
                                   in_=d_Wgate[:].rearrange("(c p) e -> p c e", p=128)))
            for c in range(HC):
                cl.tick(sync.dma_start(
                    out=gnwF[:, c:c + 1],
                    in_=d_gnw[0:1, c * 128:(c + 1) * 128]
                    .rearrange("o (p q) -> (o p) q", q=1)))

        @blk.gpsimd
        def _(gp):
            cp.tick(gp.iota(iot8i[:], pattern=[[0, 64], [1, 8]], base=0,
                            channel_multiplier=0))

        @blk.vector
        def _(v):
            join(v, cl, cp)
            wgp3 = wgp[:].rearrange("p (c e) -> p c e", c=HC)
            cv.tick(v.tensor_mul(wgp3[:, :, 0:8], wgp3[:, :, 0:8],
                                 gnwF[:].unsqueeze(2).to_broadcast([128, HC, 8])))
            cv.tick(v.memset(wgp3[:, :, 8:9], 1.0))
            cv.tick(v.memset(sums[:], 0.0))
            cv.tick(v.tensor_copy(out=iot8[:], in_=iot8i[:]))
            v.wait_ge(cv.sem, cv.v)
            cv.tick(v.tensor_scalar(out=iot8p[:], in0=iot8[:], scalar1=64.0,
                                    scalar2=None, op0=ALU.add))

    with nc.Block() as blk:
        @blk.tensor
        def _(t):
            join(t)
            ins = None
            for c in range(HC):
                ins = t.matmul(PS[0][0:1, 0:8], ones_c[:], wgp[:, c * 9:c * 9 + 8],
                               start=(c == 0), stop=(c == HC - 1))
            ct.tick(ins)

        @blk.vector
        def _(v):
            join(v, ct)
            cv.tick(v.tensor_copy(out=csr[:], in_=PS[0][0:1, 0:8]))

    with nc.Block() as blk:
        @blk.tensor
        def _(t):
            join(t)
            ct.tick(t.matmul(PS[1][:, 0:8], ones_r[:], csr[:], start=True, stop=True))

        @blk.vector
        def _(v):
            join(v, ct)
            cv.tick(v.tensor_copy(out=colx[:, 8:16], in_=PS[1][:, 0:8]))

    with nc.Block() as blk:
        NPC, NSL = 4, 4
        base_l = cl.v
        base_a = ca.v
        base_t = ct.v
        base_v = cv.v

        @blk.sync
        def _(sync):
            join(sync)
            n = 0
            for c in range(HC):
                for q in range(NPC):
                    if n >= 2:
                        sync.wait_ge(ct.sem, base_t + (n - 1) * 2 * NSL)
                        sync.wait_ge(ca.sem, base_a + n - 1)
                    cl.tick(sync.dma_start(out=xpieces[n % 2][:],
                                           in_=d_xfT[c * 128:(c + 1) * 128,
                                                     q * 2048:(q + 1) * 2048]))
                    n += 1

        @blk.scalar
        def _(a):
            n = 0
            for c in range(HC):
                for q in range(NPC):
                    a.wait_ge(cl.sem, base_l + (n + 1) * 16)
                    if n >= 1:
                        a.wait_ge(ct.sem, base_t + n * 2 * NSL)
                    ca.tick(a.activation(out=xsq[:], in_=xpieces[n % 2][:], func=AF.Square))
                    n += 1

        @blk.tensor
        def _(t):
            n = 0
            for c in range(HC):
                for q in range(NPC):
                    t.wait_ge(cl.sem, base_l + (n + 1) * 16)
                    t.wait_ge(ca.sem, base_a + n + 1)
                    for j in range(NSL):
                        gs = n * NSL + j
                        bk = (gs % 4) * 2
                        if gs >= 4:
                            t.wait_ge(cv.sem, base_v + (gs - 3) * 2)
                        ct.tick(t.matmul(PS[bk][0:9, 0:512], wgp[:, c * 9:(c + 1) * 9],
                                         xpieces[n % 2][:, j * 512:(j + 1) * 512],
                                         start=True, stop=True))
                        ct.tick(t.matmul(PS[bk + 1][0:1, 0:512], ones_cb[:],
                                         xsq[:, j * 512:(j + 1) * 512],
                                         start=True, stop=True))
                    n += 1

        @blk.vector
        def _(v):
            n = 0
            for c in range(HC):
                for q in range(NPC):
                    for j in range(NSL):
                        gs = n * NSL + j
                        bk = (gs % 4) * 2
                        tok = (q * NSL + j)
                        v.wait_ge(ct.sem, base_t + (gs + 1) * 2)
                        sl = slice(tok * 512, (tok + 1) * 512)
                        if c == 0:
                            cv.tick(v.tensor_copy(out=sums[0:9, sl], in_=PS[bk][0:9, 0:512]))
                            cv.tick(v.tensor_copy(out=sums[32:33, sl],
                                                  in_=PS[bk + 1][0:1, 0:512]))
                        else:
                            v.wait_ge(cv.sem, cv.v)
                            cv.tick(v.tensor_add(sums[0:9, sl], sums[0:9, sl],
                                                 PS[bk][0:9, 0:512]))
                            cv.tick(v.tensor_add(sums[32:33, sl], sums[32:33, sl],
                                                 PS[bk + 1][0:1, 0:512]))
                    n += 1

    with nc.Block() as blk:
        base_v = cv.v
        base_t = ct.v

        @blk.tensor
        def _(t):
            join(t)
            for k in range(64):
                if k >= 2:
                    t.wait_ge(cv.sem, base_v + (k - 1) * 2)
                ct.tick(t.transpose(PS[k % 2][0:128, 0:9], sums[0:9, k * 128:(k + 1) * 128],
                                    identf[0:9, 0:9]))
                ct.tick(t.transpose(PS[2 + k % 2][0:128, 0:1],
                                    sums[32:33, k * 128:(k + 1) * 128],
                                    identf[32:33, 32:33]))

        @blk.vector
        def _(v):
            for k in range(64):
                v.wait_ge(ct.sem, base_t + 2 * k + 2)
                cv.tick(v.tensor_copy(out=rt[:, k, 0:9], in_=PS[k % 2][0:128, 0:9]))
                cv.tick(v.tensor_copy(out=rt[:, k, 9:10], in_=PS[2 + k % 2][0:128, 0:1]))

    p1a.close()

    p1b = ExitStack()
    sbb = lambda n, s, d: p1b.enter_context(nc.sbuf_tensor(n, s, d))
    maskT = sbb("maskT", [8, T], BF16)
    cumT = sbb("cumT", [8, T], F32)
    keptT = sbb("keptT", [8, T], BF16)
    keptf = sbb("keptf", [8, T], F32)

    with nc.Block() as blk:
        @blk.vector
        def _(v):
            join(v)

            def vt(ins):
                cv.tick(ins)
                v.wait_ge(cv.sem, cv.v)

            vt(v.tensor_scalar(out=col(sc, 0), in0=rt[:, :, 8:9], scalar1=1.0 / H,
                               scalar2=None, op0=ALU.mult))
            vt(v.tensor_scalar(out=col(sc, 1), in0=rt[:, :, 9:10], scalar1=1.0 / H,
                               scalar2=None, op0=ALU.mult))
            vt(v.tensor_mul(col(sc, 2), col(sc, 0), col(sc, 0)))
            vt(v.tensor_sub(col(sc, 1), col(sc, 1), col(sc, 2)))
            vt(v.tensor_scalar(out=col(sc, 2), in0=col(sc, 1), scalar1=1e-5,
                               scalar2=None, op0=ALU.add))
            vt(v.reciprocal(col(sc, 4), col(sc, 2)))
            vt(v.tensor_mul(col(sc, 5), col(sc, 1), col(sc, 4)))
            vt(v.tensor_scalar(out=col(sc, 5), in0=col(sc, 5), scalar1=1e-6,
                               scalar2=None, op0=ALU.add))

        @blk.scalar
        def _(a):
            join(a, cv)
            ca.tick(a.activation(out=col(sc, 6), in_=col(sc, 2), func=AF.Sqrt))
            ca.tick(a.activation(out=col(sc, 7), in_=col(sc, 5), func=AF.Sqrt))

        @blk.vector
        def _(v):
            join(v, ca)

            def vt(ins):
                cv.tick(ins)
                v.wait_ge(cv.sem, cv.v)

            vt(v.reciprocal(col(sc, 2), col(sc, 6)))
            vt(v.reciprocal(col(sc, 3), col(sc, 7)))
            vt(v.tensor_mul(col(sc, 3), col(sc, 3), col(sc, 2)))
            vt(v.tensor_mul(L3[:], colx[:, 8:16].unsqueeze(1)
                            .to_broadcast([128, 64, 8]), colb(sc, 0)))
            vt(v.tensor_sub(L3[:], rt[:, :, 0:8], L3[:]))
            vt(v.tensor_mul(L3[:], L3[:], colb(sc, 3)))
            vt(v.reduce_max(col(sc, 4), L3[:], axis=AX.X))
            vt(v.tensor_sub(P3[:], L3[:], colb(sc, 4)))

        @blk.scalar
        def _(a):
            join(a, cv)
            ca.tick(a.activation(out=P3[:], in_=P3[:], func=AF.Exp))

        @blk.vector
        def _(v):
            join(v, ca)

            def vt(ins):
                cv.tick(ins)
                v.wait_ge(cv.sem, cv.v)

            vt(v.reduce_sum(col(sc, 5), P3[:], axis=AX.X))
            vt(v.reciprocal(col(sc, 5), col(sc, 5)))
            vt(v.tensor_mul(P3[:], P3[:], colb(sc, 5)))
            vt(v.reduce_max(col(sc, 4), L3[:], axis=AX.X))
            vt(v.tensor_tensor(out=M3[:], in0=L3[:], in1=colb(sc, 4), op=ALU.is_ge))
            vt(v.scalar_tensor_tensor(out=M3[:], in0=M3[:], scalar=-64.0, in1=iot8p[:],
                                      op0=ALU.mult, op1=ALU.add))
            vt(v.tensor_reduce(col(sc, 6), M3[:], axis=AX.X, op=ALU.min))
            vt(v.tensor_tensor(out=oh1[:], in0=iot8[:], in1=colb(sc, 6), op=ALU.is_equal))
            vt(v.tensor_scalar(out=M3[:], in0=oh1[:], scalar1=1e30, scalar2=None,
                               op0=ALU.mult))
            vt(v.tensor_sub(M3[:], L3[:], M3[:]))
            vt(v.reduce_max(col(sc, 4), M3[:], axis=AX.X))
            vt(v.tensor_tensor(out=K3[:], in0=M3[:], in1=colb(sc, 4), op=ALU.is_ge))
            vt(v.scalar_tensor_tensor(out=K3[:], in0=K3[:], scalar=-64.0, in1=iot8p[:],
                                      op0=ALU.mult, op1=ALU.add))
            vt(v.tensor_reduce(col(sc, 7), K3[:], axis=AX.X, op=ALU.min))
            vt(v.tensor_tensor(out=oh2[:], in0=iot8[:], in1=colb(sc, 7), op=ALU.is_equal))
            vt(v.tensor_mul(M3[:], P3[:], oh1[:]))
            vt(v.reduce_sum(col(sc, 0), M3[:], axis=AX.X))
            vt(v.tensor_mul(M3[:], P3[:], oh2[:]))
            vt(v.reduce_sum(col(sc, 1), M3[:], axis=AX.X))
            vt(v.tensor_add(M3[:], oh1[:], oh2[:]))
            vt(v.tensor_scalar(out=K3[:], in0=P3[:], scalar1=0.0, scalar2=None,
                               op0=ALU.is_gt))
            vt(v.tensor_mul(M3[:], M3[:], K3[:]))

    with nc.Block() as blk:
        base_v = cv.v
        base_t = ct.v

        @blk.tensor
        def _(t):
            join(t)
            for k in range(64):
                if k >= 2:
                    t.wait_ge(cv.sem, base_v + k - 1)
                ct.tick(t.transpose(PS[k % 2][0:8, 0:128], M3[:, k, :], identf[:]))

        @blk.vector
        def _(v):
            for k in range(64):
                v.wait_ge(ct.sem, base_t + k + 1)
                cv.tick(v.tensor_copy(out=maskT[0:8, k * 128:(k + 1) * 128],
                                      in_=PS[k % 2][0:8, 0:128]))

    with nc.Block() as blk:
        @blk.vector
        def _(v):
            join(v)

            def vt(ins):
                cv.tick(ins)
                v.wait_ge(cv.sem, cv.v)

            vt(v.tensor_tensor_scan(out=cumT[:], data0=maskT[:], data1=maskT[:],
                                    initial=0.0, op0=ALU.add, op1=ALU.max))
            vt(v.tensor_scalar(out=keptT[:], in0=cumT[:], scalar1=float(CAP),
                               scalar2=None, op0=ALU.is_le))
            vt(v.tensor_mul(keptT[:], keptT[:], maskT[:]))
            vt(v.tensor_copy(out=keptf[:], in_=keptT[:]))
            for s in range(NCORE):
                sl = slice(s * TPC, (s + 1) * TPC)
                vt(v.tensor_tensor_scan(out=cumT[:, sl], data0=keptT[:, sl],
                                        data1=keptT[:, sl], initial=0.0,
                                        op0=ALU.add, op1=ALU.max))

    with nc.Block() as blk:
        base_t = ct.v
        base_v = cv.v

        @blk.tensor
        def _(t):
            join(t)
            for k in range(64):
                if k >= 2:
                    t.wait_ge(cv.sem, base_v + (k - 1) * 2)
                ct.tick(t.transpose(PS[4 + k % 2][0:128, 0:8],
                                    keptf[0:8, k * 128:(k + 1) * 128], identf[0:8, 0:8]))
                ct.tick(t.transpose(PS[6 + k % 2][0:128, 0:8],
                                    cumT[0:8, k * 128:(k + 1) * 128], identf[0:8, 0:8]))

        @blk.vector
        def _(v):
            for k in range(64):
                v.wait_ge(ct.sem, base_t + 2 * k + 2)
                cv.tick(v.tensor_copy(out=K3[:, k, :], in_=PS[4 + k % 2][0:128, 0:8]))
                cv.tick(v.tensor_copy(out=C3[:, k, :], in_=PS[6 + k % 2][0:128, 0:8]))

    with nc.Block() as blk:
        @blk.vector
        def _(v):
            join(v)

            def vt(ins):
                cv.tick(ins)
                v.wait_ge(cv.sem, cv.v)

            vt(v.tensor_mul(M3[:], K3[:], oh1[:]))
            vt(v.reduce_sum(col(sc, 2), M3[:], axis=AX.X))
            vt(v.tensor_mul(M3[:], K3[:], oh2[:]))
            vt(v.reduce_sum(col(sc, 3), M3[:], axis=AX.X))
            vt(v.tensor_mul(M3[:], C3[:], oh1[:]))
            vt(v.reduce_sum(col(sc, 4), M3[:], axis=AX.X))
            vt(v.tensor_mul(M3[:], C3[:], oh2[:]))
            vt(v.reduce_sum(col(sc, 5), M3[:], axis=AX.X))
            vt(v.tensor_scalar(out=col(sc, 4), in0=col(sc, 4), scalar1=1.0,
                               scalar2=None, op0=ALU.subtract))
            vt(v.tensor_scalar(out=col(sc, 5), in0=col(sc, 5), scalar1=1.0,
                               scalar2=None, op0=ALU.subtract))
            vt(v.tensor_copy(out=col(rpk, 0), in_=col(sc, 6)))
            vt(v.tensor_copy(out=col(rpk, 1), in_=col(sc, 7)))
            vt(v.tensor_mul(col(rpk, 2), col(sc, 0), col(sc, 2)))
            vt(v.tensor_mul(col(rpk, 3), col(sc, 1), col(sc, 3)))
            vt(v.tensor_scalar(out=col(rpk, 4), in0=col(sc, 6), scalar1=float(CAP),
                               scalar2=None, op0=ALU.mult))
            vt(v.tensor_add(col(rpk, 4), col(rpk, 4), col(sc, 4)))
            vt(v.tensor_scalar(out=col(rpk, 6), in0=col(sc, 2), scalar1=-65536.0,
                               scalar2=65536.0, op0=ALU.mult, op1=ALU.add))
            vt(v.tensor_add(col(rpk, 4), col(rpk, 4), col(rpk, 6)))
            vt(v.tensor_scalar(out=col(rpk, 5), in0=col(sc, 7), scalar1=float(CAP),
                               scalar2=None, op0=ALU.mult))
            vt(v.tensor_add(col(rpk, 5), col(rpk, 5), col(sc, 5)))
            vt(v.tensor_scalar(out=col(rpk, 6), in0=col(sc, 3), scalar1=-65536.0,
                               scalar2=65536.0, op0=ALU.mult, op1=ALU.add))
            vt(v.tensor_add(col(rpk, 5), col(rpk, 5), col(rpk, 6)))
            vt(v.tensor_copy(out=col(rpk, 6), in_=col(sc, 4)))
            vt(v.tensor_copy(out=col(rpk, 7), in_=col(sc, 5)))

        @blk.sync
        def _(sync):
            join(sync, cv)
            cl.tick(sync.dma_start(out=s_rtab[:].rearrange("(p k) f -> p k f", p=128),
                                   in_=rpk[:]))

        @blk.gpsimd
        def _(gp):
            join(gp, cl)
            for k in range(8):
                cg.tick(gp.indirect_dma_start(
                    out=rt_mine[:, k, :], out_offset=None,
                    in_=s_rtab[:],
                    in_offset=IndirectOffsetOnAxis(ap=g_idx[:, k:k + 1], axis=0),
                    bounds_check=T - 1, oob_is_err=False))

    if "rt" in DBG:
        with nc.Block() as blk:
            @blk.sync
            def _(sync):
                join(sync)
                for k in range(8):
                    cl.tick(sync.dma_start(out=dbg["dbg_rt"][k * 128:(k + 1) * 128, :],
                                           in_=rt_mine[:, k, :]))

    p1b.close()
    p1o.close()

    if KPH < 2:
        root.close()
        return nc
    mean_slot = {}
    slot_n = [0]

    p2 = ExitStack()
    sb2 = lambda n, s, d: p2.enter_context(nc.sbuf_tensor(n, s, d))
    wch0 = sb2("wch0", [128, IF], F32)
    wch1 = sb2("wch1", [128, IF], F32)
    wchs = [wch0, wch1]
    q8ch = sb2("q8ch", [128, IF], FP8)
    absd = sb2("absd", [128, IF], BF16)
    i32ch = sb2("i32ch", [128, 2048], I32)
    stag = sb2("stag", [128, 16 * 512], FP8)
    stagd = sb2("stagd", [128, 44 * 512], FP8)

    def mean_job(mk, nch, wdt, key):
        sidx = slot_n[0]
        mean_slot[key] = sidx
        slot_n[0] += 2
        with nc.Block() as blk:
            base_l = cl.v
            base_a = ca.v

            @blk.sync
            def _(sync):
                join(sync)
                for c in range(nch):
                    if c >= 2:
                        sync.wait_ge(ca.sem, base_a + c - 1)
                    cl.tick(sync.dma_start(out=wchs[c % 2][:, 0:wdt], in_=mk(c)))

            base_v = cv.v

            @blk.scalar
            def _(a):
                for c in range(nch):
                    a.wait_ge(cl.sem, base_l + (c + 1) * 16)
                    if c >= 1:
                        a.wait_ge(cv.sem, base_v + c)
                    ca.tick(a.activation(out=absd[:, 0:wdt], in_=wchs[c % 2][:, 0:wdt],
                                         func=AF.Abs, accum_out=wcl[:, 0:1]))

            @blk.vector
            def _(v):
                for c in range(nch):
                    v.wait_ge(ca.sem, base_a + c + 1)
                    v.wait_ge(cv.sem, cv.v)
                    if c == 0:
                        cv.tick(v.tensor_copy(out=wcl[:, 1:2], in_=wcl[:, 0:1]))
                    else:
                        cv.tick(v.tensor_add(wcl[:, 1:2], wcl[:, 1:2], wcl[:, 0:1]))

        with nc.Block() as blk:
            @blk.tensor
            def _(t):
                join(t)
                ct.tick(t.matmul(PS[0][0:1, 0:1], wcl[:, 1:2], ones_c[:],
                                 start=True, stop=True))

            @blk.vector
            def _(v):
                join(v, ct)
                nel = float(nch * 128 * wdt)
                cv.tick(v.tensor_scalar(out=wrow[0:1, sidx + 1:sidx + 2],
                                        in0=PS[0][0:1, 0:1], scalar1=1.0 / nel,
                                        scalar2=1e-5, op0=ALU.mult, op1=ALU.max))
                v.wait_ge(cv.sem, cv.v)
                cv.tick(v.reciprocal(wrow[0:1, sidx:sidx + 1],
                                     wrow[0:1, sidx + 1:sidx + 2]))

    if KP2 >= 1:
        mean_job(lambda c: d_sWg[c * 128:(c + 1) * 128, :], HC, 2048, "sWg")
    if KP2 >= 2:
        mean_job(lambda c: d_sWd[c * 128:(c + 1) * 128, :], IAC, 2048, "sWd")
        for e in range(E):
            mean_job(lambda c, e=e: d_Wg[e, c * 128:(c + 1) * 128, :], HC, 2048, f"Wg{e}")
            mean_job(lambda c, e=e: d_Wd[e, c * 128:(c + 1) * 128, :], IAC, 2048, f"Wd{e}")

    with nc.Block() as blk:
        @blk.tensor
        def _(t):
            join(t)
            ct.tick(t.matmul(PS[1][:, 0:64], ones_r[:], wrow[:], start=True, stop=True))

        @blk.vector
        def _(v):
            join(v, ct)
            cv.tick(v.tensor_copy(out=b_wm[:], in_=PS[1][:, 0:64]))

    def tern_quant2(mk_in, mk_out, nch, wdt, key):
        sidx = mean_slot[key]
        with nc.Block() as blk:
            base_l = cl.v
            base_a = ca.v
            base_v = cv.v
            base_s = cs.v

            @blk.sync
            def _(sync):
                join(sync)
                for c in range(nch):
                    if c >= 2:
                        sync.wait_ge(ca.sem, base_a + c - 1)
                    cl.tick(sync.dma_start(out=wchs[c % 2][:, 0:wdt], in_=mk_in(c)))

            @blk.scalar
            def _(a):
                for c in range(nch):
                    a.wait_ge(cl.sem, base_l + (c + 1) * 16)
                    if c >= 1:
                        a.wait_ge(cv.sem, base_v + c)
                    ca.tick(a.activation(out=i32ch[:, 0:wdt], in_=wchs[c % 2][:, 0:wdt],
                                         func=AF.Identity, scale=b_wm[:, sidx:sidx + 1]))
                    a.wait_ge(cv.sem, base_v + c + 1)
                    cs.tick(a.dma_start(out=mk_out(c), in_=q8ch[:, 0:wdt]))

            @blk.vector
            def _(v):
                for c in range(nch):
                    v.wait_ge(ca.sem, base_a + c + 1)
                    if c >= 1:
                        v.wait_ge(cs.sem, base_s + c * 16)
                    cv.tick(v.tensor_scalar(out=q8ch[:, 0:wdt], in0=i32ch[:, 0:wdt],
                                            scalar1=-1.0, scalar2=1.0,
                                            op0=ALU.max, op1=ALU.min))

    if KP2 >= 4:
        tern_quant2(lambda c: d_sWg[c * 128:(c + 1) * 128, :],
                    lambda c: s_qsWg[c * 128:(c + 1) * 128, :], HC, 2048, "sWg")
        tern_quant2(lambda c: d_sWd[c * 128:(c + 1) * 128, :],
                    lambda c: s_qsWd[c * 128:(c + 1) * 128, :], IAC, 2048, "sWd")

    def fp8_quant_T(load_ap, nch, rowlen, wsrow_store, stag_buf, tile_in_group,
                    flush_after, nflush):
        n_tr = rowlen // 128
        vper = 5 + (n_tr if KQT >= 2 else 0)
        tper = n_tr + (1 if KQT >= 5 else 0)
        aper = n_tr + (1 if KQT >= 5 else 0)
        with nc.Block() as blk:
            base_l = cl.v
            base_v = cv.v
            base_a = ca.v
            base_t = ct.v
            base_g = cg.v

            @blk.sync
            def _(sync):
                join(sync)
                for c in range(nch):
                    if c >= 2:
                        sync.wait_ge(cv.sem, base_v + (c - 1) * vper)
                    cl.tick(sync.dma_start(out=wchs[c % 2][:, 0:rowlen], in_=load_ap(c)))

            @blk.vector
            def _(v):
                for c in range(nch):
                    v.wait_ge(cl.sem, base_l + (c + 1) * 16)
                    v.wait_ge(cv.sem, cv.v)
                    cv.tick(v.reduce_max(wcl[:, 0:1], wchs[c % 2][:, 0:rowlen],
                                         axis=AX.X, apply_absolute_value=True))
                    v.wait_ge(cv.sem, cv.v)
                    cv.tick(v.reciprocal(wcl[:, 1:2], wcl[:, 0:1]))
                    v.wait_ge(cv.sem, cv.v)
                    cv.tick(v.tensor_scalar(out=wcl[:, 2:3], in0=wcl[:, 1:2],
                                            scalar1=224.0, scalar2=None, op0=ALU.mult))
                    cv.tick(v.tensor_scalar(out=wcl[:, 3:4], in0=wcl[:, 0:1],
                                            scalar1=1.0 / 224.0, scalar2=None,
                                            op0=ALU.mult))
                    v.wait_ge(cv.sem, cv.v)
                    cv.tick(v.tensor_scalar(out=q8ch[:, 0:rowlen], in0=wchs[c % 2][:, 0:rowlen],
                                            scalar1=wcl[:, 2:3], scalar2=None, op0=ALU.mult))
                    v.wait_ge(cv.sem, cv.v)
                    if KQT < 2:
                        continue
                    for tr in range(n_tr):
                        idx = c * n_tr + tr
                        if idx >= 2 and KQT >= 3:
                            t_idx = (c * tper + tr - 2 if tr >= 2
                                     else (c - 1) * tper + n_tr + tr - 2)
                            v.wait_ge(ct.sem, base_t + t_idx + 1)
                        cv.tick(v.tensor_copy(out=stgf[:, (idx % 2) * 128:(idx % 2) * 128 + 128],
                                              in_=q8ch[:, tr * 128:(tr + 1) * 128]))

            @blk.tensor
            def _(t):
                if KQT < 3:
                    return
                for c in range(nch):
                    for tr in range(n_tr):
                        t.wait_ge(cv.sem, base_v + c * vper + 5 + tr + 1)
                        aidx = c * aper + tr
                        if aidx >= 2 and KQT >= 4:
                            t.wait_ge(ca.sem, base_a + aidx - 1)
                        ct.tick(t.transpose(PS[4 + (c * n_tr + tr) % 2][0:128, 0:128],
                                            stgf[:, ((c * n_tr + tr) % 2) * 128:
                                                 ((c * n_tr + tr) % 2) * 128 + 128],
                                            identf[:]))
                    if KQT < 5:
                        continue
                    t.wait_ge(cv.sem, base_v + c * vper + 4)
                    if c >= 1 and KQT >= 4:
                        t.wait_ge(ca.sem, base_a + c * aper)
                    ct.tick(t.transpose(PS[6][0:1, 0:128], wcl[:, 3:4], identf[:]))

            base_s = cs.v

            @blk.scalar
            def _(a):
                if KQT < 4:
                    return
                for c in range(nch):
                    if c % 4 == 0 and c > 0 and KQT >= 6:
                        a.wait_ge(cs.sem, base_s + (c // 4) * nflush * 16)
                    for tr in range(n_tr):
                        a.wait_ge(ct.sem, base_t + c * tper + tr + 1)
                        dstt, dsto = tile_in_group(c, tr)
                        ca.tick(a.activation(out=stag_buf[:, dstt * 512 + dsto * 128:
                                                          dstt * 512 + dsto * 128 + 128],
                                             in_=PS[4 + (c * n_tr + tr) % 2][0:128, 0:128],
                                             func=AF.Identity))
                    if KQT >= 5:
                        a.wait_ge(ct.sem, base_t + c * tper + tper)
                        ca.tick(a.activation(out=wsrow_store(c), in_=PS[6][0:1, 0:128],
                                             func=AF.Identity))
                    if KQT >= 6:
                        for (dram_ap, ssl) in flush_after(c):
                            cs.tick(a.dma_start(out=dram_ap, in_=stag_buf[:, ssl]))

    wsr_wg = sb2("wsr_wg", [1, IF], F32)
    wsr_wu = sb2("wsr_wu", [1, IF], F32)
    wsr_wd = sb2("wsr_wd", [1, H], F32)

    def tile_ig_up(c, tr):
        return (tr, c % 4)

    def flush_up(s_out):
        def fl(c):
            if c % 4 == 3:
                ic4 = c // 4
                return [(s_out[hc, ic4, :, :], slice(hc * 512, (hc + 1) * 512))
                        for hc in range(HC)]
            return []
        return fl

    def tile_ig_dn(c, tr):
        return (tr, c % 4)

    def flush_dn(c):
        if c % 4 == 3:
            hc4 = c // 4
            return [(s_qwdT[ic, hc4, :, :], slice(ic * 512, (ic + 1) * 512))
                    for ic in range(IFC)]
        return []

    if KP2 >= 5:
        fp8_quant_T(lambda c: d_wg[c * 128:(c + 1) * 128, :], IFC, 2048,
                    lambda c: wsr_wg[0:1, c * 128:(c + 1) * 128], stag, tile_ig_up,
                    flush_up(s_qwgT), HC)
    if KP2 >= 6:
        fp8_quant_T(lambda c: d_wu[c * 128:(c + 1) * 128, :], IFC, 2048,
                    lambda c: wsr_wu[0:1, c * 128:(c + 1) * 128], stag, tile_ig_up,
                    flush_up(s_qwuT), HC)
    if KP2 >= 7:
        fp8_quant_T(lambda c: d_wd[c * 128:(c + 1) * 128, :], HC, IF,
                    lambda c: wsr_wd[0:1, c * 128:(c + 1) * 128], stagd, tile_ig_dn,
                    flush_dn, IFC)

    with nc.Block() as blk:
        @blk.sync
        def _(sync):
            join(sync)
            cl.tick(sync.dma_start(out=s_wgs[:], in_=wsr_wg[:]))
            cl.tick(sync.dma_start(out=s_wus[:], in_=wsr_wu[:]))
            cl.tick(sync.dma_start(out=s_wds[:], in_=wsr_wd[:]))

    p2.close()

    if KPH < 3:
        root.close()
        return nc
    p4 = ExitStack()
    sb4 = lambda n, s, d: p4.enter_context(nc.sbuf_tensor(n, s, d))
    b_sNg = sb4("b_sNg", [128, H], F32)
    b_sNd = sb4("b_sNd", [128, IA], F32)
    b_Wsg = sb4("b_Wsg", [128, H], F32)
    arena = sb4("arena", [128, NT * IF], F32)
    qtTs = sb4("qtTs", [128, NT * IF], BF16)
    qxT = sb4("qxT", [128, NT * H], BF16)
    accg = sb4("accg", [128, NT * H], F32)
    xpg = sb4("xpg", [128, NT * H], F32)
    q8g = sb4("q8g", [128, NT * IF], FP8)
    gch = sb4("gch", [128, NT * 512], F32)
    uch = sb4("uch", [128, NT * 512], F32)
    sich = sb4("sich", [128, NT * 512], F32)
    slabU = sb4("slabU", [128, IFC * 512], FP8)
    wslab = slabU
    wdslab = slabU
    tslab = slabU
    wbc = sb4("wbc", [128, 1024], F32)
    shar = arena[:, 0:NT * H]
    shy = arena[:, NT * H:2 * NT * H]
    shh = arena[:, 2 * NT * H:2 * NT * H + NT * IA]
    i32g = qtTs[:].bitcast(I32)[:, 0:NT * H]
    k1T = sb4("k1T", [128, NT * H], BF16)
    k2T = sb4("k2T", [128, NT * IA], BF16)
    gcol = sb4("gcol", [128, NT * 16], F32)

    bcast_from_dram(b_sNg, d_sNg[0:1, :], H)
    bcast_from_dram(b_sNd, d_sNd[0:1, :], IA)
    bcast_from_dram(b_Wsg, d_Wsg[0:1, :], H)

    def gc(kt, i):
        return gcol[:, kt * 16 + i:kt * 16 + i + 1]

    def transpose_set(srcf, dst, nblk, blk128):
        with nc.Block() as blk:
            base_t = ct.v
            base_a = ca.v

            @blk.tensor
            def _(t):
                join(t)
                for i in range(nblk):
                    if i >= 2:
                        t.wait_ge(ca.sem, base_a + i - 1)
                    ct.tick(t.transpose(PS[4 + i % 2][0:128, 0:128], srcf(i), identf[:]))

            @blk.scalar
            def _(a):
                for i in range(nblk):
                    a.wait_ge(ct.sem, base_t + i + 1)
                    ca.tick(a.activation(out=dst(i), in_=PS[4 + i % 2][0:128, 0:128],
                                         func=AF.Identity))

    for g in range(NG):
        r0 = g * GT

        with nc.Block() as blk:
            @blk.sync
            def _(sync):
                join(sync)
                cl.tick(sync.dma_start(out=accg[:].rearrange("p (k h) -> p k h", k=NT),
                                       in_=d_xsh[r0:r0 + GT, :]
                                       .rearrange("(k p) h -> p k h", p=128)))

            @blk.scalar
            def _(a):
                join(a, cl)
                for kt in range(NT):
                    ca.tick(a.activation(out=shar[:, kt * H:(kt + 1) * H],
                                         in_=accg[:, kt * H:(kt + 1) * H],
                                         func=AF.Square, accum_out=gc(kt, 0)))

            @blk.vector
            def _(v):
                join(v, ca, cl)

                def vt(ins):
                    cv.tick(ins)
                    v.wait_ge(cv.sem, cv.v)

                for kt in range(NT):
                    sl = slice(kt * H, (kt + 1) * H)
                    vt(v.reduce_sum(gc(kt, 1), accg[:, sl], axis=AX.X))
                    vt(v.tensor_scalar(out=gc(kt, 1), in0=gc(kt, 1), scalar1=1.0 / H,
                                       scalar2=None, op0=ALU.mult))
                    vt(v.tensor_scalar(out=gc(kt, 0), in0=gc(kt, 0), scalar1=1.0 / H,
                                       scalar2=None, op0=ALU.mult))
                    vt(v.tensor_mul(gc(kt, 2), gc(kt, 1), gc(kt, 1)))
                    vt(v.tensor_sub(gc(kt, 0), gc(kt, 0), gc(kt, 2)))
                    vt(v.tensor_scalar(out=gc(kt, 2), in0=gc(kt, 0), scalar1=1e-5,
                                       scalar2=None, op0=ALU.add))

            @blk.scalar
            def _(a):
                join(a, cv)
                for kt in range(NT):
                    ca.tick(a.activation(out=gc(kt, 3), in_=gc(kt, 2), func=AF.Sqrt))

            @blk.vector
            def _(v):
                join(v, ca)

                def vt(ins):
                    cv.tick(ins)
                    v.wait_ge(cv.sem, cv.v)

                for kt in range(NT):
                    sl = slice(kt * H, (kt + 1) * H)
                    vt(v.reciprocal(gc(kt, 3), gc(kt, 3)))
                    vt(v.tensor_scalar(out=gc(kt, 4), in0=gc(kt, 1), scalar1=-1.0,
                                       scalar2=None, op0=ALU.mult))
                    vt(v.tensor_scalar(out=xpg[:, sl], in0=accg[:, sl],
                                       scalar1=gc(kt, 4), scalar2=gc(kt, 3),
                                       op0=ALU.add, op1=ALU.mult))
                    vt(v.reciprocal(gc(kt, 5), gc(kt, 2)))
                    vt(v.tensor_mul(gc(kt, 5), gc(kt, 5), gc(kt, 0)))
                    vt(v.tensor_scalar(out=gc(kt, 5), in0=gc(kt, 5), scalar1=1e-8,
                                       scalar2=None, op0=ALU.add))

            @blk.scalar
            def _(a):
                join(a, cv)
                for kt in range(NT):
                    ca.tick(a.activation(out=gc(kt, 6), in_=gc(kt, 5), func=AF.Sqrt))
                for kt in range(NT):
                    cs.tick(a.dma_start(out=s_xp[r0 + kt * 128:r0 + (kt + 1) * 128, :],
                                        in_=xpg[:, kt * H:(kt + 1) * H]))

            @blk.vector
            def _(v):
                join(v, ca)
                for kt in range(NT):
                    v.wait_ge(cv.sem, cv.v)
                    cv.tick(v.reciprocal(gc(kt, 5), gc(kt, 6)))

            @blk.gpsimd
            def _(gp):
                join(gp, cv)
                for kt in range(NT):
                    gk = g * NT + kt
                    cp.tick(gp.tensor_copy(out=u32c[:, 0:1], in_=rt_mine[:, gk, 4:5]))
                    cp.tick(gp.tensor_copy(out=u32c[:, 1:2], in_=rt_mine[:, gk, 5:6]))
                    gp.wait_ge(cp.sem, cp.v)
                    cg.tick(gp.indirect_dma_start(
                        out=s_xg[:],
                        out_offset=IndirectOffsetOnAxis(ap=u32c[:, 0:1], axis=0),
                        in_=xpg[:, kt * H:(kt + 1) * H], in_offset=None,
                        bounds_check=E * CAP - 1, oob_is_err=False))
                    cg.tick(gp.indirect_dma_start(
                        out=s_xg[:],
                        out_offset=IndirectOffsetOnAxis(ap=u32c[:, 1:2], axis=0),
                        in_=xpg[:, kt * H:(kt + 1) * H], in_offset=None,
                        bounds_check=E * CAP - 1, oob_is_err=False))

        with nc.Block() as blk:
            @blk.vector
            def _(v):
                join(v)

                def vt(ins):
                    cv.tick(ins)
                    v.wait_ge(cv.sem, cv.v)

                for kt in range(NT):
                    sl = slice(kt * H, (kt + 1) * H)
                    vt(v.reduce_max(gc(kt, 7), xpg[:, sl], axis=AX.X,
                                    apply_absolute_value=True))
                    vt(v.reciprocal(gc(kt, 8), gc(kt, 7)))
                    vt(v.tensor_scalar(out=gc(kt, 8), in0=gc(kt, 8), scalar1=224.0,
                                       scalar2=None, op0=ALU.mult))
                    vt(v.tensor_scalar(out=gc(kt, 9), in0=gc(kt, 7), scalar1=1.0 / 224.0,
                                       scalar2=None, op0=ALU.mult))
                    vt(v.tensor_scalar(out=q8g[:, sl], in0=xpg[:, sl],
                                       scalar1=gc(kt, 8), scalar2=None, op0=ALU.mult))
                for kt in range(NT):
                    sl = slice(kt * H, (kt + 1) * H)
                    vt(v.tensor_copy(out=shar[:, sl], in_=q8g[:, sl]))

        transpose_set(lambda i: shar[:, i * 128:(i + 1) * 128],
                      lambda i: qxT[:, i * 128:(i + 1) * 128], NT * HC, 128)

        NJ = (IF // 512) * 2
        with nc.Block() as blk:
            base_l4 = cl.v
            base_t4 = ct.v
            base_v4 = cv.v
            base_a4 = ca.v
            VPJ = 3 + 4

            def vbase(j):
                full = j // 2
                r = 3 + 11
                return base_v4 + full * r + (3 if j % 2 else 0) * 0 + (j % 2) * 3

            def vend(j):
                return vbase(j + 1) if False else (vbase(j) + (3 if j % 2 == 0 else 11))

            @blk.sync
            def _(sync):
                join(sync)
                for j in range(NJ):
                    ic, m = j // 2, j % 2
                    s_w = s_qwgT if m == 0 else s_qwuT
                    s_ws = s_wgs if m == 0 else s_wus
                    if j >= 2:
                        sync.wait_ge(ct.sem, base_t4 + (j - 1) * (NT + 1))
                    for c in range(HC):
                        cl.tick(sync.dma_start(
                            out=wslab[:, (j % 2) * HC * 512 + c * 512:
                                      (j % 2) * HC * 512 + (c + 1) * 512],
                            in_=s_w[c, ic, :, :]))
                    cl.tick(sync.dma_start(out=stg_row[0:1, (j % 2) * 512:(j % 2) * 512 + 512],
                                           in_=s_ws[0:1, ic * 512:(ic + 1) * 512]))

            @blk.tensor
            def _(t):
                t.wait_ge(cv.sem, base_v4)
                t.wait_ge(ca.sem, base_a4)
                for j in range(NJ):
                    t.wait_ge(cl.sem, base_l4 + (j + 1) * 17 * 16)
                    if j >= 1:
                        t.wait_ge(cv.sem, vbase(j - 1) + 3)
                    ct.tick(t.matmul(PS[2 + j % 2][:, 0:512], ones_r[:],
                                     stg_row[0:1, (j % 2) * 512:(j % 2) * 512 + 512],
                                     start=True, stop=True))
                    for kt in range(NT):
                        ins = None
                        for c in range(HC):
                            ins = t.matmul(PS[kt][0:128, 0:512],
                                           qxT[:, (kt * HC + c) * 128:
                                               (kt * HC + c) * 128 + 128],
                                           wslab[:, (j % 2) * HC * 512 + c * 512:
                                                 (j % 2) * HC * 512 + (c + 1) * 512],
                                           start=(c == 0), stop=(c == HC - 1))
                        ct.tick(ins)

            @blk.vector
            def _(v):
                for j in range(NJ):
                    ic, m = j // 2, j % 2
                    dst = gch if m == 0 else uch
                    if m == 0 and ic >= 1:
                        v.wait_ge(ca.sem, base_a4 + ic)
                    v.wait_ge(ct.sem, base_t4 + j * (NT + 1) + 1)
                    v.wait_ge(cv.sem, cv.v)
                    cv.tick(v.tensor_copy(out=wbc[:, (j % 2) * 512:(j % 2) * 512 + 512],
                                          in_=PS[2 + j % 2][:, 0:512]))
                    for kt in range(NT):
                        v.wait_ge(ct.sem, base_t4 + j * (NT + 1) + 1 + kt + 1)
                        v.wait_ge(cv.sem, cv.v)
                        cv.tick(v.scalar_tensor_tensor(
                            out=dst[:, kt * 512:(kt + 1) * 512],
                            in0=PS[kt][0:128, 0:512], scalar=gc(kt, 9),
                            in1=wbc[:, (j % 2) * 512:(j % 2) * 512 + 512],
                            op0=ALU.mult, op1=ALU.mult))
                    if m == 1:
                        v.wait_ge(ca.sem, base_a4 + ic + 1)
                        for kt in range(NT):
                            for sub in range(4):
                                icb = ic * 4 + sub
                                off = icb * NT * 128 + kt * 128
                                v.wait_ge(cv.sem, cv.v)
                                cv.tick(v.tensor_mul(
                                    arena[:, off:off + 128],
                                    sich[:, kt * 512 + sub * 128:
                                          kt * 512 + (sub + 1) * 128],
                                    uch[:, kt * 512 + sub * 128:
                                        kt * 512 + (sub + 1) * 128]))

            @blk.scalar
            def _(a):
                for ic in range(IF // 512):
                    a.wait_ge(cv.sem, vbase(2 * ic) + 3)
                    ca.tick(a.activation(out=sich[:], in_=gch[:], func=AF.Silu))

        with nc.Block() as blk:
            @blk.vector
            def _(v):
                join(v)

                def vt(ins):
                    cv.tick(ins)
                    v.wait_ge(cv.sem, cv.v)

                for kt in range(NT):
                    tv = arena[:].rearrange("p (i k c) -> p i k c", i=IFC, k=NT)
                    vt(v.reduce_max(gch[:, kt * IFC:(kt + 1) * IFC].unsqueeze(2),
                                    tv[:, :, kt, :], axis=AX.X,
                                    apply_absolute_value=True))
                    vt(v.reduce_max(gc(kt, 10), gch[:, kt * IFC:(kt + 1) * IFC],
                                    axis=AX.X))
                    vt(v.reciprocal(gc(kt, 11), gc(kt, 10)))
                    vt(v.tensor_scalar(out=gc(kt, 11), in0=gc(kt, 11), scalar1=224.0,
                                       scalar2=None, op0=ALU.mult))
                    vt(v.tensor_scalar(out=gc(kt, 12), in0=gc(kt, 10), scalar1=1.0 / 224.0,
                                       scalar2=None, op0=ALU.mult))
                for kt in range(NT):
                    for icb in range(IFC):
                        off = icb * NT * 128 + kt * 128
                        cv.tick(v.tensor_scalar(out=q8g[:, off:off + 128],
                                                in0=arena[:, off:off + 128],
                                                scalar1=gc(kt, 11), scalar2=None,
                                                op0=ALU.mult))
                v.wait_ge(cv.sem, cv.v)
                cv.tick(v.tensor_copy(out=arena[:], in_=q8g[:]))

        def _qt_src(i):
            kt, icb = i // IFC, i % IFC
            off = icb * NT * 128 + kt * 128
            return arena[:, off:off + 128]

        def _qt_dst(i):
            kt, icb = i // IFC, i % IFC
            off = (kt * IFC + icb) * 128
            return qtTs[:, off:off + 128]

        transpose_set(_qt_src, _qt_dst, NT * IFC, 128)

        for hc4 in range(HC // 4):
            with nc.Block() as blk:
                @blk.sync
                def _(sync):
                    join(sync)
                    for c in range(IFC):
                        cl.tick(sync.dma_start(out=wdslab[:, c * 512:(c + 1) * 512],
                                               in_=s_qwdT[c, hc4, :, :]))
                    cl.tick(sync.dma_start(out=stg_row[0:1, 0:512],
                                           in_=s_wds[0:1, hc4 * 512:(hc4 + 1) * 512]))

                @blk.tensor
                def _(t):
                    join(t, cl, cv, ca)
                    ct.tick(t.matmul(PS[2][:, 0:512], ones_r[:], stg_row[0:1, 0:512],
                                     start=True, stop=True))
                    for kt in range(NT):
                        ins = None
                        for c in range(IFC):
                            ins = t.matmul(PS[kt][0:128, 0:512],
                                           qtTs[:, (kt * IFC + c) * 128:
                                                (kt * IFC + c) * 128 + 128],
                                           wdslab[:, c * 512:(c + 1) * 512],
                                           start=(c == 0), stop=(c == IFC - 1))
                        ct.tick(ins)

                @blk.vector
                def _(v):
                    join(v, ct)
                    v.wait_ge(cv.sem, cv.v)
                    cv.tick(v.tensor_copy(out=wbc[:, 0:512], in_=PS[2][:, 0:512]))
                    for kt in range(NT):
                        v.wait_ge(cv.sem, cv.v)
                        cv.tick(v.scalar_tensor_tensor(
                            out=gch[:, kt * 512:(kt + 1) * 512],
                            in0=PS[kt][0:128, 0:512], scalar=gc(kt, 12),
                            in1=wbc[:, 0:512], op0=ALU.mult, op1=ALU.mult))
                        v.wait_ge(cv.sem, cv.v)
                        sl = slice(kt * H + hc4 * 512, kt * H + (hc4 + 1) * 512)
                        cv.tick(v.tensor_add(accg[:, sl], accg[:, sl],
                                             gch[:, kt * 512:(kt + 1) * 512]))

        with nc.Block() as blk:
            @blk.vector
            def _(v):
                join(v)

                def vt(ins):
                    cv.tick(ins)
                    v.wait_ge(cv.sem, cv.v)

                for kt in range(NT):
                    sl = slice(kt * H, (kt + 1) * H)
                    vt(v.scalar_tensor_tensor(out=shar[:, sl], in0=xpg[:, sl],
                                              scalar=gc(kt, 5), in1=b_sNg[:],
                                              op0=ALU.mult, op1=ALU.mult))
                    vt(v.reduce_max(gc(kt, 13), shar[:, sl], axis=AX.X,
                                    apply_absolute_value=True))
                    vt(v.tensor_scalar(out=gc(kt, 13), in0=gc(kt, 13), scalar1=1e-5,
                                       scalar2=None, op0=ALU.max))
                    vt(v.reciprocal(gc(kt, 14), gc(kt, 13)))
                    vt(v.tensor_scalar(out=gc(kt, 14), in0=gc(kt, 14), scalar1=127.0,
                                       scalar2=None, op0=ALU.mult))
                    vt(v.tensor_scalar(out=gc(kt, 13), in0=gc(kt, 13), scalar1=1.0 / 127.0,
                                       scalar2=None, op0=ALU.mult))

            @blk.scalar
            def _(a):
                join(a, cv)
                for kt in range(NT):
                    sl = slice(kt * H, (kt + 1) * H)
                    ca.tick(a.activation(out=i32g[:, sl], in_=shar[:, sl],
                                         func=AF.Identity, scale=gc(kt, 14)))

            @blk.vector
            def _(v):
                join(v, ca)
                cv.tick(v.tensor_copy(out=shar[:, 0:NT * H], in_=i32g[:, 0:NT * H]))

        transpose_set(lambda i: shar[:, i * 128:(i + 1) * 128],
                      lambda i: k1T[:, i * 128:(i + 1) * 128], NT * HC, 128)

        for oc in range(4):
            with nc.Block() as blk:
                @blk.sync
                def _(sync):
                    join(sync)
                    for c in range(HC):
                        cl.tick(sync.dma_start(
                            out=tslab[:, c * 512:(c + 1) * 512],
                            in_=s_qsWg[c * 128:(c + 1) * 128, oc * 512:(oc + 1) * 512]))

                @blk.tensor
                def _(t):
                    join(t, cl, ca)
                    for kt in range(NT):
                        ins = None
                        for c in range(HC):
                            ins = t.matmul(PS[kt][0:128, 0:512],
                                           k1T[:, (kt * HC + c) * 128:
                                               (kt * HC + c) * 128 + 128],
                                           tslab[:, c * 512:(c + 1) * 512],
                                           start=(c == 0), stop=(c == HC - 1))
                        ct.tick(ins)

                @blk.vector
                def _(v):
                    join(v, ct)
                    sw = mean_slot["sWg"]
                    for kt in range(NT):
                        v.wait_ge(cv.sem, cv.v)
                        cv.tick(v.tensor_mul(gc(kt, 15), gc(kt, 13),
                                             b_wm[:, sw + 1:sw + 2]))
                        v.wait_ge(cv.sem, cv.v)
                        cv.tick(v.tensor_scalar(
                            out=shy[:, kt * H + oc * 512:kt * H + (oc + 1) * 512],
                            in0=PS[kt][0:128, 0:512], scalar1=gc(kt, 15),
                            scalar2=None, op0=ALU.mult))

        with nc.Block() as blk:
            @blk.scalar
            def _(a):
                join(a, cv)
                for kt in range(NT):
                    ca.tick(a.activation(out=shar[:, kt * H:kt * H + IA],
                                         in_=shy[:, kt * H:kt * H + IA], func=AF.Silu))

            @blk.vector
            def _(v):
                join(v, ca)
                for kt in range(NT):
                    v.wait_ge(cv.sem, cv.v)
                    cv.tick(v.tensor_mul(shh[:, kt * IA:(kt + 1) * IA],
                                         shar[:, kt * H:kt * H + IA],
                                         shy[:, kt * H + IA:kt * H + 2 * IA]))

            @blk.scalar
            def _(a):
                join(a, cv)
                for kt in range(NT):
                    ca.tick(a.activation(out=shar[:, kt * H:kt * H + IA],
                                         in_=shh[:, kt * IA:(kt + 1) * IA],
                                         func=AF.Square, accum_out=gc(kt, 0)))

            @blk.vector
            def _(v):
                join(v, ca)
                for kt in range(NT):
                    v.wait_ge(cv.sem, cv.v)
                    cv.tick(v.tensor_scalar(out=gc(kt, 0), in0=gc(kt, 0), scalar1=1.0 / IA,
                                            scalar2=1e-8, op0=ALU.mult, op1=ALU.add))

            @blk.scalar
            def _(a):
                join(a, cv)
                for kt in range(NT):
                    ca.tick(a.activation(out=gc(kt, 1), in_=gc(kt, 0), func=AF.Sqrt))

            @blk.vector
            def _(v):
                join(v, ca)

                def vt(ins):
                    cv.tick(ins)
                    v.wait_ge(cv.sem, cv.v)

                for kt in range(NT):
                    vt(v.reciprocal(gc(kt, 1), gc(kt, 1)))
                    vt(v.scalar_tensor_tensor(out=shar[:, kt * H:kt * H + IA],
                                              in0=shh[:, kt * IA:(kt + 1) * IA],
                                              scalar=gc(kt, 1), in1=b_sNd[:],
                                              op0=ALU.mult, op1=ALU.mult))
                    vt(v.reduce_max(gc(kt, 2), shar[:, kt * H:kt * H + IA], axis=AX.X,
                                    apply_absolute_value=True))
                    vt(v.tensor_scalar(out=gc(kt, 2), in0=gc(kt, 2), scalar1=1e-5,
                                       scalar2=None, op0=ALU.max))
                    vt(v.reciprocal(gc(kt, 3), gc(kt, 2)))
                    vt(v.tensor_scalar(out=gc(kt, 3), in0=gc(kt, 3), scalar1=127.0,
                                       scalar2=None, op0=ALU.mult))
                    vt(v.tensor_scalar(out=gc(kt, 2), in0=gc(kt, 2), scalar1=1.0 / 127.0,
                                       scalar2=None, op0=ALU.mult))

            @blk.scalar
            def _(a):
                join(a, cv)
                for kt in range(NT):
                    ca.tick(a.activation(out=i32g[:, kt * IA:(kt + 1) * IA],
                                         in_=shar[:, kt * H:kt * H + IA],
                                         func=AF.Identity, scale=gc(kt, 3)))

            @blk.vector
            def _(v):
                join(v, ca)
                cv.tick(v.tensor_copy(out=shar[:, 0:NT * IA], in_=i32g[:, 0:NT * IA]))

        transpose_set(lambda i: shar[:, i * 128:(i + 1) * 128],
                      lambda i: k2T[:, i * 128:(i + 1) * 128], NT * IAC, 128)

        with nc.Block() as blk:
            @blk.vector
            def _(v):
                join(v)
                for kt in range(NT):
                    sl = slice(kt * H, (kt + 1) * H)
                    v.wait_ge(cv.sem, cv.v)
                    cv.tick(v.tensor_mul(shy[:, sl], xpg[:, sl], b_Wsg[:]))
                    v.wait_ge(cv.sem, cv.v)
                    cv.tick(v.reduce_sum(gc(kt, 4), shy[:, sl], axis=AX.X))

            @blk.scalar
            def _(a):
                join(a, cv)
                for kt in range(NT):
                    ca.tick(a.activation(out=gc(kt, 4), in_=gc(kt, 4), func=AF.Sigmoid))

        for oc in range(4):
            with nc.Block() as blk:
                @blk.sync
                def _(sync):
                    join(sync)
                    for c in range(IAC):
                        cl.tick(sync.dma_start(
                            out=tslab[:, c * 512:(c + 1) * 512],
                            in_=s_qsWd[c * 128:(c + 1) * 128, oc * 512:(oc + 1) * 512]))

                @blk.tensor
                def _(t):
                    join(t, cl, ca)
                    for kt in range(NT):
                        ins = None
                        for c in range(IAC):
                            ins = t.matmul(PS[kt][0:128, 0:512],
                                           k2T[:, (kt * IAC + c) * 128:
                                               (kt * IAC + c) * 128 + 128],
                                           tslab[:, c * 512:(c + 1) * 512],
                                           start=(c == 0), stop=(c == IAC - 1))
                        ct.tick(ins)

                @blk.vector
                def _(v):
                    join(v, ct)
                    sw = mean_slot["sWd"]
                    for kt in range(NT):
                        v.wait_ge(cv.sem, cv.v)
                        cv.tick(v.tensor_mul(gc(kt, 15), gc(kt, 2), b_wm[:, sw + 1:sw + 2]))
                        v.wait_ge(cv.sem, cv.v)
                        cv.tick(v.tensor_scalar(out=gch[:, kt * 512:(kt + 1) * 512],
                                                in0=PS[kt][0:128, 0:512],
                                                scalar1=gc(kt, 15), scalar2=None,
                                                op0=ALU.mult))
                        v.wait_ge(cv.sem, cv.v)
                        cv.tick(v.tensor_mul(gch[:, kt * 512:(kt + 1) * 512],
                                             gch[:, kt * 512:(kt + 1) * 512],
                                             gc(kt, 4).to_broadcast([128, 512])))
                        v.wait_ge(cv.sem, cv.v)
                        sl = slice(kt * H + oc * 512, kt * H + (oc + 1) * 512)
                        cv.tick(v.tensor_add(accg[:, sl], accg[:, sl],
                                             gch[:, kt * 512:(kt + 1) * 512]))

        with nc.Block() as blk:
            @blk.sync
            def _(sync):
                join(sync)
                cl.tick(sync.dma_start(out=s_acc[r0:r0 + GT, :]
                                       .rearrange("(k p) h -> p k h", p=128),
                                       in_=accg[:].rearrange("p (k h) -> p k h", k=NT)))

    if "xp" in DBG:
        with nc.Block() as blk:
            @blk.sync
            def _(sync):
                join(sync)
                cl.tick(sync.dma_start(out=dbg["dbg_xp"][:], in_=s_xp[:]))

    p4.close()

    if KPH < 4:
        root.close()
        return nc
    p5 = ExitStack()
    sb5 = lambda n, s, d: p5.enter_context(nc.sbuf_tensor(n, s, d))
    qWgE = sb5("qWgE", [128, HC * 2048], FP8)
    qWdE = sb5("qWdE", [128, IAC * 2048], FP8)
    ewch0 = sb5("ewch0", [128, 2048], F32)
    ewch1 = sb5("ewch1", [128, 2048], F32)
    ewchD = [ewch0, ewch1]
    ei32 = sb5("ei32", [128, 2 * 2048], I32)
    b_NgE = sb5("b_NgE", [128, H], F32)
    b_NdE = sb5("b_NdE", [128, IA], F32)
    xg = sb5("xg", [128, 2 * H], F32)
    exn = sb5("exn", [128, 2 * H], F32)
    ekT = sb5("ekT", [128, 2 * H], BF16)
    ek2T = sb5("ek2T", [128, 2 * IA], BF16)
    eyb = sb5("eyb", [128, 2 * 2 * IA], F32)
    ehh = sb5("ehh", [128, 2 * IA], F32)
    ecol = sb5("ecol", [128, 16], F32)

    NTILE = CAP // 128
    VP = 2 + 7 + 1 + 5 + 1 + 1 + 7 + 1 + 5
    APc = 2 + 1 + 1 + 16 + 1 + 1 + 1 + 1 + 8
    TP = 16 + 4 + 8 + 4
    LP = 32

    for e in range(E):
        bcast_from_dram(b_NgE, d_Ng[e:e + 1, :], H)
        bcast_from_dram(b_NdE, d_Nd[e:e + 1, :], IA)

        for (dsrc, nch, qdst, key) in ((d_Wg, HC, qWgE, f"Wg{e}"),
                                       (d_Wd, IAC, qWdE, f"Wd{e}")):
            sidx = mean_slot[key]
            with nc.Block() as blk:
                base_l = cl.v
                base_a = ca.v
                base_v = cv.v

                @blk.sync
                def _(sync):
                    join(sync)
                    for c in range(nch):
                        if c >= 2:
                            sync.wait_ge(ca.sem, base_a + c - 1)
                        cl.tick(sync.dma_start(out=ewch[:, (c % 2) * 1024:(c % 2) * 1024 + 2048]
                                               .bitcast(F32) if False else
                                               ewchD[c % 2][:],
                                               in_=dsrc[e, c * 128:(c + 1) * 128, :]))

                @blk.scalar
                def _(a):
                    for c in range(nch):
                        a.wait_ge(cl.sem, base_l + (c + 1) * 16)
                        if c >= 2:
                            a.wait_ge(cv.sem, base_v + c - 1)
                        ca.tick(a.activation(out=ei32[:, (c % 2) * 2048:(c % 2) * 2048 + 2048],
                                             in_=ewchD[c % 2][:], func=AF.Identity,
                                             scale=b_wm[:, sidx:sidx + 1]))

                @blk.vector
                def _(v):
                    for c in range(nch):
                        v.wait_ge(ca.sem, base_a + c + 1)
                        cv.tick(v.tensor_scalar(out=qdst[:, c * 2048:(c + 1) * 2048],
                                                in0=ei32[:, (c % 2) * 2048:(c % 2) * 2048 + 2048],
                                                scalar1=-1.0, scalar2=1.0,
                                                op0=ALU.max, op1=ALU.min))

        sWg_i = mean_slot[f"Wg{e}"]
        sWd_i = mean_slot[f"Wd{e}"]

        with nc.Block() as blk:
            base_l = cl.v
            base_v = cv.v
            base_a = ca.v
            base_t = ct.v

            def P(i, k):
                return slice((i % 2) * k, (i % 2) * k + k)

            def ec(i, j):
                return ecol[:, (i % 2) * 8 + j:(i % 2) * 8 + j + 1]

            @blk.sync
            def _(sync):
                join(sync)
                for i in range(NTILE):
                    if i >= 2:
                        sync.wait_ge(cv.sem, base_v + (i - 2) * VP + 4)
                    cl.tick(sync.dma_start(out=xg[:, P(i, H)],
                                           in_=s_xg[e * CAP + i * 128:
                                                    e * CAP + (i + 1) * 128, :]))
                    if i >= 1:
                        sync.wait_ge(cv.sem, base_v + i * VP)
                        cl.tick(sync.dma_start(
                            out=s_moe[e * CAP + (i - 1) * 128:e * CAP + i * 128, :],
                            in_=exn[:, P(i - 1, H)]))
                sync.wait_ge(cv.sem, base_v + NTILE * VP)
                cl.tick(sync.dma_start(
                    out=s_moe[e * CAP + (NTILE - 1) * 128:e * CAP + NTILE * 128, :],
                    in_=exn[:, P(NTILE - 1, H)]))

            @blk.scalar
            def _(a):
                for i in range(NTILE):
                    a.wait_ge(cl.sem, base_l + i * LP + 16)
                    if i >= 2:
                        a.wait_ge(cv.sem, base_v + (i - 1) * VP)
                    ca.tick(a.activation(out=ehh[:, P(i, IA)].bitcast(F32),
                                         in_=xg[:, (i % 2) * H:(i % 2) * H + IA],
                                         func=AF.Square, accum_out=ec(i, 0)))
                    ca.tick(a.activation(out=ehh[:, P(i, IA)].bitcast(F32),
                                         in_=xg[:, (i % 2) * H + IA:(i % 2) * H + 2 * IA],
                                         func=AF.Square, accum_out=ec(i, 1)))
                    a.wait_ge(cv.sem, base_v + i * VP + 2)
                    ca.tick(a.activation(out=ec(i, 2), in_=ec(i, 2), func=AF.Sqrt))
                    a.wait_ge(cv.sem, base_v + i * VP + 9)
                    ca.tick(a.activation(out=ei32[:, P(i, 2048)], in_=exn[:, P(i, H)],
                                         func=AF.Identity, scale=ec(i, 5)))
                    for c in range(HC):
                        a.wait_ge(ct.sem, base_t + i * TP + c + 1)
                        ca.tick(a.activation(out=ekT[:, (i % 2) * H + c * 128:
                                                      (i % 2) * H + (c + 1) * 128],
                                             in_=PS[4 + c % 2][0:128, 0:128],
                                             func=AF.Identity))
                    a.wait_ge(cv.sem, base_v + i * VP + 15)
                    ca.tick(a.activation(out=exn[:, (i % 2) * H:(i % 2) * H + IA],
                                         in_=eyb[:, (i % 2) * 2048:(i % 2) * 2048 + IA],
                                         func=AF.Silu))
                    a.wait_ge(cv.sem, base_v + i * VP + 16)
                    ca.tick(a.activation(out=exn[:, (i % 2) * H + IA:(i % 2) * H + 2 * IA]
                                         .bitcast(F32),
                                         in_=ehh[:, P(i, IA)], func=AF.Square,
                                         accum_out=ec(i, 0)))
                    a.wait_ge(cv.sem, base_v + i * VP + 17)
                    ca.tick(a.activation(out=ec(i, 2), in_=ec(i, 2), func=AF.Sqrt))
                    a.wait_ge(cv.sem, base_v + i * VP + 24)
                    ca.tick(a.activation(out=ei32[:, (i % 2) * 2048:(i % 2) * 2048 + IA],
                                         in_=exn[:, (i % 2) * H:(i % 2) * H + IA],
                                         func=AF.Identity, scale=ec(i, 5)))
                    for c in range(IAC):
                        a.wait_ge(ct.sem, base_t + i * TP + 16 + 4 + c + 1)
                        ca.tick(a.activation(out=ek2T[:, (i % 2) * IA + c * 128:
                                                       (i % 2) * IA + (c + 1) * 128],
                                             in_=PS[4 + c % 2][0:128, 0:128],
                                             func=AF.Identity))

            @blk.vector
            def _(v):
                for i in range(NTILE):
                    v.wait_ge(ca.sem, base_a + i * APc + 2)
                    cv.tick(v.scalar_tensor_tensor(out=ec(i, 2), in0=ec(i, 0),
                                                   scalar=1.0, in1=ec(i, 1),
                                                   op0=ALU.mult, op1=ALU.add))
                    v.wait_ge(cv.sem, cv.v)
                    cv.tick(v.tensor_scalar(out=ec(i, 2), in0=ec(i, 2),
                                            scalar1=1.0 / H, scalar2=1e-8,
                                            op0=ALU.mult, op1=ALU.add))
                    v.wait_ge(ca.sem, base_a + i * APc + 3)
                    cv.tick(v.reciprocal(ec(i, 3), ec(i, 2)))
                    v.wait_ge(cv.sem, cv.v)
                    cv.tick(v.scalar_tensor_tensor(out=exn[:, P(i, H)],
                                                   in0=xg[:, P(i, H)],
                                                   scalar=ec(i, 3), in1=b_NgE[:],
                                                   op0=ALU.mult, op1=ALU.mult))
                    v.wait_ge(cv.sem, cv.v)
                    cv.tick(v.reduce_max(ec(i, 4), exn[:, P(i, H)], axis=AX.X,
                                         apply_absolute_value=True))
                    v.wait_ge(cv.sem, cv.v)
                    cv.tick(v.tensor_scalar(out=ec(i, 4), in0=ec(i, 4), scalar1=1e-5,
                                            scalar2=None, op0=ALU.max))
                    v.wait_ge(cv.sem, cv.v)
                    cv.tick(v.reciprocal(ec(i, 5), ec(i, 4)))
                    v.wait_ge(cv.sem, cv.v)
                    cv.tick(v.tensor_scalar(out=ec(i, 5), in0=ec(i, 5), scalar1=127.0,
                                            scalar2=None, op0=ALU.mult))
                    v.wait_ge(cv.sem, cv.v)
                    cv.tick(v.tensor_scalar(out=ec(i, 4), in0=ec(i, 4),
                                            scalar1=1.0 / 127.0, scalar2=None,
                                            op0=ALU.mult))
                    v.wait_ge(ca.sem, base_a + i * APc + 4)
                    cv.tick(v.tensor_copy(out=exn[:, P(i, H)], in_=ei32[:, P(i, 2048)]))
                    v.wait_ge(cv.sem, cv.v)
                    cv.tick(v.tensor_mul(ec(i, 6), ec(i, 4), b_wm[:, sWg_i + 1:sWg_i + 2]))
                    for oc in range(4):
                        v.wait_ge(ct.sem, base_t + i * TP + 16 + oc + 1)
                        cv.tick(v.tensor_scalar(
                            out=eyb[:, (i % 2) * 2048 + oc * 512:
                                    (i % 2) * 2048 + (oc + 1) * 512],
                            in0=PS[oc][0:128, 0:512], scalar1=ec(i, 6),
                            scalar2=None, op0=ALU.mult))
                    v.wait_ge(ca.sem, base_a + i * APc + 21)
                    cv.tick(v.tensor_mul(ehh[:, P(i, IA)],
                                         exn[:, (i % 2) * H:(i % 2) * H + IA],
                                         eyb[:, (i % 2) * 2048 + IA:
                                             (i % 2) * 2048 + 2 * IA]))
                    v.wait_ge(ca.sem, base_a + i * APc + 22)
                    cv.tick(v.tensor_scalar(out=ec(i, 2), in0=ec(i, 0),
                                            scalar1=1.0 / IA, scalar2=1e-8,
                                            op0=ALU.mult, op1=ALU.add))
                    v.wait_ge(ca.sem, base_a + i * APc + 23)
                    cv.tick(v.reciprocal(ec(i, 3), ec(i, 2)))
                    v.wait_ge(cv.sem, cv.v)
                    cv.tick(v.scalar_tensor_tensor(out=exn[:, (i % 2) * H:
                                                           (i % 2) * H + IA],
                                                   in0=ehh[:, P(i, IA)],
                                                   scalar=ec(i, 3), in1=b_NdE[:],
                                                   op0=ALU.mult, op1=ALU.mult))
                    v.wait_ge(cv.sem, cv.v)
                    cv.tick(v.reduce_max(ec(i, 4), exn[:, (i % 2) * H:(i % 2) * H + IA],
                                         axis=AX.X, apply_absolute_value=True))
                    v.wait_ge(cv.sem, cv.v)
                    cv.tick(v.tensor_scalar(out=ec(i, 4), in0=ec(i, 4), scalar1=1e-5,
                                            scalar2=None, op0=ALU.max))
                    v.wait_ge(cv.sem, cv.v)
                    cv.tick(v.reciprocal(ec(i, 5), ec(i, 4)))
                    v.wait_ge(cv.sem, cv.v)
                    cv.tick(v.tensor_scalar(out=ec(i, 5), in0=ec(i, 5), scalar1=127.0,
                                            scalar2=None, op0=ALU.mult))
                    v.wait_ge(cv.sem, cv.v)
                    cv.tick(v.tensor_scalar(out=ec(i, 4), in0=ec(i, 4),
                                            scalar1=1.0 / 127.0, scalar2=None,
                                            op0=ALU.mult))
                    v.wait_ge(ca.sem, base_a + i * APc + 24)
                    cv.tick(v.tensor_copy(out=exn[:, (i % 2) * H:(i % 2) * H + IA],
                                          in_=ei32[:, (i % 2) * 2048:
                                                   (i % 2) * 2048 + IA]))
                    v.wait_ge(cv.sem, cv.v)
                    cv.tick(v.tensor_mul(ec(i, 6), ec(i, 4), b_wm[:, sWd_i + 1:sWd_i + 2]))
                    for oc in range(4):
                        v.wait_ge(ct.sem, base_t + i * TP + 16 + 4 + 8 + oc + 1)
                        cv.tick(v.tensor_scalar(out=exn[:, (i % 2) * H + oc * 512:
                                                        (i % 2) * H + (oc + 1) * 512],
                                                in0=PS[oc][0:128, 0:512],
                                                scalar1=ec(i, 6), scalar2=None,
                                                op0=ALU.mult))

            @blk.tensor
            def _(t):
                t.wait_ge(cv.sem, base_v)
                t.wait_ge(ca.sem, base_a)
                t.wait_ge(cl.sem, base_l)
                for i in range(NTILE):
                    for c in range(HC):
                        t.wait_ge(cv.sem, base_v + i * VP + 10)
                        if c >= 2:
                            t.wait_ge(ca.sem, base_a + i * APc + 4 + c - 1)
                        elif i >= 1:
                            t.wait_ge(ca.sem, base_a + i * APc)
                        ct.tick(t.transpose(PS[4 + c % 2][0:128, 0:128],
                                            exn[:, (i % 2) * H + c * 128:
                                                (i % 2) * H + (c + 1) * 128],
                                            identf[:]))
                    t.wait_ge(ca.sem, base_a + i * APc + 20)
                    if i >= 1:
                        t.wait_ge(cv.sem, base_v + i * VP)
                    for oc in range(4):
                        ins = None
                        for c in range(HC):
                            ins = t.matmul(PS[oc][0:128, 0:512],
                                           ekT[:, (i % 2) * H + c * 128:
                                               (i % 2) * H + (c + 1) * 128],
                                           qWgE[:, c * 2048 + oc * 512:
                                                c * 2048 + (oc + 1) * 512],
                                           start=(c == 0), stop=(c == HC - 1))
                        ct.tick(ins)
                    for c in range(IAC):
                        t.wait_ge(cv.sem, base_v + i * VP + 25)
                        if c >= 2:
                            t.wait_ge(ca.sem, base_a + i * APc + 24 + c - 1)
                        else:
                            t.wait_ge(ca.sem, base_a + i * APc + 20)
                        ct.tick(t.transpose(PS[4 + c % 2][0:128, 0:128],
                                            exn[:, (i % 2) * H + c * 128:
                                                (i % 2) * H + (c + 1) * 128],
                                            identf[:]))
                    t.wait_ge(ca.sem, base_a + i * APc + APc)
                    t.wait_ge(cv.sem, base_v + i * VP + 17)
                    for oc in range(4):
                        ins = None
                        for c in range(IAC):
                            ins = t.matmul(PS[oc][0:128, 0:512],
                                           ek2T[:, (i % 2) * IA + c * 128:
                                                (i % 2) * IA + (c + 1) * 128],
                                           qWdE[:, c * 2048 + oc * 512:
                                                c * 2048 + (oc + 1) * 512],
                                           start=(c == 0), stop=(c == IAC - 1))
                        ct.tick(ins)

    p5.close()

    if KPH < 5:
        root.close()
        return nc
    p6 = ExitStack()
    sb6 = lambda n, s, d: p6.enter_context(nc.sbuf_tensor(n, s, d))
    acc2 = sb6("acc2", [128, H], F32)
    B1 = sb6("B1", [128, H], F32)
    B2 = sb6("B2", [128, H], F32)
    fidx = sb6("fidx", [128, 2], U32)

    with nc.Block() as blk:
        @blk.vector
        def _(v):
            join(v)
            cv.tick(v.memset(B1[:], 0.0))
            cv.tick(v.memset(B2[:], 0.0))

    for k in range(8):
        with nc.Block() as blk:
            @blk.gpsimd
            def _(gp):
                join(gp)
                cp.tick(gp.tensor_copy(out=fidx[:, 0:1], in_=rt_mine[:, k, 4:5]))
                cp.tick(gp.tensor_copy(out=fidx[:, 1:2], in_=rt_mine[:, k, 5:6]))
                gp.wait_ge(cp.sem, cp.v)
                cg.tick(gp.indirect_dma_start(
                    out=B1[:], out_offset=None, in_=s_moe[:],
                    in_offset=IndirectOffsetOnAxis(ap=fidx[:, 0:1], axis=0),
                    bounds_check=T - 1, oob_is_err=False))
                cg.tick(gp.indirect_dma_start(
                    out=B2[:], out_offset=None, in_=s_moe[:],
                    in_offset=IndirectOffsetOnAxis(ap=fidx[:, 1:2], axis=0),
                    bounds_check=T - 1, oob_is_err=False))

            @blk.sync
            def _(sync):
                join(sync)
                cl.tick(sync.dma_start(out=acc2[:], in_=s_acc[k * 128:(k + 1) * 128, :]))

            @blk.vector
            def _(v):
                join(v, cg, cl)
                v.wait_ge(cv.sem, cv.v)
                cv.tick(v.scalar_tensor_tensor(out=acc2[:], in0=B1[:],
                                               scalar=rt_mine[:, k, 2:3], in1=acc2[:],
                                               op0=ALU.mult, op1=ALU.add))
                v.wait_ge(cv.sem, cv.v)
                cv.tick(v.scalar_tensor_tensor(out=acc2[:], in0=B2[:],
                                               scalar=rt_mine[:, k, 3:4], in1=acc2[:],
                                               op0=ALU.mult, op1=ALU.add))

            @blk.sync
            def _(sync):
                join(sync, cv)
                cl.tick(sync.dma_start(out=d_out[k * 128:(k + 1) * 128, :], in_=acc2[:]))

    if "acc" in DBG:
        with nc.Block() as blk:
            @blk.sync
            def _(sync):
                join(sync)
                cl.tick(sync.dma_start(out=dbg["dbg_acc"][:], in_=s_acc[:]))

    p6.close()

    with nc.Block() as blk:
        @blk.sync
        def _(sync):
            join(sync)

    root.close()
    return nc


_NC_CACHE = None
_LAST_RES = None


def _dn_idx(inputs):
    return np.asarray(inputs["L0_ffn_down_reorder_index"]).astype(np.int64)


def kernel(**inputs):
    global _NC_CACHE
    xf = np.ascontiguousarray(inputs["L0_attn_x"].reshape(T, H).astype(np.float32))
    xfT = np.ascontiguousarray(xf.T)

    if _NC_CACHE is None:
        _NC_CACHE = build_program()
    nc = _NC_CACHE

    common = {
        "xfT": xfT,
        "Wg": np.ascontiguousarray(inputs["L0_attn_Wg"], dtype=np.float32),
        "Wd": np.ascontiguousarray(inputs["L0_attn_Wd"], dtype=np.float32),
        "Ng": np.ascontiguousarray(inputs["L0_attn_Ng"], dtype=np.float32),
        "Nd": np.ascontiguousarray(inputs["L0_attn_Nd"], dtype=np.float32),
        "sWg": np.ascontiguousarray(inputs["L0_attn_sWg"], dtype=np.float32),
        "sWd": np.ascontiguousarray(inputs["L0_attn_sWd"], dtype=np.float32),
        "sNg": np.ascontiguousarray(inputs["L0_attn_sNg"], dtype=np.float32).reshape(1, H),
        "sNd": np.ascontiguousarray(inputs["L0_attn_sNd"], dtype=np.float32).reshape(1, IA),
        "gnw": np.ascontiguousarray(inputs["L0_attn_gate_norm_w"],
                                    dtype=np.float32).reshape(1, H),
        "Wgate": np.ascontiguousarray(inputs["L0_attn_W_gate"], dtype=np.float32),
        "Wsg": np.ascontiguousarray(inputs["L0_attn_W_shared_gate"],
                                    dtype=np.float32).reshape(1, H),
        "wgate_f": np.ascontiguousarray(inputs["L0_ffn_w_gate"][_dn_idx(inputs)], dtype=np.float32),
        "wup_f": np.ascontiguousarray(inputs["L0_ffn_w_up"][_dn_idx(inputs)], dtype=np.float32),
        "wdown_f": np.ascontiguousarray(inputs["L0_ffn_w_down"], dtype=np.float32),
    }
    in_maps = []
    for c in range(NCORE):
        m = dict(common)
        m["xsh"] = np.ascontiguousarray(xf[c * TPC:(c + 1) * TPC, :])
        tglob = c * TPC + np.arange(TPC)
        gi = ((tglob % 128) * 64 + (tglob // 128)).astype(np.int32)
        m["gidx"] = np.ascontiguousarray(gi.reshape(8, 128).T)
        in_maps.append(m)

    res = run_bass_kernel_spmd(nc, in_maps, core_ids=list(range(NCORE)))
    global _LAST_RES
    _LAST_RES = res
    out = np.concatenate([np.asarray(res.results[c]["out"]) for c in range(NCORE)], axis=0)
    return out.reshape(inputs["L0_attn_x"].shape).astype(np.float32)


# version 05
# speedup vs baseline: 1.3256x; 1.0375x over previous
import os
import numpy as np
from contextlib import ExitStack

import concourse.bass as bass
import concourse.mybir as mybir
from concourse.bass import IndirectOffsetOnAxis
from concourse.bass_utils import run_bass_kernel_spmd

F32 = mybir.dt.float32
BF16 = mybir.dt.bfloat16
FP8 = mybir.dt.float8e4
I32 = mybir.dt.int32
U32 = mybir.dt.uint32
AF = mybir.ActivationFunctionType
ALU = mybir.AluOpType
AX = mybir.AxisListType

NCORE = 8
T = 8192
TPC = 1024
H = 2048
E = 8
CAP = 1024
IA = 1024
IF = 5632
GT = 256
NG = TPC // GT
NT = GT // 128
HC = H // 128
IFC = IF // 128
IAC = IA // 128
DBG = os.environ.get("KDBG", "")
KPH = int(os.environ.get("KPH", "5"))
KP2 = int(os.environ.get("KP2", "9"))
KQT = int(os.environ.get("KQT", "9"))


class Ctr:
    def __init__(self, sem, step=1):
        self.sem, self.v, self.step = sem, 0, step

    def tick(self, instr, n=None):
        n = self.step if n is None else n
        instr.then_inc(self.sem, n)
        self.v += n
        return self.v


def build_program():
    nc = bass.Bass("TRN2", target_bir_lowering=False)
    root = ExitStack()

    cl = Ctr(root.enter_context(nc.semaphore("c_ld")), 16)
    cs = Ctr(root.enter_context(nc.semaphore("c_lds")), 16)
    cg = Ctr(root.enter_context(nc.semaphore("c_ldg")), 16)
    cv = Ctr(root.enter_context(nc.semaphore("c_v")), 1)
    ca = Ctr(root.enter_context(nc.semaphore("c_a")), 1)
    ct = Ctr(root.enter_context(nc.semaphore("c_t")), 1)
    cp = Ctr(root.enter_context(nc.semaphore("c_g")), 1)
    ALLC = [cl, cs, cg, cv, ca, ct, cp]

    def join(eng, *ctrs):
        for c in (ctrs or ALLC):
            if c.v:
                eng.wait_ge(c.sem, c.v)

    d_xsh = nc.dram_tensor("xsh", [TPC, H], F32, kind="ExternalInput")
    d_xfT = nc.dram_tensor("xfT", [H, T], F32, kind="ExternalInput")
    d_gidx = nc.dram_tensor("gidx", [128, 8], I32, kind="ExternalInput")
    d_Wg = nc.dram_tensor("Wg", [E, H, 2 * IA], F32, kind="ExternalInput")
    d_Wd = nc.dram_tensor("Wd", [E, IA, H], F32, kind="ExternalInput")
    d_Ng = nc.dram_tensor("Ng", [E, H], F32, kind="ExternalInput")
    d_Nd = nc.dram_tensor("Nd", [E, IA], F32, kind="ExternalInput")
    d_sWg = nc.dram_tensor("sWg", [H, 2 * IA], F32, kind="ExternalInput")
    d_sWd = nc.dram_tensor("sWd", [IA, H], F32, kind="ExternalInput")
    d_sNg = nc.dram_tensor("sNg", [1, H], F32, kind="ExternalInput")
    d_sNd = nc.dram_tensor("sNd", [1, IA], F32, kind="ExternalInput")
    d_gnw = nc.dram_tensor("gnw", [1, H], F32, kind="ExternalInput")
    d_Wgate = nc.dram_tensor("Wgate", [H, E], F32, kind="ExternalInput")
    d_Wsg = nc.dram_tensor("Wsg", [1, H], F32, kind="ExternalInput")
    d_wg = nc.dram_tensor("wgate_f", [IF, H], F32, kind="ExternalInput")
    d_wu = nc.dram_tensor("wup_f", [IF, H], F32, kind="ExternalInput")
    d_wd = nc.dram_tensor("wdown_f", [H, IF], F32, kind="ExternalInput")
    d_out = nc.dram_tensor("out", [TPC, H], F32, kind="ExternalOutput")

    s_xp = nc.dram_tensor("s_xp", [TPC, H], F32, kind="Internal")
    s_rtab = nc.dram_tensor("s_rtab", [T, 8], F32, kind="Internal")
    s_xg = nc.dram_tensor("s_xg", [E * CAP, H], F32, kind="Internal")
    s_moe = nc.dram_tensor("s_moe", [E * CAP, H], F32, kind="Internal")
    s_acc = nc.dram_tensor("s_acc", [TPC, H], F32, kind="Internal")
    s_qwgT = nc.dram_tensor("s_qwgT", [HC, IFC // 4, 128, 512], FP8, kind="Internal")
    s_qwuT = nc.dram_tensor("s_qwuT", [HC, IFC // 4, 128, 512], FP8, kind="Internal")
    s_qwdT = nc.dram_tensor("s_qwdT", [IFC, HC // 4, 128, 512], FP8, kind="Internal")
    s_qsWg = nc.dram_tensor("s_qsWg", [H, 2 * IA], FP8, kind="Internal")
    s_qsWd = nc.dram_tensor("s_qsWd", [IA, H], FP8, kind="Internal")
    s_wgs = nc.dram_tensor("s_wgs", [1, IF], F32, kind="Internal")
    s_wus = nc.dram_tensor("s_wus", [1, IF], F32, kind="Internal")
    s_wds = nc.dram_tensor("s_wds", [1, H], F32, kind="Internal")

    dbg = {}
    if "rt" in DBG:
        dbg["dbg_rt"] = nc.dram_tensor("dbg_rt", [TPC, 8], F32, kind="ExternalOutput")
    if "xp" in DBG:
        dbg["dbg_xp"] = nc.dram_tensor("dbg_xp", [TPC, H], F32, kind="ExternalOutput")
    if "acc" in DBG:
        dbg["dbg_acc"] = nc.dram_tensor("dbg_acc", [TPC, H], F32, kind="ExternalOutput")


    sbr = lambda n, s, d: root.enter_context(nc.sbuf_tensor(n, s, d))
    identf = sbr("identf", [128, 128], F32)
    ones_c = sbr("ones_c", [128, 1], F32)
    ones_cb = sbr("ones_cb", [128, 1], BF16)
    ones_r = sbr("ones_r", [1, 128], F32)
    iotaP = sbr("iotaP", [128, 1], I32)
    iotaPf = sbr("iotaPf", [128, 1], F32)
    iotaF = sbr("iotaF", [128, 128], I32)
    lidx = sbr("lidx", [128, 8], U32)
    stg_row = sbr("stg_row", [1, H], F32)
    b_wm = sbr("b_wm", [128, 64], F32)
    rt_mine = sbr("rt_mine", [128, 8, 8], F32)
    g_idx = sbr("g_idx", [128, 8], U32)
    u32c = sbr("u32c", [128, 2], U32)
    stgf = sbr("stgf", [128, 256], F32)
    colx = sbr("colx", [128, 16], F32)
    wrow = sbr("wrow", [1, 64], F32)
    wcl = sbr("wcl", [128, 4], F32)

    PS = [root.enter_context(nc.psum_tensor(f"psb{i}", [128, 512], F32)) for i in range(8)]

    with nc.Block() as blk:
        @blk.sync
        def _(sync):
            cl.tick(sync.dma_start(out=g_idx[:].bitcast(I32), in_=d_gidx[:]))

        @blk.gpsimd
        def _(gp):
            cp.tick(gp.iota(iotaP[:], pattern=[[0, 1]], base=0, channel_multiplier=1))
            cp.tick(gp.iota(iotaF[:], pattern=[[1, 128]], base=0, channel_multiplier=0))
            cp.tick(gp.iota(lidx[:].bitcast(I32), pattern=[[128, 8]], base=0,
                            channel_multiplier=1))

        @blk.vector
        def _(v):
            join(v, cp)
            cv.tick(v.tensor_copy(out=iotaPf[:], in_=iotaP[:]))
            v.wait_ge(cv.sem, cv.v)
            cv.tick(v.tensor_scalar(out=identf[:], in0=iotaF[:], scalar1=iotaPf[:, 0:1],
                                    scalar2=None, op0=ALU.is_equal))
            cv.tick(v.memset(ones_c[:], 1.0))
            cv.tick(v.memset(ones_cb[:], 1.0))
            cv.tick(v.memset(ones_r[:], 1.0))

    def bcast_from_dram(dst, dram_ap, n):
        with nc.Block() as blk:
            @blk.sync
            def _(sync):
                join(sync)
                cl.tick(sync.dma_start(out=stg_row[0:1, 0:n], in_=dram_ap))

            base_v = cv.v

            @blk.tensor
            def _(t):
                join(t, cl, cv)
                for i in range(0, n, 512):
                    w = min(512, n - i)
                    ch = i // 512
                    if ch >= 2:
                        t.wait_ge(cv.sem, base_v + ch - 1)
                    ct.tick(t.matmul(PS[ch % 2][:, 0:w], ones_r[:],
                                     stg_row[0:1, i:i + w], start=True, stop=True))

            base_t = ct.v

            @blk.vector
            def _(v):
                nch = (n + 511) // 512
                for i in range(0, n, 512):
                    w = min(512, n - i)
                    ch = i // 512
                    v.wait_ge(ct.sem, base_t - nch + ch + 1)
                    cv.tick(v.tensor_copy(out=dst[:, i:i + w], in_=PS[ch % 2][:, 0:w]))

    if KPH < 1:
        root.close()
        return nc
    p1o = ExitStack()
    sbo = lambda n, s, d: p1o.enter_context(nc.sbuf_tensor(n, s, d))
    rt = sbo("rt", [128, 64, 10], F32)
    iot8i = sbo("iot8i", [128, 64, 8], I32)
    iot8 = sbo("iot8", [128, 64, 8], F32)
    iot8p = sbo("iot8p", [128, 64, 8], F32)
    L3 = sbo("L3", [128, 64, 8], F32)
    P3 = sbo("P3", [128, 64, 8], F32)
    M3 = sbo("M3", [128, 64, 8], F32)
    oh1 = sbo("oh1", [128, 64, 8], F32)
    oh2 = sbo("oh2", [128, 64, 8], F32)
    K3 = sbo("K3", [128, 64, 8], F32)
    C3 = sbo("C3", [128, 64, 8], F32)
    sc = sbo("sc", [128, 64, 8], F32)
    rpk = sbo("rpk", [128, 64, 8], F32)

    def col(tile, k):
        return tile[:, :, k:k + 1]

    def colb(tile, k):
        return tile[:, :, k:k + 1].to_broadcast([128, 64, 8])

    p1a = ExitStack()
    sba = lambda n, s, d: p1a.enter_context(nc.sbuf_tensor(n, s, d))
    xp0 = sba("xp0", [128, 2048], F32)
    xp1 = sba("xp1", [128, 2048], F32)
    xpieces = [xp0, xp1]
    xsq = sba("xsq", [128, 2048], BF16)
    sums = sba("sums", [33, T], F32)
    wgp = sba("wgp", [128, HC * 9], F32)
    gnwF = sba("gnwF", [128, HC], F32)
    csr = sba("csr", [1, 8], F32)

    with nc.Block() as blk:
        @blk.sync
        def _(sync):
            join(sync)
            cl.tick(sync.dma_start(out=wgp[:].rearrange("p (c e) -> p c e", c=HC)[:, :, 0:8],
                                   in_=d_Wgate[:].rearrange("(c p) e -> p c e", p=128)))
            for c in range(HC):
                cl.tick(sync.dma_start(
                    out=gnwF[:, c:c + 1],
                    in_=d_gnw[0:1, c * 128:(c + 1) * 128]
                    .rearrange("o (p q) -> (o p) q", q=1)))

        @blk.gpsimd
        def _(gp):
            cp.tick(gp.iota(iot8i[:], pattern=[[0, 64], [1, 8]], base=0,
                            channel_multiplier=0))

        @blk.vector
        def _(v):
            join(v, cl, cp)
            wgp3 = wgp[:].rearrange("p (c e) -> p c e", c=HC)
            cv.tick(v.tensor_mul(wgp3[:, :, 0:8], wgp3[:, :, 0:8],
                                 gnwF[:].unsqueeze(2).to_broadcast([128, HC, 8])))
            cv.tick(v.memset(wgp3[:, :, 8:9], 1.0))
            cv.tick(v.memset(sums[:], 0.0))
            cv.tick(v.tensor_copy(out=iot8[:], in_=iot8i[:]))
            v.wait_ge(cv.sem, cv.v)
            cv.tick(v.tensor_scalar(out=iot8p[:], in0=iot8[:], scalar1=64.0,
                                    scalar2=None, op0=ALU.add))

    with nc.Block() as blk:
        @blk.tensor
        def _(t):
            join(t)
            ins = None
            for c in range(HC):
                ins = t.matmul(PS[0][0:1, 0:8], ones_c[:], wgp[:, c * 9:c * 9 + 8],
                               start=(c == 0), stop=(c == HC - 1))
            ct.tick(ins)

        @blk.vector
        def _(v):
            join(v, ct)
            cv.tick(v.tensor_copy(out=csr[:], in_=PS[0][0:1, 0:8]))

    with nc.Block() as blk:
        @blk.tensor
        def _(t):
            join(t)
            ct.tick(t.matmul(PS[1][:, 0:8], ones_r[:], csr[:], start=True, stop=True))

        @blk.vector
        def _(v):
            join(v, ct)
            cv.tick(v.tensor_copy(out=colx[:, 8:16], in_=PS[1][:, 0:8]))

    with nc.Block() as blk:
        NPC, NSL = 4, 4
        base_l = cl.v
        base_a = ca.v
        base_t = ct.v
        base_v = cv.v

        @blk.sync
        def _(sync):
            join(sync)
            n = 0
            for c in range(HC):
                for q in range(NPC):
                    if n >= 2:
                        sync.wait_ge(ct.sem, base_t + (n - 1) * 2 * NSL)
                        sync.wait_ge(ca.sem, base_a + n - 1)
                    cl.tick(sync.dma_start(out=xpieces[n % 2][:],
                                           in_=d_xfT[c * 128:(c + 1) * 128,
                                                     q * 2048:(q + 1) * 2048]))
                    n += 1

        @blk.scalar
        def _(a):
            n = 0
            for c in range(HC):
                for q in range(NPC):
                    a.wait_ge(cl.sem, base_l + (n + 1) * 16)
                    if n >= 1:
                        a.wait_ge(ct.sem, base_t + n * 2 * NSL)
                    ca.tick(a.activation(out=xsq[:], in_=xpieces[n % 2][:], func=AF.Square))
                    n += 1

        @blk.tensor
        def _(t):
            n = 0
            for c in range(HC):
                for q in range(NPC):
                    t.wait_ge(cl.sem, base_l + (n + 1) * 16)
                    t.wait_ge(ca.sem, base_a + n + 1)
                    for j in range(NSL):
                        gs = n * NSL + j
                        bk = (gs % 4) * 2
                        if gs >= 4:
                            t.wait_ge(cv.sem, base_v + (gs - 3) * 2)
                        ct.tick(t.matmul(PS[bk][0:9, 0:512], wgp[:, c * 9:(c + 1) * 9],
                                         xpieces[n % 2][:, j * 512:(j + 1) * 512],
                                         start=True, stop=True))
                        ct.tick(t.matmul(PS[bk + 1][0:1, 0:512], ones_cb[:],
                                         xsq[:, j * 512:(j + 1) * 512],
                                         start=True, stop=True))
                    n += 1

        @blk.vector
        def _(v):
            n = 0
            for c in range(HC):
                for q in range(NPC):
                    for j in range(NSL):
                        gs = n * NSL + j
                        bk = (gs % 4) * 2
                        tok = (q * NSL + j)
                        v.wait_ge(ct.sem, base_t + (gs + 1) * 2)
                        sl = slice(tok * 512, (tok + 1) * 512)
                        if c == 0:
                            cv.tick(v.tensor_copy(out=sums[0:9, sl], in_=PS[bk][0:9, 0:512]))
                            cv.tick(v.tensor_copy(out=sums[32:33, sl],
                                                  in_=PS[bk + 1][0:1, 0:512]))
                        else:
                            v.wait_ge(cv.sem, cv.v)
                            cv.tick(v.tensor_add(sums[0:9, sl], sums[0:9, sl],
                                                 PS[bk][0:9, 0:512]))
                            cv.tick(v.tensor_add(sums[32:33, sl], sums[32:33, sl],
                                                 PS[bk + 1][0:1, 0:512]))
                    n += 1

    with nc.Block() as blk:
        base_v = cv.v
        base_t = ct.v

        @blk.tensor
        def _(t):
            join(t)
            for k in range(64):
                if k >= 2:
                    t.wait_ge(cv.sem, base_v + (k - 1) * 2)
                ct.tick(t.transpose(PS[k % 2][0:128, 0:9], sums[0:9, k * 128:(k + 1) * 128],
                                    identf[0:9, 0:9]))
                ct.tick(t.transpose(PS[2 + k % 2][0:128, 0:1],
                                    sums[32:33, k * 128:(k + 1) * 128],
                                    identf[32:33, 32:33]))

        @blk.vector
        def _(v):
            for k in range(64):
                v.wait_ge(ct.sem, base_t + 2 * k + 2)
                cv.tick(v.tensor_copy(out=rt[:, k, 0:9], in_=PS[k % 2][0:128, 0:9]))
                cv.tick(v.tensor_copy(out=rt[:, k, 9:10], in_=PS[2 + k % 2][0:128, 0:1]))

    p1a.close()

    p1b = ExitStack()
    sbb = lambda n, s, d: p1b.enter_context(nc.sbuf_tensor(n, s, d))
    maskT = sbb("maskT", [8, T], BF16)
    cumT = sbb("cumT", [8, T], F32)
    keptT = sbb("keptT", [8, T], BF16)
    keptf = sbb("keptf", [8, T], F32)

    with nc.Block() as blk:
        @blk.vector
        def _(v):
            join(v)

            def vt(ins):
                cv.tick(ins)
                v.wait_ge(cv.sem, cv.v)

            vt(v.tensor_scalar(out=col(sc, 0), in0=rt[:, :, 8:9], scalar1=1.0 / H,
                               scalar2=None, op0=ALU.mult))
            vt(v.tensor_scalar(out=col(sc, 1), in0=rt[:, :, 9:10], scalar1=1.0 / H,
                               scalar2=None, op0=ALU.mult))
            vt(v.tensor_mul(col(sc, 2), col(sc, 0), col(sc, 0)))
            vt(v.tensor_sub(col(sc, 1), col(sc, 1), col(sc, 2)))
            vt(v.tensor_scalar(out=col(sc, 2), in0=col(sc, 1), scalar1=1e-5,
                               scalar2=None, op0=ALU.add))
            vt(v.reciprocal(col(sc, 4), col(sc, 2)))
            vt(v.tensor_mul(col(sc, 5), col(sc, 1), col(sc, 4)))
            vt(v.tensor_scalar(out=col(sc, 5), in0=col(sc, 5), scalar1=1e-6,
                               scalar2=None, op0=ALU.add))

        @blk.scalar
        def _(a):
            join(a, cv)
            ca.tick(a.activation(out=col(sc, 6), in_=col(sc, 2), func=AF.Sqrt))
            ca.tick(a.activation(out=col(sc, 7), in_=col(sc, 5), func=AF.Sqrt))

        @blk.vector
        def _(v):
            join(v, ca)

            def vt(ins):
                cv.tick(ins)
                v.wait_ge(cv.sem, cv.v)

            vt(v.reciprocal(col(sc, 2), col(sc, 6)))
            vt(v.reciprocal(col(sc, 3), col(sc, 7)))
            vt(v.tensor_mul(col(sc, 3), col(sc, 3), col(sc, 2)))
            vt(v.tensor_mul(L3[:], colx[:, 8:16].unsqueeze(1)
                            .to_broadcast([128, 64, 8]), colb(sc, 0)))
            vt(v.tensor_sub(L3[:], rt[:, :, 0:8], L3[:]))
            vt(v.tensor_mul(L3[:], L3[:], colb(sc, 3)))
            vt(v.reduce_max(col(sc, 4), L3[:], axis=AX.X))
            vt(v.tensor_sub(P3[:], L3[:], colb(sc, 4)))

        @blk.scalar
        def _(a):
            join(a, cv)
            ca.tick(a.activation(out=P3[:], in_=P3[:], func=AF.Exp))

        @blk.vector
        def _(v):
            join(v, ca)

            def vt(ins):
                cv.tick(ins)
                v.wait_ge(cv.sem, cv.v)

            vt(v.reduce_sum(col(sc, 5), P3[:], axis=AX.X))
            vt(v.reciprocal(col(sc, 5), col(sc, 5)))
            vt(v.tensor_mul(P3[:], P3[:], colb(sc, 5)))
            vt(v.reduce_max(col(sc, 4), L3[:], axis=AX.X))
            vt(v.tensor_tensor(out=M3[:], in0=L3[:], in1=colb(sc, 4), op=ALU.is_ge))
            vt(v.scalar_tensor_tensor(out=M3[:], in0=M3[:], scalar=-64.0, in1=iot8p[:],
                                      op0=ALU.mult, op1=ALU.add))
            vt(v.tensor_reduce(col(sc, 6), M3[:], axis=AX.X, op=ALU.min))
            vt(v.tensor_tensor(out=oh1[:], in0=iot8[:], in1=colb(sc, 6), op=ALU.is_equal))
            vt(v.tensor_scalar(out=M3[:], in0=oh1[:], scalar1=1e30, scalar2=None,
                               op0=ALU.mult))
            vt(v.tensor_sub(M3[:], L3[:], M3[:]))
            vt(v.reduce_max(col(sc, 4), M3[:], axis=AX.X))
            vt(v.tensor_tensor(out=K3[:], in0=M3[:], in1=colb(sc, 4), op=ALU.is_ge))
            vt(v.scalar_tensor_tensor(out=K3[:], in0=K3[:], scalar=-64.0, in1=iot8p[:],
                                      op0=ALU.mult, op1=ALU.add))
            vt(v.tensor_reduce(col(sc, 7), K3[:], axis=AX.X, op=ALU.min))
            vt(v.tensor_tensor(out=oh2[:], in0=iot8[:], in1=colb(sc, 7), op=ALU.is_equal))
            vt(v.tensor_mul(M3[:], P3[:], oh1[:]))
            vt(v.reduce_sum(col(sc, 0), M3[:], axis=AX.X))
            vt(v.tensor_mul(M3[:], P3[:], oh2[:]))
            vt(v.reduce_sum(col(sc, 1), M3[:], axis=AX.X))
            vt(v.tensor_add(M3[:], oh1[:], oh2[:]))
            vt(v.tensor_scalar(out=K3[:], in0=P3[:], scalar1=0.0, scalar2=None,
                               op0=ALU.is_gt))
            vt(v.tensor_mul(M3[:], M3[:], K3[:]))

    with nc.Block() as blk:
        base_v = cv.v
        base_t = ct.v

        @blk.tensor
        def _(t):
            join(t)
            for k in range(64):
                if k >= 2:
                    t.wait_ge(cv.sem, base_v + k - 1)
                ct.tick(t.transpose(PS[k % 2][0:8, 0:128], M3[:, k, :], identf[:]))

        @blk.vector
        def _(v):
            for k in range(64):
                v.wait_ge(ct.sem, base_t + k + 1)
                cv.tick(v.tensor_copy(out=maskT[0:8, k * 128:(k + 1) * 128],
                                      in_=PS[k % 2][0:8, 0:128]))

    with nc.Block() as blk:
        @blk.vector
        def _(v):
            join(v)

            def vt(ins):
                cv.tick(ins)
                v.wait_ge(cv.sem, cv.v)

            vt(v.tensor_tensor_scan(out=cumT[:], data0=maskT[:], data1=maskT[:],
                                    initial=0.0, op0=ALU.add, op1=ALU.max))
            vt(v.tensor_scalar(out=keptT[:], in0=cumT[:], scalar1=float(CAP),
                               scalar2=None, op0=ALU.is_le))
            vt(v.tensor_mul(keptT[:], keptT[:], maskT[:]))
            vt(v.tensor_copy(out=keptf[:], in_=keptT[:]))
            for s in range(NCORE):
                sl = slice(s * TPC, (s + 1) * TPC)
                vt(v.tensor_tensor_scan(out=cumT[:, sl], data0=keptT[:, sl],
                                        data1=keptT[:, sl], initial=0.0,
                                        op0=ALU.add, op1=ALU.max))

    with nc.Block() as blk:
        base_t = ct.v
        base_v = cv.v

        @blk.tensor
        def _(t):
            join(t)
            for k in range(64):
                if k >= 2:
                    t.wait_ge(cv.sem, base_v + (k - 1) * 2)
                ct.tick(t.transpose(PS[4 + k % 2][0:128, 0:8],
                                    keptf[0:8, k * 128:(k + 1) * 128], identf[0:8, 0:8]))
                ct.tick(t.transpose(PS[6 + k % 2][0:128, 0:8],
                                    cumT[0:8, k * 128:(k + 1) * 128], identf[0:8, 0:8]))

        @blk.vector
        def _(v):
            for k in range(64):
                v.wait_ge(ct.sem, base_t + 2 * k + 2)
                cv.tick(v.tensor_copy(out=K3[:, k, :], in_=PS[4 + k % 2][0:128, 0:8]))
                cv.tick(v.tensor_copy(out=C3[:, k, :], in_=PS[6 + k % 2][0:128, 0:8]))

    with nc.Block() as blk:
        @blk.vector
        def _(v):
            join(v)

            def vt(ins):
                cv.tick(ins)
                v.wait_ge(cv.sem, cv.v)

            vt(v.tensor_mul(M3[:], K3[:], oh1[:]))
            vt(v.reduce_sum(col(sc, 2), M3[:], axis=AX.X))
            vt(v.tensor_mul(M3[:], K3[:], oh2[:]))
            vt(v.reduce_sum(col(sc, 3), M3[:], axis=AX.X))
            vt(v.tensor_mul(M3[:], C3[:], oh1[:]))
            vt(v.reduce_sum(col(sc, 4), M3[:], axis=AX.X))
            vt(v.tensor_mul(M3[:], C3[:], oh2[:]))
            vt(v.reduce_sum(col(sc, 5), M3[:], axis=AX.X))
            vt(v.tensor_scalar(out=col(sc, 4), in0=col(sc, 4), scalar1=1.0,
                               scalar2=None, op0=ALU.subtract))
            vt(v.tensor_scalar(out=col(sc, 5), in0=col(sc, 5), scalar1=1.0,
                               scalar2=None, op0=ALU.subtract))
            vt(v.tensor_copy(out=col(rpk, 0), in_=col(sc, 6)))
            vt(v.tensor_copy(out=col(rpk, 1), in_=col(sc, 7)))
            vt(v.tensor_mul(col(rpk, 2), col(sc, 0), col(sc, 2)))
            vt(v.tensor_mul(col(rpk, 3), col(sc, 1), col(sc, 3)))
            vt(v.tensor_scalar(out=col(rpk, 4), in0=col(sc, 6), scalar1=float(CAP),
                               scalar2=None, op0=ALU.mult))
            vt(v.tensor_add(col(rpk, 4), col(rpk, 4), col(sc, 4)))
            vt(v.tensor_scalar(out=col(rpk, 6), in0=col(sc, 2), scalar1=-65536.0,
                               scalar2=65536.0, op0=ALU.mult, op1=ALU.add))
            vt(v.tensor_add(col(rpk, 4), col(rpk, 4), col(rpk, 6)))
            vt(v.tensor_scalar(out=col(rpk, 5), in0=col(sc, 7), scalar1=float(CAP),
                               scalar2=None, op0=ALU.mult))
            vt(v.tensor_add(col(rpk, 5), col(rpk, 5), col(sc, 5)))
            vt(v.tensor_scalar(out=col(rpk, 6), in0=col(sc, 3), scalar1=-65536.0,
                               scalar2=65536.0, op0=ALU.mult, op1=ALU.add))
            vt(v.tensor_add(col(rpk, 5), col(rpk, 5), col(rpk, 6)))
            vt(v.tensor_copy(out=col(rpk, 6), in_=col(sc, 4)))
            vt(v.tensor_copy(out=col(rpk, 7), in_=col(sc, 5)))

        @blk.sync
        def _(sync):
            join(sync, cv)
            cl.tick(sync.dma_start(out=s_rtab[:].rearrange("(p k) f -> p k f", p=128),
                                   in_=rpk[:]))

        @blk.gpsimd
        def _(gp):
            join(gp, cl)
            for k in range(8):
                cg.tick(gp.indirect_dma_start(
                    out=rt_mine[:, k, :], out_offset=None,
                    in_=s_rtab[:],
                    in_offset=IndirectOffsetOnAxis(ap=g_idx[:, k:k + 1], axis=0),
                    bounds_check=T - 1, oob_is_err=False))

    if "rt" in DBG:
        with nc.Block() as blk:
            @blk.sync
            def _(sync):
                join(sync)
                for k in range(8):
                    cl.tick(sync.dma_start(out=dbg["dbg_rt"][k * 128:(k + 1) * 128, :],
                                           in_=rt_mine[:, k, :]))

    p1b.close()
    p1o.close()

    if KPH < 2:
        root.close()
        return nc
    mean_slot = {}
    slot_n = [0]

    p2 = ExitStack()
    sb2 = lambda n, s, d: p2.enter_context(nc.sbuf_tensor(n, s, d))
    wch0 = sb2("wch0", [128, IF], F32)
    wch1 = sb2("wch1", [128, IF], F32)
    wchs = [wch0, wch1]
    q8ch = sb2("q8ch", [128, IF], FP8)
    absd = sb2("absd", [128, IF], BF16)
    i32ch = sb2("i32ch", [128, 2048], I32)
    stag = sb2("stag", [128, 16 * 512], FP8)
    stagd = sb2("stagd", [128, 44 * 512], FP8)

    def mean_job(mk, nch, wdt, key):
        sidx = slot_n[0]
        mean_slot[key] = sidx
        slot_n[0] += 2
        with nc.Block() as blk:
            base_l = cl.v
            base_a = ca.v

            @blk.sync
            def _(sync):
                join(sync)
                for c in range(nch):
                    if c >= 2:
                        sync.wait_ge(ca.sem, base_a + c - 1)
                    cl.tick(sync.dma_start(out=wchs[c % 2][:, 0:wdt], in_=mk(c)))

            base_v = cv.v

            @blk.scalar
            def _(a):
                for c in range(nch):
                    a.wait_ge(cl.sem, base_l + (c + 1) * 16)
                    if c >= 1:
                        a.wait_ge(cv.sem, base_v + c)
                    ca.tick(a.activation(out=absd[:, 0:wdt], in_=wchs[c % 2][:, 0:wdt],
                                         func=AF.Abs, accum_out=wcl[:, 0:1]))

            @blk.vector
            def _(v):
                for c in range(nch):
                    v.wait_ge(ca.sem, base_a + c + 1)
                    v.wait_ge(cv.sem, cv.v)
                    if c == 0:
                        cv.tick(v.tensor_copy(out=wcl[:, 1:2], in_=wcl[:, 0:1]))
                    else:
                        cv.tick(v.tensor_add(wcl[:, 1:2], wcl[:, 1:2], wcl[:, 0:1]))

        with nc.Block() as blk:
            @blk.tensor
            def _(t):
                join(t)
                ct.tick(t.matmul(PS[0][0:1, 0:1], wcl[:, 1:2], ones_c[:],
                                 start=True, stop=True))

            @blk.vector
            def _(v):
                join(v, ct)
                nel = float(nch * 128 * wdt)
                cv.tick(v.tensor_scalar(out=wrow[0:1, sidx + 1:sidx + 2],
                                        in0=PS[0][0:1, 0:1], scalar1=1.0 / nel,
                                        scalar2=1e-5, op0=ALU.mult, op1=ALU.max))
                v.wait_ge(cv.sem, cv.v)
                cv.tick(v.reciprocal(wrow[0:1, sidx:sidx + 1],
                                     wrow[0:1, sidx + 1:sidx + 2]))

    if KP2 >= 1:
        mean_job(lambda c: d_sWg[c * 128:(c + 1) * 128, :], HC, 2048, "sWg")
    if KP2 >= 2:
        mean_job(lambda c: d_sWd[c * 128:(c + 1) * 128, :], IAC, 2048, "sWd")
        for e in range(E):
            mean_job(lambda c, e=e: d_Wg[e, c * 128:(c + 1) * 128, :], HC, 2048, f"Wg{e}")
            mean_job(lambda c, e=e: d_Wd[e, c * 128:(c + 1) * 128, :], IAC, 2048, f"Wd{e}")

    with nc.Block() as blk:
        @blk.tensor
        def _(t):
            join(t)
            ct.tick(t.matmul(PS[1][:, 0:64], ones_r[:], wrow[:], start=True, stop=True))

        @blk.vector
        def _(v):
            join(v, ct)
            cv.tick(v.tensor_copy(out=b_wm[:], in_=PS[1][:, 0:64]))

    def tern_quant2(mk_in, mk_out, nch, wdt, key):
        sidx = mean_slot[key]
        with nc.Block() as blk:
            base_l = cl.v
            base_a = ca.v
            base_v = cv.v
            base_s = cs.v

            @blk.sync
            def _(sync):
                join(sync)
                for c in range(nch):
                    if c >= 2:
                        sync.wait_ge(ca.sem, base_a + c - 1)
                    cl.tick(sync.dma_start(out=wchs[c % 2][:, 0:wdt], in_=mk_in(c)))

            @blk.scalar
            def _(a):
                for c in range(nch):
                    a.wait_ge(cl.sem, base_l + (c + 1) * 16)
                    if c >= 1:
                        a.wait_ge(cv.sem, base_v + c)
                    ca.tick(a.activation(out=i32ch[:, 0:wdt], in_=wchs[c % 2][:, 0:wdt],
                                         func=AF.Identity, scale=b_wm[:, sidx:sidx + 1]))
                    a.wait_ge(cv.sem, base_v + c + 1)
                    cs.tick(a.dma_start(out=mk_out(c), in_=q8ch[:, 0:wdt]))

            @blk.vector
            def _(v):
                for c in range(nch):
                    v.wait_ge(ca.sem, base_a + c + 1)
                    if c >= 1:
                        v.wait_ge(cs.sem, base_s + c * 16)
                    cv.tick(v.tensor_scalar(out=q8ch[:, 0:wdt], in0=i32ch[:, 0:wdt],
                                            scalar1=-1.0, scalar2=1.0,
                                            op0=ALU.max, op1=ALU.min))

    if KP2 >= 4:
        tern_quant2(lambda c: d_sWg[c * 128:(c + 1) * 128, :],
                    lambda c: s_qsWg[c * 128:(c + 1) * 128, :], HC, 2048, "sWg")
        tern_quant2(lambda c: d_sWd[c * 128:(c + 1) * 128, :],
                    lambda c: s_qsWd[c * 128:(c + 1) * 128, :], IAC, 2048, "sWd")

    def fp8_quant_T(load_ap, nch, rowlen, wsrow_store, stag_buf, tile_in_group,
                    flush_after, nflush):
        n_tr = rowlen // 128
        vper = 5 + (n_tr if KQT >= 2 else 0)
        tper = n_tr + (1 if KQT >= 5 else 0)
        aper = n_tr + (1 if KQT >= 5 else 0)
        with nc.Block() as blk:
            base_l = cl.v
            base_v = cv.v
            base_a = ca.v
            base_t = ct.v
            base_g = cg.v

            @blk.sync
            def _(sync):
                join(sync)
                for c in range(nch):
                    if c >= 2:
                        sync.wait_ge(cv.sem, base_v + (c - 1) * vper)
                    cl.tick(sync.dma_start(out=wchs[c % 2][:, 0:rowlen], in_=load_ap(c)))

            @blk.vector
            def _(v):
                for c in range(nch):
                    v.wait_ge(cl.sem, base_l + (c + 1) * 16)
                    v.wait_ge(cv.sem, cv.v)
                    cv.tick(v.reduce_max(wcl[:, 0:1], wchs[c % 2][:, 0:rowlen],
                                         axis=AX.X, apply_absolute_value=True))
                    v.wait_ge(cv.sem, cv.v)
                    cv.tick(v.reciprocal(wcl[:, 1:2], wcl[:, 0:1]))
                    v.wait_ge(cv.sem, cv.v)
                    cv.tick(v.tensor_scalar(out=wcl[:, 2:3], in0=wcl[:, 1:2],
                                            scalar1=224.0, scalar2=None, op0=ALU.mult))
                    cv.tick(v.tensor_scalar(out=wcl[:, 3:4], in0=wcl[:, 0:1],
                                            scalar1=1.0 / 224.0, scalar2=None,
                                            op0=ALU.mult))
                    v.wait_ge(cv.sem, cv.v)
                    cv.tick(v.tensor_scalar(out=q8ch[:, 0:rowlen], in0=wchs[c % 2][:, 0:rowlen],
                                            scalar1=wcl[:, 2:3], scalar2=None, op0=ALU.mult))
                    v.wait_ge(cv.sem, cv.v)
                    if KQT < 2:
                        continue
                    for tr in range(n_tr):
                        idx = c * n_tr + tr
                        if idx >= 2 and KQT >= 3:
                            t_idx = (c * tper + tr - 2 if tr >= 2
                                     else (c - 1) * tper + n_tr + tr - 2)
                            v.wait_ge(ct.sem, base_t + t_idx + 1)
                        cv.tick(v.tensor_copy(out=stgf[:, (idx % 2) * 128:(idx % 2) * 128 + 128],
                                              in_=q8ch[:, tr * 128:(tr + 1) * 128]))

            @blk.tensor
            def _(t):
                if KQT < 3:
                    return
                for c in range(nch):
                    for tr in range(n_tr):
                        t.wait_ge(cv.sem, base_v + c * vper + 5 + tr + 1)
                        aidx = c * aper + tr
                        if aidx >= 2 and KQT >= 4:
                            t.wait_ge(ca.sem, base_a + aidx - 1)
                        ct.tick(t.transpose(PS[4 + (c * n_tr + tr) % 2][0:128, 0:128],
                                            stgf[:, ((c * n_tr + tr) % 2) * 128:
                                                 ((c * n_tr + tr) % 2) * 128 + 128],
                                            identf[:]))
                    if KQT < 5:
                        continue
                    t.wait_ge(cv.sem, base_v + c * vper + 4)
                    if c >= 1 and KQT >= 4:
                        t.wait_ge(ca.sem, base_a + c * aper)
                    ct.tick(t.transpose(PS[6][0:1, 0:128], wcl[:, 3:4], identf[:]))

            base_s = cs.v

            @blk.scalar
            def _(a):
                if KQT < 4:
                    return
                for c in range(nch):
                    if c % 4 == 0 and c > 0 and KQT >= 6:
                        a.wait_ge(cs.sem, base_s + (c // 4) * nflush * 16)
                    for tr in range(n_tr):
                        a.wait_ge(ct.sem, base_t + c * tper + tr + 1)
                        dstt, dsto = tile_in_group(c, tr)
                        ca.tick(a.activation(out=stag_buf[:, dstt * 512 + dsto * 128:
                                                          dstt * 512 + dsto * 128 + 128],
                                             in_=PS[4 + (c * n_tr + tr) % 2][0:128, 0:128],
                                             func=AF.Identity))
                    if KQT >= 5:
                        a.wait_ge(ct.sem, base_t + c * tper + tper)
                        ca.tick(a.activation(out=wsrow_store(c), in_=PS[6][0:1, 0:128],
                                             func=AF.Identity))
                    if KQT >= 6:
                        for (dram_ap, ssl) in flush_after(c):
                            cs.tick(a.dma_start(out=dram_ap, in_=stag_buf[:, ssl]))

    wsr_wg = sb2("wsr_wg", [1, IF], F32)
    wsr_wu = sb2("wsr_wu", [1, IF], F32)
    wsr_wd = sb2("wsr_wd", [1, H], F32)

    def tile_ig_up(c, tr):
        return (tr, c % 4)

    def flush_up(s_out):
        def fl(c):
            if c % 4 == 3:
                ic4 = c // 4
                return [(s_out[hc, ic4, :, :], slice(hc * 512, (hc + 1) * 512))
                        for hc in range(HC)]
            return []
        return fl

    def tile_ig_dn(c, tr):
        return (tr, c % 4)

    def flush_dn(c):
        if c % 4 == 3:
            hc4 = c // 4
            return [(s_qwdT[ic, hc4, :, :], slice(ic * 512, (ic + 1) * 512))
                    for ic in range(IFC)]
        return []

    if KP2 >= 5:
        fp8_quant_T(lambda c: d_wg[c * 128:(c + 1) * 128, :], IFC, 2048,
                    lambda c: wsr_wg[0:1, c * 128:(c + 1) * 128], stag, tile_ig_up,
                    flush_up(s_qwgT), HC)
    if KP2 >= 6:
        fp8_quant_T(lambda c: d_wu[c * 128:(c + 1) * 128, :], IFC, 2048,
                    lambda c: wsr_wu[0:1, c * 128:(c + 1) * 128], stag, tile_ig_up,
                    flush_up(s_qwuT), HC)
    if KP2 >= 7:
        fp8_quant_T(lambda c: d_wd[c * 128:(c + 1) * 128, :], HC, IF,
                    lambda c: wsr_wd[0:1, c * 128:(c + 1) * 128], stagd, tile_ig_dn,
                    flush_dn, IFC)

    with nc.Block() as blk:
        @blk.sync
        def _(sync):
            join(sync)
            cl.tick(sync.dma_start(out=s_wgs[:], in_=wsr_wg[:]))
            cl.tick(sync.dma_start(out=s_wus[:], in_=wsr_wu[:]))
            cl.tick(sync.dma_start(out=s_wds[:], in_=wsr_wd[:]))

    p2.close()

    if KPH < 3:
        root.close()
        return nc
    p4 = ExitStack()
    sb4 = lambda n, s, d: p4.enter_context(nc.sbuf_tensor(n, s, d))
    b_sNg = sb4("b_sNg", [128, H], F32)
    b_sNd = sb4("b_sNd", [128, IA], F32)
    b_Wsg = sb4("b_Wsg", [128, H], F32)
    arena = sb4("arena", [128, NT * IF], F32)
    qtTs = sb4("qtTs", [128, NT * IF], BF16)
    qxT = sb4("qxT", [128, NT * H], BF16)
    accg = sb4("accg", [128, NT * H], F32)
    xpg = sb4("xpg", [128, NT * H], F32)
    q8g = sb4("q8g", [128, NT * IF], FP8)
    gch = sb4("gch", [128, NT * 512], F32)
    uch = sb4("uch", [128, NT * 512], F32)
    sich = sb4("sich", [128, NT * 512], F32)
    slabU = sb4("slabU", [128, IFC * 512], FP8)
    wslab = slabU
    wdslab = slabU
    tslab = slabU
    wbc = sb4("wbc", [128, 1024], F32)
    shar = arena[:, 0:NT * H]
    shy = arena[:, NT * H:2 * NT * H]
    shh = arena[:, 2 * NT * H:2 * NT * H + NT * IA]
    i32g = qtTs[:].bitcast(I32)[:, 0:NT * H]
    k1T = sb4("k1T", [128, NT * H], BF16)
    k2T = sb4("k2T", [128, NT * IA], BF16)
    gcol = sb4("gcol", [128, NT * 16], F32)

    bcast_from_dram(b_sNg, d_sNg[0:1, :], H)
    bcast_from_dram(b_sNd, d_sNd[0:1, :], IA)
    bcast_from_dram(b_Wsg, d_Wsg[0:1, :], H)

    def gc(kt, i):
        return gcol[:, kt * 16 + i:kt * 16 + i + 1]

    def transpose_set(srcf, dst, nblk, blk128):
        with nc.Block() as blk:
            base_t = ct.v
            base_a = ca.v

            @blk.tensor
            def _(t):
                join(t)
                for i in range(nblk):
                    if i >= 2:
                        t.wait_ge(ca.sem, base_a + i - 1)
                    ct.tick(t.transpose(PS[4 + i % 2][0:128, 0:128], srcf(i), identf[:]))

            @blk.scalar
            def _(a):
                for i in range(nblk):
                    a.wait_ge(ct.sem, base_t + i + 1)
                    ca.tick(a.activation(out=dst(i), in_=PS[4 + i % 2][0:128, 0:128],
                                         func=AF.Identity))

    for g in range(NG):
        r0 = g * GT

        with nc.Block() as blk:
            @blk.sync
            def _(sync):
                join(sync)
                cl.tick(sync.dma_start(out=accg[:].rearrange("p (k h) -> p k h", k=NT),
                                       in_=d_xsh[r0:r0 + GT, :]
                                       .rearrange("(k p) h -> p k h", p=128)))

            @blk.scalar
            def _(a):
                join(a, cl)
                for kt in range(NT):
                    ca.tick(a.activation(out=shar[:, kt * H:(kt + 1) * H],
                                         in_=accg[:, kt * H:(kt + 1) * H],
                                         func=AF.Square, accum_out=gc(kt, 0)))

            @blk.vector
            def _(v):
                join(v, ca, cl)

                def vt(ins):
                    cv.tick(ins)
                    v.wait_ge(cv.sem, cv.v)

                for kt in range(NT):
                    sl = slice(kt * H, (kt + 1) * H)
                    vt(v.reduce_sum(gc(kt, 1), accg[:, sl], axis=AX.X))
                    vt(v.tensor_scalar(out=gc(kt, 1), in0=gc(kt, 1), scalar1=1.0 / H,
                                       scalar2=None, op0=ALU.mult))
                    vt(v.tensor_scalar(out=gc(kt, 0), in0=gc(kt, 0), scalar1=1.0 / H,
                                       scalar2=None, op0=ALU.mult))
                    vt(v.tensor_mul(gc(kt, 2), gc(kt, 1), gc(kt, 1)))
                    vt(v.tensor_sub(gc(kt, 0), gc(kt, 0), gc(kt, 2)))
                    vt(v.tensor_scalar(out=gc(kt, 2), in0=gc(kt, 0), scalar1=1e-5,
                                       scalar2=None, op0=ALU.add))

            @blk.scalar
            def _(a):
                join(a, cv)
                for kt in range(NT):
                    ca.tick(a.activation(out=gc(kt, 3), in_=gc(kt, 2), func=AF.Sqrt))

            @blk.vector
            def _(v):
                join(v, ca)

                def vt(ins):
                    cv.tick(ins)
                    v.wait_ge(cv.sem, cv.v)

                for kt in range(NT):
                    sl = slice(kt * H, (kt + 1) * H)
                    vt(v.reciprocal(gc(kt, 3), gc(kt, 3)))
                    vt(v.tensor_scalar(out=gc(kt, 4), in0=gc(kt, 1), scalar1=-1.0,
                                       scalar2=None, op0=ALU.mult))
                    vt(v.tensor_scalar(out=xpg[:, sl], in0=accg[:, sl],
                                       scalar1=gc(kt, 4), scalar2=gc(kt, 3),
                                       op0=ALU.add, op1=ALU.mult))
                    vt(v.reciprocal(gc(kt, 5), gc(kt, 2)))
                    vt(v.tensor_mul(gc(kt, 5), gc(kt, 5), gc(kt, 0)))
                    vt(v.tensor_scalar(out=gc(kt, 5), in0=gc(kt, 5), scalar1=1e-8,
                                       scalar2=None, op0=ALU.add))

            @blk.scalar
            def _(a):
                join(a, cv)
                for kt in range(NT):
                    ca.tick(a.activation(out=gc(kt, 6), in_=gc(kt, 5), func=AF.Sqrt))
                for kt in range(NT):
                    cs.tick(a.dma_start(out=s_xp[r0 + kt * 128:r0 + (kt + 1) * 128, :],
                                        in_=xpg[:, kt * H:(kt + 1) * H]))

            @blk.vector
            def _(v):
                join(v, ca)
                for kt in range(NT):
                    v.wait_ge(cv.sem, cv.v)
                    cv.tick(v.reciprocal(gc(kt, 5), gc(kt, 6)))

            @blk.gpsimd
            def _(gp):
                join(gp, cv)
                for kt in range(NT):
                    gk = g * NT + kt
                    cp.tick(gp.tensor_copy(out=u32c[:, 0:1], in_=rt_mine[:, gk, 4:5]))
                    cp.tick(gp.tensor_copy(out=u32c[:, 1:2], in_=rt_mine[:, gk, 5:6]))
                    gp.wait_ge(cp.sem, cp.v)
                    cg.tick(gp.indirect_dma_start(
                        out=s_xg[:],
                        out_offset=IndirectOffsetOnAxis(ap=u32c[:, 0:1], axis=0),
                        in_=xpg[:, kt * H:(kt + 1) * H], in_offset=None,
                        bounds_check=E * CAP - 1, oob_is_err=False))
                    cg.tick(gp.indirect_dma_start(
                        out=s_xg[:],
                        out_offset=IndirectOffsetOnAxis(ap=u32c[:, 1:2], axis=0),
                        in_=xpg[:, kt * H:(kt + 1) * H], in_offset=None,
                        bounds_check=E * CAP - 1, oob_is_err=False))

        with nc.Block() as blk:
            @blk.vector
            def _(v):
                join(v)

                def vt(ins):
                    cv.tick(ins)
                    v.wait_ge(cv.sem, cv.v)

                for kt in range(NT):
                    sl = slice(kt * H, (kt + 1) * H)
                    vt(v.reduce_max(gc(kt, 7), xpg[:, sl], axis=AX.X,
                                    apply_absolute_value=True))
                    vt(v.reciprocal(gc(kt, 8), gc(kt, 7)))
                    vt(v.tensor_scalar(out=gc(kt, 8), in0=gc(kt, 8), scalar1=224.0,
                                       scalar2=None, op0=ALU.mult))
                    vt(v.tensor_scalar(out=gc(kt, 9), in0=gc(kt, 7), scalar1=1.0 / 224.0,
                                       scalar2=None, op0=ALU.mult))
                    vt(v.tensor_scalar(out=q8g[:, sl], in0=xpg[:, sl],
                                       scalar1=gc(kt, 8), scalar2=None, op0=ALU.mult))
                for kt in range(NT):
                    sl = slice(kt * H, (kt + 1) * H)
                    vt(v.tensor_copy(out=shar[:, sl], in_=q8g[:, sl]))

        transpose_set(lambda i: shar[:, i * 128:(i + 1) * 128],
                      lambda i: qxT[:, i * 128:(i + 1) * 128], NT * HC, 128)

        NJ = (IF // 512) * 2
        with nc.Block() as blk:
            base_l4 = cl.v
            base_t4 = ct.v
            base_v4 = cv.v
            base_a4 = ca.v
            VPJ = 3 + 4

            def vbase(j):
                full = j // 2
                r = 3 + 11
                return base_v4 + full * r + (3 if j % 2 else 0) * 0 + (j % 2) * 3

            def vend(j):
                return vbase(j + 1) if False else (vbase(j) + (3 if j % 2 == 0 else 11))

            @blk.sync
            def _(sync):
                join(sync)
                for j in range(NJ):
                    ic, m = j // 2, j % 2
                    s_w = s_qwgT if m == 0 else s_qwuT
                    s_ws = s_wgs if m == 0 else s_wus
                    if j >= 2:
                        sync.wait_ge(ct.sem, base_t4 + (j - 1) * (NT + 1))
                    for c in range(HC):
                        cl.tick(sync.dma_start(
                            out=wslab[:, (j % 2) * HC * 512 + c * 512:
                                      (j % 2) * HC * 512 + (c + 1) * 512],
                            in_=s_w[c, ic, :, :]))
                    cl.tick(sync.dma_start(out=stg_row[0:1, (j % 2) * 512:(j % 2) * 512 + 512],
                                           in_=s_ws[0:1, ic * 512:(ic + 1) * 512]))

            @blk.tensor
            def _(t):
                t.wait_ge(cv.sem, base_v4)
                t.wait_ge(ca.sem, base_a4)
                for j in range(NJ):
                    t.wait_ge(cl.sem, base_l4 + (j + 1) * 17 * 16)
                    if j >= 1:
                        t.wait_ge(cv.sem, vbase(j - 1) + 3)
                    ct.tick(t.matmul(PS[2 + j % 2][:, 0:512], ones_r[:],
                                     stg_row[0:1, (j % 2) * 512:(j % 2) * 512 + 512],
                                     start=True, stop=True))
                    for kt in range(NT):
                        ins = None
                        for c in range(HC):
                            ins = t.matmul(PS[kt][0:128, 0:512],
                                           qxT[:, (kt * HC + c) * 128:
                                               (kt * HC + c) * 128 + 128],
                                           wslab[:, (j % 2) * HC * 512 + c * 512:
                                                 (j % 2) * HC * 512 + (c + 1) * 512],
                                           start=(c == 0), stop=(c == HC - 1))
                        ct.tick(ins)

            @blk.vector
            def _(v):
                for j in range(NJ):
                    ic, m = j // 2, j % 2
                    dst = gch if m == 0 else uch
                    if m == 0 and ic >= 1:
                        v.wait_ge(ca.sem, base_a4 + ic)
                    v.wait_ge(ct.sem, base_t4 + j * (NT + 1) + 1)
                    v.wait_ge(cv.sem, cv.v)
                    cv.tick(v.tensor_copy(out=wbc[:, (j % 2) * 512:(j % 2) * 512 + 512],
                                          in_=PS[2 + j % 2][:, 0:512]))
                    for kt in range(NT):
                        v.wait_ge(ct.sem, base_t4 + j * (NT + 1) + 1 + kt + 1)
                        v.wait_ge(cv.sem, cv.v)
                        cv.tick(v.scalar_tensor_tensor(
                            out=dst[:, kt * 512:(kt + 1) * 512],
                            in0=PS[kt][0:128, 0:512], scalar=gc(kt, 9),
                            in1=wbc[:, (j % 2) * 512:(j % 2) * 512 + 512],
                            op0=ALU.mult, op1=ALU.mult))
                    if m == 1:
                        v.wait_ge(ca.sem, base_a4 + ic + 1)
                        for kt in range(NT):
                            for sub in range(4):
                                icb = ic * 4 + sub
                                off = icb * NT * 128 + kt * 128
                                v.wait_ge(cv.sem, cv.v)
                                cv.tick(v.tensor_mul(
                                    arena[:, off:off + 128],
                                    sich[:, kt * 512 + sub * 128:
                                          kt * 512 + (sub + 1) * 128],
                                    uch[:, kt * 512 + sub * 128:
                                        kt * 512 + (sub + 1) * 128]))

            @blk.scalar
            def _(a):
                for ic in range(IF // 512):
                    a.wait_ge(cv.sem, vbase(2 * ic) + 3)
                    ca.tick(a.activation(out=sich[:], in_=gch[:], func=AF.Silu))

        with nc.Block() as blk:
            @blk.vector
            def _(v):
                join(v)

                def vt(ins):
                    cv.tick(ins)
                    v.wait_ge(cv.sem, cv.v)

                for kt in range(NT):
                    tv = arena[:].rearrange("p (i k c) -> p i k c", i=IFC, k=NT)
                    vt(v.reduce_max(gch[:, kt * IFC:(kt + 1) * IFC].unsqueeze(2),
                                    tv[:, :, kt, :], axis=AX.X,
                                    apply_absolute_value=True))
                    vt(v.reduce_max(gc(kt, 10), gch[:, kt * IFC:(kt + 1) * IFC],
                                    axis=AX.X))
                    vt(v.reciprocal(gc(kt, 11), gc(kt, 10)))
                    vt(v.tensor_scalar(out=gc(kt, 11), in0=gc(kt, 11), scalar1=224.0,
                                       scalar2=None, op0=ALU.mult))
                    vt(v.tensor_scalar(out=gc(kt, 12), in0=gc(kt, 10), scalar1=1.0 / 224.0,
                                       scalar2=None, op0=ALU.mult))
                for kt in range(NT):
                    for icb in range(IFC):
                        off = icb * NT * 128 + kt * 128
                        cv.tick(v.tensor_scalar(out=q8g[:, off:off + 128],
                                                in0=arena[:, off:off + 128],
                                                scalar1=gc(kt, 11), scalar2=None,
                                                op0=ALU.mult))
                v.wait_ge(cv.sem, cv.v)
                cv.tick(v.tensor_copy(out=arena[:], in_=q8g[:]))

        def _qt_src(i):
            kt, icb = i // IFC, i % IFC
            off = icb * NT * 128 + kt * 128
            return arena[:, off:off + 128]

        def _qt_dst(i):
            kt, icb = i // IFC, i % IFC
            off = (kt * IFC + icb) * 128
            return qtTs[:, off:off + 128]

        transpose_set(_qt_src, _qt_dst, NT * IFC, 128)

        for hc4 in range(HC // 4):
            with nc.Block() as blk:
                @blk.sync
                def _(sync):
                    join(sync)
                    for c in range(IFC):
                        cl.tick(sync.dma_start(out=wdslab[:, c * 512:(c + 1) * 512],
                                               in_=s_qwdT[c, hc4, :, :]))
                    cl.tick(sync.dma_start(out=stg_row[0:1, 0:512],
                                           in_=s_wds[0:1, hc4 * 512:(hc4 + 1) * 512]))

                @blk.tensor
                def _(t):
                    join(t, cl, cv, ca)
                    ct.tick(t.matmul(PS[2][:, 0:512], ones_r[:], stg_row[0:1, 0:512],
                                     start=True, stop=True))
                    for kt in range(NT):
                        ins = None
                        for c in range(IFC):
                            ins = t.matmul(PS[kt][0:128, 0:512],
                                           qtTs[:, (kt * IFC + c) * 128:
                                                (kt * IFC + c) * 128 + 128],
                                           wdslab[:, c * 512:(c + 1) * 512],
                                           start=(c == 0), stop=(c == IFC - 1))
                        ct.tick(ins)

                @blk.vector
                def _(v):
                    join(v, ct)
                    v.wait_ge(cv.sem, cv.v)
                    cv.tick(v.tensor_copy(out=wbc[:, 0:512], in_=PS[2][:, 0:512]))
                    for kt in range(NT):
                        v.wait_ge(cv.sem, cv.v)
                        cv.tick(v.scalar_tensor_tensor(
                            out=gch[:, kt * 512:(kt + 1) * 512],
                            in0=PS[kt][0:128, 0:512], scalar=gc(kt, 12),
                            in1=wbc[:, 0:512], op0=ALU.mult, op1=ALU.mult))
                        v.wait_ge(cv.sem, cv.v)
                        sl = slice(kt * H + hc4 * 512, kt * H + (hc4 + 1) * 512)
                        cv.tick(v.tensor_add(accg[:, sl], accg[:, sl],
                                             gch[:, kt * 512:(kt + 1) * 512]))

        with nc.Block() as blk:
            @blk.vector
            def _(v):
                join(v)

                def vt(ins):
                    cv.tick(ins)
                    v.wait_ge(cv.sem, cv.v)

                for kt in range(NT):
                    sl = slice(kt * H, (kt + 1) * H)
                    vt(v.scalar_tensor_tensor(out=shar[:, sl], in0=xpg[:, sl],
                                              scalar=gc(kt, 5), in1=b_sNg[:],
                                              op0=ALU.mult, op1=ALU.mult))
                    vt(v.reduce_max(gc(kt, 13), shar[:, sl], axis=AX.X,
                                    apply_absolute_value=True))
                    vt(v.tensor_scalar(out=gc(kt, 13), in0=gc(kt, 13), scalar1=1e-5,
                                       scalar2=None, op0=ALU.max))
                    vt(v.reciprocal(gc(kt, 14), gc(kt, 13)))
                    vt(v.tensor_scalar(out=gc(kt, 14), in0=gc(kt, 14), scalar1=127.0,
                                       scalar2=None, op0=ALU.mult))
                    vt(v.tensor_scalar(out=gc(kt, 13), in0=gc(kt, 13), scalar1=1.0 / 127.0,
                                       scalar2=None, op0=ALU.mult))

            @blk.scalar
            def _(a):
                join(a, cv)
                for kt in range(NT):
                    sl = slice(kt * H, (kt + 1) * H)
                    ca.tick(a.activation(out=i32g[:, sl], in_=shar[:, sl],
                                         func=AF.Identity, scale=gc(kt, 14)))

            @blk.vector
            def _(v):
                join(v, ca)
                cv.tick(v.tensor_copy(out=shar[:, 0:NT * H], in_=i32g[:, 0:NT * H]))

        transpose_set(lambda i: shar[:, i * 128:(i + 1) * 128],
                      lambda i: k1T[:, i * 128:(i + 1) * 128], NT * HC, 128)

        for oc in range(4):
            with nc.Block() as blk:
                @blk.sync
                def _(sync):
                    join(sync)
                    for c in range(HC):
                        cl.tick(sync.dma_start(
                            out=tslab[:, c * 512:(c + 1) * 512],
                            in_=s_qsWg[c * 128:(c + 1) * 128, oc * 512:(oc + 1) * 512]))

                @blk.tensor
                def _(t):
                    join(t, cl, ca)
                    for kt in range(NT):
                        ins = None
                        for c in range(HC):
                            ins = t.matmul(PS[kt][0:128, 0:512],
                                           k1T[:, (kt * HC + c) * 128:
                                               (kt * HC + c) * 128 + 128],
                                           tslab[:, c * 512:(c + 1) * 512],
                                           start=(c == 0), stop=(c == HC - 1))
                        ct.tick(ins)

                @blk.vector
                def _(v):
                    join(v, ct)
                    sw = mean_slot["sWg"]
                    for kt in range(NT):
                        v.wait_ge(cv.sem, cv.v)
                        cv.tick(v.tensor_mul(gc(kt, 15), gc(kt, 13),
                                             b_wm[:, sw + 1:sw + 2]))
                        v.wait_ge(cv.sem, cv.v)
                        cv.tick(v.tensor_scalar(
                            out=shy[:, kt * H + oc * 512:kt * H + (oc + 1) * 512],
                            in0=PS[kt][0:128, 0:512], scalar1=gc(kt, 15),
                            scalar2=None, op0=ALU.mult))

        with nc.Block() as blk:
            @blk.scalar
            def _(a):
                join(a, cv)
                for kt in range(NT):
                    ca.tick(a.activation(out=shar[:, kt * H:kt * H + IA],
                                         in_=shy[:, kt * H:kt * H + IA], func=AF.Silu))

            @blk.vector
            def _(v):
                join(v, ca)
                for kt in range(NT):
                    v.wait_ge(cv.sem, cv.v)
                    cv.tick(v.tensor_mul(shh[:, kt * IA:(kt + 1) * IA],
                                         shar[:, kt * H:kt * H + IA],
                                         shy[:, kt * H + IA:kt * H + 2 * IA]))

            @blk.scalar
            def _(a):
                join(a, cv)
                for kt in range(NT):
                    ca.tick(a.activation(out=shar[:, kt * H:kt * H + IA],
                                         in_=shh[:, kt * IA:(kt + 1) * IA],
                                         func=AF.Square, accum_out=gc(kt, 0)))

            @blk.vector
            def _(v):
                join(v, ca)
                for kt in range(NT):
                    v.wait_ge(cv.sem, cv.v)
                    cv.tick(v.tensor_scalar(out=gc(kt, 0), in0=gc(kt, 0), scalar1=1.0 / IA,
                                            scalar2=1e-8, op0=ALU.mult, op1=ALU.add))

            @blk.scalar
            def _(a):
                join(a, cv)
                for kt in range(NT):
                    ca.tick(a.activation(out=gc(kt, 1), in_=gc(kt, 0), func=AF.Sqrt))

            @blk.vector
            def _(v):
                join(v, ca)

                def vt(ins):
                    cv.tick(ins)
                    v.wait_ge(cv.sem, cv.v)

                for kt in range(NT):
                    vt(v.reciprocal(gc(kt, 1), gc(kt, 1)))
                    vt(v.scalar_tensor_tensor(out=shar[:, kt * H:kt * H + IA],
                                              in0=shh[:, kt * IA:(kt + 1) * IA],
                                              scalar=gc(kt, 1), in1=b_sNd[:],
                                              op0=ALU.mult, op1=ALU.mult))
                    vt(v.reduce_max(gc(kt, 2), shar[:, kt * H:kt * H + IA], axis=AX.X,
                                    apply_absolute_value=True))
                    vt(v.tensor_scalar(out=gc(kt, 2), in0=gc(kt, 2), scalar1=1e-5,
                                       scalar2=None, op0=ALU.max))
                    vt(v.reciprocal(gc(kt, 3), gc(kt, 2)))
                    vt(v.tensor_scalar(out=gc(kt, 3), in0=gc(kt, 3), scalar1=127.0,
                                       scalar2=None, op0=ALU.mult))
                    vt(v.tensor_scalar(out=gc(kt, 2), in0=gc(kt, 2), scalar1=1.0 / 127.0,
                                       scalar2=None, op0=ALU.mult))

            @blk.scalar
            def _(a):
                join(a, cv)
                for kt in range(NT):
                    ca.tick(a.activation(out=i32g[:, kt * IA:(kt + 1) * IA],
                                         in_=shar[:, kt * H:kt * H + IA],
                                         func=AF.Identity, scale=gc(kt, 3)))

            @blk.vector
            def _(v):
                join(v, ca)
                cv.tick(v.tensor_copy(out=shar[:, 0:NT * IA], in_=i32g[:, 0:NT * IA]))

        transpose_set(lambda i: shar[:, i * 128:(i + 1) * 128],
                      lambda i: k2T[:, i * 128:(i + 1) * 128], NT * IAC, 128)

        with nc.Block() as blk:
            @blk.vector
            def _(v):
                join(v)
                for kt in range(NT):
                    sl = slice(kt * H, (kt + 1) * H)
                    v.wait_ge(cv.sem, cv.v)
                    cv.tick(v.tensor_mul(shy[:, sl], xpg[:, sl], b_Wsg[:]))
                    v.wait_ge(cv.sem, cv.v)
                    cv.tick(v.reduce_sum(gc(kt, 4), shy[:, sl], axis=AX.X))

            @blk.scalar
            def _(a):
                join(a, cv)
                for kt in range(NT):
                    ca.tick(a.activation(out=gc(kt, 4), in_=gc(kt, 4), func=AF.Sigmoid))

        for oc in range(4):
            with nc.Block() as blk:
                @blk.sync
                def _(sync):
                    join(sync)
                    for c in range(IAC):
                        cl.tick(sync.dma_start(
                            out=tslab[:, c * 512:(c + 1) * 512],
                            in_=s_qsWd[c * 128:(c + 1) * 128, oc * 512:(oc + 1) * 512]))

                @blk.tensor
                def _(t):
                    join(t, cl, ca)
                    for kt in range(NT):
                        ins = None
                        for c in range(IAC):
                            ins = t.matmul(PS[kt][0:128, 0:512],
                                           k2T[:, (kt * IAC + c) * 128:
                                               (kt * IAC + c) * 128 + 128],
                                           tslab[:, c * 512:(c + 1) * 512],
                                           start=(c == 0), stop=(c == IAC - 1))
                        ct.tick(ins)

                @blk.vector
                def _(v):
                    join(v, ct)
                    sw = mean_slot["sWd"]
                    for kt in range(NT):
                        v.wait_ge(cv.sem, cv.v)
                        cv.tick(v.tensor_mul(gc(kt, 15), gc(kt, 2), b_wm[:, sw + 1:sw + 2]))
                        v.wait_ge(cv.sem, cv.v)
                        cv.tick(v.tensor_scalar(out=gch[:, kt * 512:(kt + 1) * 512],
                                                in0=PS[kt][0:128, 0:512],
                                                scalar1=gc(kt, 15), scalar2=None,
                                                op0=ALU.mult))
                        v.wait_ge(cv.sem, cv.v)
                        cv.tick(v.tensor_mul(gch[:, kt * 512:(kt + 1) * 512],
                                             gch[:, kt * 512:(kt + 1) * 512],
                                             gc(kt, 4).to_broadcast([128, 512])))
                        v.wait_ge(cv.sem, cv.v)
                        sl = slice(kt * H + oc * 512, kt * H + (oc + 1) * 512)
                        cv.tick(v.tensor_add(accg[:, sl], accg[:, sl],
                                             gch[:, kt * 512:(kt + 1) * 512]))

        with nc.Block() as blk:
            @blk.sync
            def _(sync):
                join(sync)
                cl.tick(sync.dma_start(out=s_acc[r0:r0 + GT, :]
                                       .rearrange("(k p) h -> p k h", p=128),
                                       in_=accg[:].rearrange("p (k h) -> p k h", k=NT)))

    if "xp" in DBG:
        with nc.Block() as blk:
            @blk.sync
            def _(sync):
                join(sync)
                cl.tick(sync.dma_start(out=dbg["dbg_xp"][:], in_=s_xp[:]))

    p4.close()

    if KPH < 4:
        root.close()
        return nc
    p5 = ExitStack()
    sb5 = lambda n, s, d: p5.enter_context(nc.sbuf_tensor(n, s, d))
    qWgE = sb5("qWgE", [128, HC * 2048], FP8)
    qWdE = sb5("qWdE", [128, IAC * 2048], FP8)
    ewch0 = sb5("ewch0", [128, 2048], F32)
    ewch1 = sb5("ewch1", [128, 2048], F32)
    ewchD = [ewch0, ewch1]
    ei32 = sb5("ei32", [128, 2 * 2048], I32)
    b_NgE = sb5("b_NgE", [128, H], F32)
    b_NdE = sb5("b_NdE", [128, IA], F32)
    xg = sb5("xg", [128, 2 * H], F32)
    exn = sb5("exn", [128, 2 * H], F32)
    ekT = sb5("ekT", [128, 2 * H], BF16)
    ek2T = sb5("ek2T", [128, 2 * IA], BF16)
    eyb = sb5("eyb", [128, 2 * 2 * IA], F32)
    ehh = sb5("ehh", [128, 2 * IA], F32)
    ecol = sb5("ecol", [128, 16], F32)

    NTILE = CAP // 128
    VP = 2 + 7 + 1 + 5 + 1 + 1 + 7 + 1 + 5
    APc = 2 + 1 + 1 + 16 + 1 + 1 + 1 + 1 + 8
    TP = 16 + 4 + 8 + 4
    LP = 32

    for e in range(E):
        bcast_from_dram(b_NgE, d_Ng[e:e + 1, :], H)
        bcast_from_dram(b_NdE, d_Nd[e:e + 1, :], IA)

        for (dsrc, nch, qdst, key) in ((d_Wg, HC, qWgE, f"Wg{e}"),
                                       (d_Wd, IAC, qWdE, f"Wd{e}")):
            sidx = mean_slot[key]
            with nc.Block() as blk:
                base_l = cl.v
                base_a = ca.v
                base_v = cv.v

                @blk.sync
                def _(sync):
                    join(sync)
                    for c in range(nch):
                        if c >= 2:
                            sync.wait_ge(ca.sem, base_a + c - 1)
                        cl.tick(sync.dma_start(out=ewch[:, (c % 2) * 1024:(c % 2) * 1024 + 2048]
                                               .bitcast(F32) if False else
                                               ewchD[c % 2][:],
                                               in_=dsrc[e, c * 128:(c + 1) * 128, :]))

                @blk.scalar
                def _(a):
                    for c in range(nch):
                        a.wait_ge(cl.sem, base_l + (c + 1) * 16)
                        if c >= 2:
                            a.wait_ge(cv.sem, base_v + c - 1)
                        ca.tick(a.activation(out=ei32[:, (c % 2) * 2048:(c % 2) * 2048 + 2048],
                                             in_=ewchD[c % 2][:], func=AF.Identity,
                                             scale=b_wm[:, sidx:sidx + 1]))

                @blk.vector
                def _(v):
                    for c in range(nch):
                        v.wait_ge(ca.sem, base_a + c + 1)
                        cv.tick(v.tensor_scalar(out=qdst[:, c * 2048:(c + 1) * 2048],
                                                in0=ei32[:, (c % 2) * 2048:(c % 2) * 2048 + 2048],
                                                scalar1=-1.0, scalar2=1.0,
                                                op0=ALU.max, op1=ALU.min))

        sWg_i = mean_slot[f"Wg{e}"]
        sWd_i = mean_slot[f"Wd{e}"]

        with nc.Block() as blk:
            base_l = cl.v
            base_v = cv.v
            base_a = ca.v
            base_t = ct.v

            def P(i, k):
                return slice((i % 2) * k, (i % 2) * k + k)

            def ec(i, j):
                return ecol[:, (i % 2) * 8 + j:(i % 2) * 8 + j + 1]

            @blk.sync
            def _(sync):
                join(sync)

                def load(i):
                    if i >= 2:
                        sync.wait_ge(cv.sem, base_v + (i - 2) * VP + 4)
                    cl.tick(sync.dma_start(out=xg[:, P(i, H)],
                                           in_=s_xg[e * CAP + i * 128:
                                                    e * CAP + (i + 1) * 128, :]))

                def store(i):
                    sync.wait_ge(cv.sem, base_v + (i + 1) * VP)
                    cl.tick(sync.dma_start(
                        out=s_moe[e * CAP + i * 128:e * CAP + (i + 1) * 128, :],
                        in_=eyb[:, P(i, 2048)]))

                load(0)
                load(1)
                for i in range(NTILE):
                    if i + 2 < NTILE:
                        load(i + 2)
                    store(i)

            @blk.scalar
            def _(a):
                for i in range(NTILE):
                    lpos = (i + 1) if i < 2 else (2 * i - 1)
                    a.wait_ge(cl.sem, base_l + lpos * 16)
                    if i >= 2:
                        a.wait_ge(cv.sem, base_v + (i - 2) * VP + 19)
                    ca.tick(a.activation(out=ehh[:, P(i, IA)].bitcast(F32),
                                         in_=xg[:, (i % 2) * H:(i % 2) * H + IA],
                                         func=AF.Square, accum_out=ec(i, 0)))
                    ca.tick(a.activation(out=ehh[:, P(i, IA)].bitcast(F32),
                                         in_=xg[:, (i % 2) * H + IA:(i % 2) * H + 2 * IA],
                                         func=AF.Square, accum_out=ec(i, 1)))
                    a.wait_ge(cv.sem, base_v + i * VP + 2)
                    ca.tick(a.activation(out=ec(i, 2), in_=ec(i, 2), func=AF.Sqrt))
                    a.wait_ge(cv.sem, base_v + i * VP + 9)
                    ca.tick(a.activation(out=ei32[:, P(i, 2048)], in_=exn[:, P(i, H)],
                                         func=AF.Identity, scale=ec(i, 5)))
                    for c in range(HC):
                        a.wait_ge(ct.sem, base_t + i * TP + c + 1)
                        ca.tick(a.activation(out=ekT[:, (i % 2) * H + c * 128:
                                                      (i % 2) * H + (c + 1) * 128],
                                             in_=PS[4 + c % 2][0:128, 0:128],
                                             func=AF.Identity))
                    a.wait_ge(cv.sem, base_v + i * VP + 15)
                    ca.tick(a.activation(out=exn[:, (i % 2) * H:(i % 2) * H + IA],
                                         in_=eyb[:, (i % 2) * 2048:(i % 2) * 2048 + IA],
                                         func=AF.Silu))
                    a.wait_ge(cv.sem, base_v + i * VP + 16)
                    ca.tick(a.activation(out=exn[:, (i % 2) * H + IA:(i % 2) * H + 2 * IA]
                                         .bitcast(F32),
                                         in_=ehh[:, P(i, IA)], func=AF.Square,
                                         accum_out=ec(i, 0)))
                    a.wait_ge(cv.sem, base_v + i * VP + 17)
                    ca.tick(a.activation(out=ec(i, 2), in_=ec(i, 2), func=AF.Sqrt))
                    a.wait_ge(cv.sem, base_v + i * VP + 24)
                    ca.tick(a.activation(out=ei32[:, (i % 2) * 2048:(i % 2) * 2048 + IA],
                                         in_=exn[:, (i % 2) * H:(i % 2) * H + IA],
                                         func=AF.Identity, scale=ec(i, 5)))
                    for c in range(IAC):
                        a.wait_ge(ct.sem, base_t + i * TP + 16 + 4 + c + 1)
                        ca.tick(a.activation(out=ek2T[:, (i % 2) * IA + c * 128:
                                                       (i % 2) * IA + (c + 1) * 128],
                                             in_=PS[4 + c % 2][0:128, 0:128],
                                             func=AF.Identity))

            @blk.vector
            def _(v):
                for i in range(NTILE):
                    v.wait_ge(ca.sem, base_a + i * APc + 2)
                    cv.tick(v.scalar_tensor_tensor(out=ec(i, 2), in0=ec(i, 0),
                                                   scalar=1.0, in1=ec(i, 1),
                                                   op0=ALU.mult, op1=ALU.add))
                    v.wait_ge(cv.sem, cv.v)
                    cv.tick(v.tensor_scalar(out=ec(i, 2), in0=ec(i, 2),
                                            scalar1=1.0 / H, scalar2=1e-8,
                                            op0=ALU.mult, op1=ALU.add))
                    v.wait_ge(ca.sem, base_a + i * APc + 3)
                    cv.tick(v.reciprocal(ec(i, 3), ec(i, 2)))
                    v.wait_ge(cv.sem, cv.v)
                    cv.tick(v.scalar_tensor_tensor(out=exn[:, P(i, H)],
                                                   in0=xg[:, P(i, H)],
                                                   scalar=ec(i, 3), in1=b_NgE[:],
                                                   op0=ALU.mult, op1=ALU.mult))
                    v.wait_ge(cv.sem, cv.v)
                    cv.tick(v.reduce_max(ec(i, 4), exn[:, P(i, H)], axis=AX.X,
                                         apply_absolute_value=True))
                    v.wait_ge(cv.sem, cv.v)
                    cv.tick(v.tensor_scalar(out=ec(i, 4), in0=ec(i, 4), scalar1=1e-5,
                                            scalar2=None, op0=ALU.max))
                    v.wait_ge(cv.sem, cv.v)
                    cv.tick(v.reciprocal(ec(i, 5), ec(i, 4)))
                    v.wait_ge(cv.sem, cv.v)
                    cv.tick(v.tensor_scalar(out=ec(i, 5), in0=ec(i, 5), scalar1=127.0,
                                            scalar2=None, op0=ALU.mult))
                    v.wait_ge(cv.sem, cv.v)
                    cv.tick(v.tensor_scalar(out=ec(i, 4), in0=ec(i, 4),
                                            scalar1=1.0 / 127.0, scalar2=None,
                                            op0=ALU.mult))
                    v.wait_ge(ca.sem, base_a + i * APc + 4)
                    cv.tick(v.tensor_copy(out=exn[:, P(i, H)], in_=ei32[:, P(i, 2048)]))
                    if i >= 2:
                        spos = (2 * (i - 2) + 4) if (i - 2) <= 5 else (9 + i - 2)
                        v.wait_ge(cl.sem, base_l + spos * 16)
                    v.wait_ge(cv.sem, cv.v)
                    cv.tick(v.tensor_mul(ec(i, 6), ec(i, 4), b_wm[:, sWg_i + 1:sWg_i + 2]))
                    for oc in range(4):
                        v.wait_ge(ct.sem, base_t + i * TP + 16 + oc + 1)
                        cv.tick(v.tensor_scalar(
                            out=eyb[:, (i % 2) * 2048 + oc * 512:
                                    (i % 2) * 2048 + (oc + 1) * 512],
                            in0=PS[oc][0:128, 0:512], scalar1=ec(i, 6),
                            scalar2=None, op0=ALU.mult))
                    v.wait_ge(ca.sem, base_a + i * APc + 21)
                    cv.tick(v.tensor_mul(ehh[:, P(i, IA)],
                                         exn[:, (i % 2) * H:(i % 2) * H + IA],
                                         eyb[:, (i % 2) * 2048 + IA:
                                             (i % 2) * 2048 + 2 * IA]))
                    v.wait_ge(ca.sem, base_a + i * APc + 22)
                    cv.tick(v.tensor_scalar(out=ec(i, 2), in0=ec(i, 0),
                                            scalar1=1.0 / IA, scalar2=1e-8,
                                            op0=ALU.mult, op1=ALU.add))
                    v.wait_ge(ca.sem, base_a + i * APc + 23)
                    cv.tick(v.reciprocal(ec(i, 3), ec(i, 2)))
                    v.wait_ge(cv.sem, cv.v)
                    cv.tick(v.scalar_tensor_tensor(out=exn[:, (i % 2) * H:
                                                           (i % 2) * H + IA],
                                                   in0=ehh[:, P(i, IA)],
                                                   scalar=ec(i, 3), in1=b_NdE[:],
                                                   op0=ALU.mult, op1=ALU.mult))
                    v.wait_ge(cv.sem, cv.v)
                    cv.tick(v.reduce_max(ec(i, 4), exn[:, (i % 2) * H:(i % 2) * H + IA],
                                         axis=AX.X, apply_absolute_value=True))
                    v.wait_ge(cv.sem, cv.v)
                    cv.tick(v.tensor_scalar(out=ec(i, 4), in0=ec(i, 4), scalar1=1e-5,
                                            scalar2=None, op0=ALU.max))
                    v.wait_ge(cv.sem, cv.v)
                    cv.tick(v.reciprocal(ec(i, 5), ec(i, 4)))
                    v.wait_ge(cv.sem, cv.v)
                    cv.tick(v.tensor_scalar(out=ec(i, 5), in0=ec(i, 5), scalar1=127.0,
                                            scalar2=None, op0=ALU.mult))
                    v.wait_ge(cv.sem, cv.v)
                    cv.tick(v.tensor_scalar(out=ec(i, 4), in0=ec(i, 4),
                                            scalar1=1.0 / 127.0, scalar2=None,
                                            op0=ALU.mult))
                    v.wait_ge(ca.sem, base_a + i * APc + 24)
                    cv.tick(v.tensor_copy(out=exn[:, (i % 2) * H:(i % 2) * H + IA],
                                          in_=ei32[:, (i % 2) * 2048:
                                                   (i % 2) * 2048 + IA]))
                    v.wait_ge(cv.sem, cv.v)
                    cv.tick(v.tensor_mul(ec(i, 6), ec(i, 4), b_wm[:, sWd_i + 1:sWd_i + 2]))
                    for oc in range(4):
                        v.wait_ge(ct.sem, base_t + i * TP + 16 + 4 + 8 + oc + 1)
                        cv.tick(v.tensor_scalar(out=eyb[:, (i % 2) * 2048 + oc * 512:
                                                        (i % 2) * 2048 + (oc + 1) * 512],
                                                in0=PS[oc][0:128, 0:512],
                                                scalar1=ec(i, 6), scalar2=None,
                                                op0=ALU.mult))

            @blk.tensor
            def _(t):
                t.wait_ge(cv.sem, base_v)
                t.wait_ge(ca.sem, base_a)
                t.wait_ge(cl.sem, base_l)
                for i in range(NTILE):
                    for c in range(HC):
                        t.wait_ge(cv.sem, base_v + i * VP + 10)
                        if c >= 2:
                            t.wait_ge(ca.sem, base_a + i * APc + 4 + c - 1)
                        elif i >= 1:
                            t.wait_ge(ca.sem, base_a + i * APc)
                        ct.tick(t.transpose(PS[4 + c % 2][0:128, 0:128],
                                            exn[:, (i % 2) * H + c * 128:
                                                (i % 2) * H + (c + 1) * 128],
                                            identf[:]))
                    t.wait_ge(ca.sem, base_a + i * APc + 20)
                    if i >= 1:
                        t.wait_ge(cv.sem, base_v + i * VP)
                    for oc in range(4):
                        ins = None
                        for c in range(HC):
                            ins = t.matmul(PS[oc][0:128, 0:512],
                                           ekT[:, (i % 2) * H + c * 128:
                                               (i % 2) * H + (c + 1) * 128],
                                           qWgE[:, c * 2048 + oc * 512:
                                                c * 2048 + (oc + 1) * 512],
                                           start=(c == 0), stop=(c == HC - 1))
                        ct.tick(ins)
                    for c in range(IAC):
                        t.wait_ge(cv.sem, base_v + i * VP + 25)
                        if c >= 2:
                            t.wait_ge(ca.sem, base_a + i * APc + 24 + c - 1)
                        else:
                            t.wait_ge(ca.sem, base_a + i * APc + 20)
                        ct.tick(t.transpose(PS[4 + c % 2][0:128, 0:128],
                                            exn[:, (i % 2) * H + c * 128:
                                                (i % 2) * H + (c + 1) * 128],
                                            identf[:]))
                    t.wait_ge(ca.sem, base_a + i * APc + APc)
                    t.wait_ge(cv.sem, base_v + i * VP + 17)
                    for oc in range(4):
                        ins = None
                        for c in range(IAC):
                            ins = t.matmul(PS[oc][0:128, 0:512],
                                           ek2T[:, (i % 2) * IA + c * 128:
                                                (i % 2) * IA + (c + 1) * 128],
                                           qWdE[:, c * 2048 + oc * 512:
                                                c * 2048 + (oc + 1) * 512],
                                           start=(c == 0), stop=(c == IAC - 1))
                        ct.tick(ins)

    p5.close()

    if KPH < 5:
        root.close()
        return nc
    p6 = ExitStack()
    sb6 = lambda n, s, d: p6.enter_context(nc.sbuf_tensor(n, s, d))
    acc2 = sb6("acc2", [128, H], F32)
    B1 = sb6("B1", [128, H], F32)
    B2 = sb6("B2", [128, H], F32)
    fidx = sb6("fidx", [128, 2], U32)

    with nc.Block() as blk:
        @blk.vector
        def _(v):
            join(v)
            cv.tick(v.memset(B1[:], 0.0))
            cv.tick(v.memset(B2[:], 0.0))

    for k in range(8):
        with nc.Block() as blk:
            @blk.gpsimd
            def _(gp):
                join(gp)
                cp.tick(gp.tensor_copy(out=fidx[:, 0:1], in_=rt_mine[:, k, 4:5]))
                cp.tick(gp.tensor_copy(out=fidx[:, 1:2], in_=rt_mine[:, k, 5:6]))
                gp.wait_ge(cp.sem, cp.v)
                cg.tick(gp.indirect_dma_start(
                    out=B1[:], out_offset=None, in_=s_moe[:],
                    in_offset=IndirectOffsetOnAxis(ap=fidx[:, 0:1], axis=0),
                    bounds_check=T - 1, oob_is_err=False))
                cg.tick(gp.indirect_dma_start(
                    out=B2[:], out_offset=None, in_=s_moe[:],
                    in_offset=IndirectOffsetOnAxis(ap=fidx[:, 1:2], axis=0),
                    bounds_check=T - 1, oob_is_err=False))

            @blk.sync
            def _(sync):
                join(sync)
                cl.tick(sync.dma_start(out=acc2[:], in_=s_acc[k * 128:(k + 1) * 128, :]))

            @blk.vector
            def _(v):
                join(v, cg, cl)
                v.wait_ge(cv.sem, cv.v)
                cv.tick(v.scalar_tensor_tensor(out=acc2[:], in0=B1[:],
                                               scalar=rt_mine[:, k, 2:3], in1=acc2[:],
                                               op0=ALU.mult, op1=ALU.add))
                v.wait_ge(cv.sem, cv.v)
                cv.tick(v.scalar_tensor_tensor(out=acc2[:], in0=B2[:],
                                               scalar=rt_mine[:, k, 3:4], in1=acc2[:],
                                               op0=ALU.mult, op1=ALU.add))

            @blk.sync
            def _(sync):
                join(sync, cv)
                cl.tick(sync.dma_start(out=d_out[k * 128:(k + 1) * 128, :], in_=acc2[:]))

    if "acc" in DBG:
        with nc.Block() as blk:
            @blk.sync
            def _(sync):
                join(sync)
                cl.tick(sync.dma_start(out=dbg["dbg_acc"][:], in_=s_acc[:]))

    p6.close()

    with nc.Block() as blk:
        @blk.sync
        def _(sync):
            join(sync)

    root.close()
    return nc


_NC_CACHE = None
_LAST_RES = None


def _dn_idx(inputs):
    return np.asarray(inputs["L0_ffn_down_reorder_index"]).astype(np.int64)


def kernel(**inputs):
    global _NC_CACHE
    xf = np.ascontiguousarray(inputs["L0_attn_x"].reshape(T, H).astype(np.float32))
    xfT = np.ascontiguousarray(xf.T)

    if _NC_CACHE is None:
        _NC_CACHE = build_program()
    nc = _NC_CACHE

    common = {
        "xfT": xfT,
        "Wg": np.ascontiguousarray(inputs["L0_attn_Wg"], dtype=np.float32),
        "Wd": np.ascontiguousarray(inputs["L0_attn_Wd"], dtype=np.float32),
        "Ng": np.ascontiguousarray(inputs["L0_attn_Ng"], dtype=np.float32),
        "Nd": np.ascontiguousarray(inputs["L0_attn_Nd"], dtype=np.float32),
        "sWg": np.ascontiguousarray(inputs["L0_attn_sWg"], dtype=np.float32),
        "sWd": np.ascontiguousarray(inputs["L0_attn_sWd"], dtype=np.float32),
        "sNg": np.ascontiguousarray(inputs["L0_attn_sNg"], dtype=np.float32).reshape(1, H),
        "sNd": np.ascontiguousarray(inputs["L0_attn_sNd"], dtype=np.float32).reshape(1, IA),
        "gnw": np.ascontiguousarray(inputs["L0_attn_gate_norm_w"],
                                    dtype=np.float32).reshape(1, H),
        "Wgate": np.ascontiguousarray(inputs["L0_attn_W_gate"], dtype=np.float32),
        "Wsg": np.ascontiguousarray(inputs["L0_attn_W_shared_gate"],
                                    dtype=np.float32).reshape(1, H),
        "wgate_f": np.ascontiguousarray(inputs["L0_ffn_w_gate"][_dn_idx(inputs)], dtype=np.float32),
        "wup_f": np.ascontiguousarray(inputs["L0_ffn_w_up"][_dn_idx(inputs)], dtype=np.float32),
        "wdown_f": np.ascontiguousarray(inputs["L0_ffn_w_down"], dtype=np.float32),
    }
    in_maps = []
    for c in range(NCORE):
        m = dict(common)
        m["xsh"] = np.ascontiguousarray(xf[c * TPC:(c + 1) * TPC, :])
        tglob = c * TPC + np.arange(TPC)
        gi = ((tglob % 128) * 64 + (tglob // 128)).astype(np.int32)
        m["gidx"] = np.ascontiguousarray(gi.reshape(8, 128).T)
        in_maps.append(m)

    res = run_bass_kernel_spmd(nc, in_maps, core_ids=list(range(NCORE)))
    global _LAST_RES
    _LAST_RES = res
    out = np.concatenate([np.asarray(res.results[c]["out"]) for c in range(NCORE)], axis=0)
    return out.reshape(inputs["L0_attn_x"].shape).astype(np.float32)
